# version 08
# baseline (speedup 1.0000x reference)
import numpy as np
import ml_dtypes
import concourse.bass as bass
import concourse.mybir as mybir
from concourse.bass_utils import run_bass_kernel_spmd

F32 = mybir.dt.float32
BF16 = mybir.dt.bfloat16
AF = mybir.ActivationFunctionType
ALU = mybir.AluOpType
AX = mybir.AxisListType
BF16NP = ml_dtypes.bfloat16

D = 2048
DI = 4096
DS = 128
HD = 64
NH = 64
INTER = 8192
CONV = DI + 2 * DS
DIP = 2 * DI + 2 * DS + NH
EPS = 1e-5
FEPS = 1e-10
T = 1024
TH = T + 3
NCK = 8
KD = D // 128

ENGINES = ("sync", "gpsimd", "vector", "scalar", "tensor")


class Plan:
    def __init__(self):
        self.ops = []
        self.cnt = {}

    SELF_SEM = {"vector": "V", "scalar": "S", "gpsimd": "G"}

    def op(self, eng, fn, waits=(), inc=None):
        ws = [(s, v) for (s, v) in waits if v > 0]
        sem = self.SELF_SEM.get(eng)
        if sem is not None and fn is not None:
            if eng != "gpsimd" or (inc is not None and inc[0] == "G"):
                v = self.cnt.get(sem, 0)
                if v > 0:
                    ws.append((sem, v))
        self.ops.append((eng, fn, ws, inc))
        if inc is not None:
            s, amt = inc
            self.cnt[s] = self.cnt.get(s, 0) + amt
            return (s, self.cnt[s])
        return None

    def tok(self, sem):
        return (sem, self.cnt.get(sem, 0))

    def barrier(self):
        toks = [(s, v) for s, v in self.cnt.items() if s != "CC"]
        for eng in ENGINES:
            self.op(eng, None, waits=toks)

    def emit(self, nc):
        from contextlib import ExitStack
        sems = {}
        with ExitStack() as ctx:
            names = set(self.cnt.keys())
            for (_, _, ws, _) in self.ops:
                names.update(s for s, _ in ws)
            for s in sorted(names):
                sems[s] = ctx.enter_context(nc.semaphore(s))
            block = ctx.enter_context(nc.Block())

            def mk(eng_name):
                def emit_eng(eng):
                    for (e, fn, ws, inc) in self.ops:
                        if e != eng_name:
                            continue
                        for s, v in ws:
                            eng.wait_ge(sems[s], v)
                        if fn is not None:
                            inst = fn(eng)
                            if inc is not None:
                                inst.then_inc(sems[inc[0]], inc[1])
                return emit_eng

            block.sync(mk("sync"))
            block.gpsimd(mk("gpsimd"))
            block.vector(mk("vector"))
            block.scalar(mk("scalar"))
            block.tensor(mk("tensor"))


class Carver:

    def __init__(self, nc, total_bytes):
        self.h32 = nc.alloc_sbuf_tensor("arena", [128, total_bytes // 4], F32)
        self.h16 = self.h32.bitcast(BF16)
        self.segs = {}

    def seg(self, name, start, end):
        self.segs[name] = [start, start, end]

    def reset(self, name):
        self.segs[name][1] = self.segs[name][0]

    def alloc(self, seg, p, n, dt, align=64):
        s = self.segs[seg]
        off = (s[1] + align - 1) // align * align
        size = n * (4 if dt == F32 else 2)
        assert off + size <= s[2], f"seg {seg} overflow {off}+{size}>{s[2]}"
        s[1] = off + size
        h = self.h32 if dt == F32 else self.h16
        es = 4 if dt == F32 else 2
        return h[0:p, off // es: off // es + n]


def build(upto=8, dbg=(), nocc=False, nch=NCK, noxl=False, sp=4, noseg=False):
    nc = bass.Bass("TRN2", target_bir_lowering=False)

    def din(name, shape, dt=F32):
        return nc.dram_tensor(name, shape, dt, kind="ExternalInput")

    def dscr(name, shape, dt):
        kind = "ExternalOutput" if name in dbg else "Internal"
        return nc.dram_tensor(name, shape, dt, kind=kind)

    xin = din("xin", [TH, D])
    wipT = din("wipT", [D, DIP])
    wopT = din("wopT", [DI, D])
    wgT = din("wgT", [D, INTER])
    wuT = din("wuT", [D, INTER])
    wdT = din("wdT", [INTER, D])
    wfT = din("wfT", [D, D])
    convw = din("convw", [128, 34 * 4])
    convb = din("convb", [128, 34])
    dtb = din("dtb", [64, 1])
    alog = din("alog", [64, 1])
    dexp_bc = din("dexp_bc", [128, DI])
    nw_bc = din("nw_bc", [128, DI])
    ln1_bc = din("ln1_bc", [128, D])
    ln2_bc = din("ln2_bc", [128, D])
    bias_bc = din("bias_bc", [128, D])
    cle = din("cle", [128, 128])
    cle16 = din("cle16", [128, 128], BF16)
    up16 = din("up16", [128, 128], BF16)
    iden = din("iden", [128, 128])
    ones64 = din("ones64", [128, 64])
    ones1 = din("ones1", [1, 128])
    mcol = din("mcol", [128, 7])
    minv = din("minv", [128, 7])

    xn_h = dscr("xn_h", [T, D], F32)
    xh1_h = dscr("xh1_h", [TH, D], BF16)
    xn16_h = dscr("xn16_h", [T, D], BF16)
    xbcp_h = dscr("xbcp_h", [CONV, T], BF16)
    sz_h = dscr("sz_h", [T, DI], BF16)
    yp_h = dscr("yp_h", [T, DI], F32)
    yn16_h = dscr("yn16_h", [T, DI], BF16)
    x2_h = dscr("x2_h", [T, D], BF16)
    g_h = dscr("g_h", [T, D], BF16)
    hs_h = dscr("hs_h", [T, D], F32) if "hs_h" in dbg else None
    dtp_h = dscr("dtp_h", [64, T], F32) if "dtp_h" in dbg else None
    agin = nc.dram_tensor("agin", [128, 2 * DI], F32)
    agout = nc.dram_tensor("agout", [8 * 128, 2 * DI], F32)
    out_h = nc.dram_tensor("out_h", [T, D], F32, kind="ExternalOutput")

    pb = [nc.alloc_psum_tensor(f"pb{i}", [128, 512], F32) for i in range(8)]

    KB = 1024
    cv = Carver(nc, 212000)
    cv.seg("C", 0, 4 * KB)
    cv.seg("A", 4 * KB, 17 * KB)
    cv.seg("B", 17 * KB, 51 * KB)
    cv.seg("H", 51 * KB, 115 * KB)
    cv.seg("S", 115 * KB, 212000)

    P = Plan()
    _refs = {}

    c_cle = cv.alloc("C", 128, 128, F32)
    c_cle16 = cv.alloc("C", 128, 128, BF16)
    c_up = cv.alloc("C", 128, 128, BF16)
    c_iden = cv.alloc("C", 128, 128, F32)
    c_ones64 = cv.alloc("C", 128, 64, F32)
    c_ones1 = cv.alloc("C", 1, 128, F32)
    c_convw = cv.alloc("C", 128, 34 * 4, F32)
    c_convb = cv.alloc("C", 128, 34, F32)
    c_dtb = cv.alloc("C", 64, 1, F32)
    c_alog = cv.alloc("C", 64, 1, F32)
    c_nega = cv.alloc("C", 64, 1, F32)
    c_mcol = cv.alloc("C", 128, 7, F32)
    c_minv = cv.alloc("C", 128, 7, F32)

    a_hm = cv.alloc("A", 64, T, F32)
    dtp_hm = cv.alloc("A", 64, T, F32)
    v_hm = cv.alloc("A", 64, T, F32)
    offac = cv.alloc("A", 64, 1, F32)
    fcell = cv.alloc("C", 1, 1, F32)

    def sfence(waits=()):
        return P.op("scalar", lambda e: e.mul(fcell, fcell, 1.0),
                    waits=list(waits), inc=("S", 1))

    def dma(eng, dst, src, sem, waits=(), transpose=False):
        if transpose:
            fn = lambda e, d=dst, s=src: e.dma_start(d, s, transpose=True)
        else:
            fn = lambda e, d=dst, s=src: e.dma_start(d, s)
        return P.op(eng, fn, waits=waits, inc=(sem, 16))

    def wload(src_h, f0, width, dst_ap, sem, DD, waits=(), cast=True):
        C = DD // 128
        Fw = src_h.shape[1]
        src = bass.AP(src_h, f0, [[Fw, 128], [128 * Fw, C], [1, width]])
        d3 = dst_ap.rearrange("p (c j) -> p c j", j=width)
        eng = "gpsimd" if cast else "sync"
        return P.op(eng, lambda e, d=d3, s=src: e.dma_start(d, s),
                    waits=waits, inc=(sem, 16))

    def act(out, in_, f, bias=0.0, scale=1.0, accum=None):
        return lambda e, o=out, i=in_, ff=f, b=bias, sc=scale, a=accum: \
            e.activation(o, i, ff, bias=b, scale=sc, accum_out=a)

    def rms_scale(src, n, dump_ap, c_ss, c_ms, c_e, out_r, waits):
        P.op("scalar", act(dump_ap, src, AF.Square), waits=list(waits),
             inc=("S", 1))
        P.op("vector", lambda e, o=c_ss, i=dump_ap: e.reduce_sum(
            o, i, AX.X), waits=[P.tok("S")], inc=("V", 1))
        P.op("scalar", lambda e, o=c_ms, i=c_ss: e.activation(
            o, i, AF.Copy, scale=1.0 / n), waits=[P.tok("V")], inc=("S", 1))
        P.op("vector", lambda e, o=c_e, a=c_ms: e.tensor_scalar(
            o, a, 1.0, EPS, ALU.mult, ALU.add), waits=[P.tok("S")],
            inc=("V", 1))
        P.op("scalar", lambda e, o=c_ms, a=c_e: e.activation(o, a, AF.Ln),
             waits=[P.tok("V")], inc=("S", 1))
        return P.op("scalar", lambda e, o=out_r, a=c_ms: e.activation(
            o, a, AF.Exp, scale=-0.5), inc=("S", 1))

    for dst, src in [(c_cle, cle[:, :]), (c_cle16, cle16[:, :]),
                     (c_up, up16[:, :]),
                     (c_iden, iden[:, :]), (c_ones64, ones64[:, :]),
                     (c_ones1, ones1[:, :]), (c_convw, convw[:, :]),
                     (c_convb, convb[:, :]), (c_dtb, dtb[:, :]),
                     (c_alog, alog[:, :]), (c_mcol, mcol[:, :]),
                     (c_minv, minv[:, :])]:
        dma("sync", dst, src, "SD")

    def stage_ln():
        cv.reset("S")
        cv.reset("B")
        x1fm = cv.alloc("B", 128, KD * 1040, BF16)
        _refs["x1fm"] = x1fm
        xt = [cv.alloc("S", 128, D, F32) for _ in range(2)]
        xnt = [cv.alloc("S", 128, D, F32) for _ in range(2)]
        xh1t = [cv.alloc("S", 128, D, BF16) for _ in range(2)]
        xn16t = [cv.alloc("S", 128, D, BF16) for _ in range(2)]
        dumps = [cv.alloc("S", 128, D, F32) for _ in range(2)]
        cols = [{n: cv.alloc("S", 128, 1, F32)
                 for n in ("sumc", "ssc", "mean", "msq", "m2", "t1", "vpe",
                           "lnv", "rstd", "negmr", "ms1", "msm", "mse", "r1")}
                for _ in range(2)]
        lnbc = cv.alloc("S", 128, D, F32)
        dma("sync", lnbc, ln1_bc[:, :], "LB")
        lnbc_tok = P.tok("LB")
        P.op("vector", lambda e: e.memset(xh1t[0][:, :], 0.0), inc=("V", 1))
        P.op("vector", lambda e: e.memset(xh1t[1][:, :], 0.0), inc=("V", 1))

        tiles = [(-1, 0, 3)] + [(c, 3 + c * 128, 128) for c in range(8)]
        tile_done = [None, None]
        spillt = [None, None]
        xpt = [None, None]
        for idx, (c, r0, rows) in enumerate(tiles):
            par = idx % 2
            cl = {n: a[0:rows, :] for n, a in cols[par].items()}
            xb = xt[par][0:rows, :]
            dump = dumps[par]
            gw = [] if tile_done[par] is None else [tile_done[par]]
            lt = dma("sync", xb, xin[r0:r0 + rows, :], f"L{par}", waits=gw)
            P.op("scalar", act(dump[0:rows, :], xb, AF.Square),
                 waits=[lt] + gw, inc=("S", 1))
            P.op("vector", lambda e, a=cl["ssc"], i=dump[0:rows, :]:
                 e.reduce_sum(a, i, AX.X), waits=[P.tok("S")], inc=("V", 1))
            P.op("vector", lambda e, a=cl["sumc"], b=xb: e.reduce_sum(
                a, b, AX.X), waits=[lt] + gw, inc=("V", 1))
            P.op("scalar", lambda e, o=cl["mean"], i=cl["sumc"]:
                 e.activation(o, i, AF.Copy, scale=1.0 / D),
                 waits=[P.tok("V")], inc=("S", 1))
            P.op("scalar", lambda e, o=cl["msq"], i=cl["ssc"]:
                 e.activation(o, i, AF.Copy, scale=1.0 / D), inc=("S", 1))
            sq_tok = P.tok("S")
            P.op("vector", lambda e, o=cl["m2"], m=cl["mean"]:
                 e.tensor_tensor(o, m, m, ALU.mult), waits=[sq_tok],
                 inc=("V", 1))
            P.op("vector", lambda e, o=cl["t1"], a=cl["m2"]: e.tensor_scalar(
                o, a, -1.0, EPS, ALU.mult, ALU.add), inc=("V", 1))
            P.op("vector", lambda e, o=cl["vpe"], a=cl["msq"], b=cl["t1"]:
                 e.tensor_tensor(o, a, b, ALU.add), inc=("V", 1))
            P.op("scalar", lambda e, o=cl["lnv"], a=cl["vpe"]: e.activation(
                o, a, AF.Ln), waits=[P.tok("V")], inc=("S", 1))
            P.op("scalar", lambda e, o=cl["rstd"], a=cl["lnv"]:
                 e.activation(o, a, AF.Exp, scale=-0.5), inc=("S", 1))
            P.op("vector", lambda e, o=cl["negmr"], m=cl["mean"],
                 r=cl["rstd"]:
                 e.scalar_tensor_tensor(o, m, -1.0, r, ALU.mult, ALU.mult),
                 waits=[P.tok("S")], inc=("V", 1))
            xw = [P.tok("V")]
            if spillt[par] is not None:
                xw.append(spillt[par])
            xo = xnt[par][0:rows, :]
            P.op("vector", lambda e, o=xo, i=xb, r=cl["rstd"],
                 b=cl["negmr"]:
                 e.tensor_scalar(o, i, r, b, ALU.mult, ALU.add),
                 waits=xw, inc=("V", 1))
            xn_tok = P.tok("V")
            if c >= 0:
                dma("sync", xn_h[c * 128:(c + 1) * 128, :], xo, f"SP{par}",
                    waits=[xn_tok])
            rms_scale(xo, D, dump[0:rows, :], cl["ms1"], cl["msm"],
                      cl["mse"], cl["r1"], [xn_tok])
            xw2 = [P.tok("S"), lnbc_tok]
            if xpt[par] is not None:
                xw2.append(xpt[par])
            P.op("vector", lambda e, o=xh1t[par][0:rows, :], i=xo,
                 r=cl["r1"], l=lnbc[0:rows, :]: e.scalar_tensor_tensor(
                     o, i, r, l, ALU.mult, ALU.mult),
                 waits=xw2, inc=("V", 1))
            dma("sync", xh1_h[r0:r0 + rows, :], xh1t[par][0:rows, :],
                f"XP{par}", waits=[P.tok("V")])
            if c >= 0:
                P.op("vector", lambda e, o=xn16t[par][0:rows, :], i=xo:
                     e.tensor_copy(o, i), inc=("V", 1))
                dma("sync", xn16_h[c * 128:(c + 1) * 128, :],
                    xn16t[par][0:rows, :], f"SP{par}", waits=[P.tok("V")])
                spillt[par] = P.tok(f"SP{par}")
            xpt[par] = (f"XP{par}", P.cnt[f"XP{par}"])
            tile_done[par] = P.tok("V")

    def stage_inproj():
        cv.reset("S")
        x1fm = _refs["x1fm"]
        for k in range(KD):
            b = k * 1040
            eng = "sync"
            dma(eng, x1fm[:, b:b + 16], xh1_h[0:16, k * 128:(k + 1) * 128],
                "X0", transpose=True)
            dma(eng, x1fm[:, b + 16:b + 528],
                xh1_h[3:515, k * 128:(k + 1) * 128], "X0", transpose=True)
            dma(eng, x1fm[:, b + 528:b + 1040],
                xh1_h[515:1027, k * 128:(k + 1) * 128], "X0", transpose=True)
        x1_tok = P.tok("X0")
        wz = [cv.alloc("S", 128, KD * 512, BF16) for _ in range(2)]
        wxbc = [cv.alloc("S", 128, KD * 128, BF16) for _ in range(2)]
        wdt = cv.alloc("S", 128, KD * 64, BF16)
        xbcbuf = [cv.alloc("S", 128, 1027, BF16) for _ in range(2)]
        acc = [[cv.alloc("S", 128, T, F32) for _ in range(2)] for _ in range(2)]
        xpo = [cv.alloc("S", 128, T, BF16) for _ in range(2)]
        szt = [[cv.alloc("S", 128, 512, BF16) for _ in range(8)]
               for _ in range(2)]
        tmp64 = cv.alloc("S", 64, T, F32)


        def lhs_tc(k, c):
            b = k * 1040 + 16 + c * 128
            return x1fm[:, b:b + 128]

        def rhs_main(k, half):
            b = k * 1040 + 16 + half * 512
            return x1fm[:, b:b + 512]

        def rhs_halo(k):
            return x1fm[:, k * 1040:k * 1040 + 3]

        wl = wload(wipT, 2 * DI + 2 * DS, 64, wdt, "W0", D)
        for half in range(2):
            ps = pb[6 + half][0:64, :]
            for k in range(KD):
                P.op("tensor", lambda e, p=ps, l=wdt[:, k * 64:(k + 1) * 64],
                     r=rhs_main(k, half), st=(k == 0), sp=(k == KD - 1):
                     e.matmul(p, l, r, start=st, stop=sp),
                     waits=[wl, x1_tok] if k == 0 else [], inc=("T", 1))
            P.op("vector", lambda e, o=v_hm[:, half * 512:(half + 1) * 512],
                 p=ps: e.tensor_scalar(o, p, c_dtb, None, ALU.add),
                 waits=[P.tok("T")], inc=("V", 1))
        dt_evac = P.tok("V")

        evac_t = [None, None]
        conv_t = [None, None]
        silu_t = [None, None]
        spill_t = [None, None]
        for fc in range(34):
            par = fc % 2
            wl = wload(wipT, DI + fc * 128, 128, wxbc[par], f"WX{par}", D,
                       waits=[evac_t[par]] if evac_t[par] else [])
            bs = [pb[3 * par], pb[3 * par + 1], pb[3 * par + 2]]
            w0 = [wl, x1_tok]
            if evac_t[par]:
                w0.append(evac_t[par])
            for k in range(KD):
                lt = wxbc[par][:, k * 128:(k + 1) * 128]
                P.op("tensor", lambda e, p=bs[0][:, :], l=lt, r=rhs_main(k, 0),
                     st=(k == 0), sp=(k == KD - 1):
                     e.matmul(p, l, r, start=st, stop=sp),
                     waits=w0 if k == 0 else [], inc=("T", 1))
                P.op("tensor", lambda e, p=bs[1][:, :], l=lt, r=rhs_main(k, 1),
                     st=(k == 0), sp=(k == KD - 1):
                     e.matmul(p, l, r, start=st, stop=sp), inc=("T", 1))
                P.op("tensor", lambda e, p=bs[2][:, 0:3], l=lt, r=rhs_halo(k),
                     st=(k == 0), sp=(k == KD - 1):
                     e.matmul(p, l, r, start=st, stop=sp), inc=("T", 1))
            mm = P.tok("T")
            ew = [mm]
            if conv_t[par]:
                ew.append(conv_t[par])
            xb = xbcbuf[par]
            P.op("scalar", act(xb[:, 0:3], bs[2][:, 0:3], AF.Copy),
                 waits=ew, inc=("S", 1))
            P.op("scalar", act(xb[:, 3:515], bs[0][:, :], AF.Copy),
                 inc=("S", 1))
            P.op("scalar", act(xb[:, 515:1027], bs[1][:, :], AF.Copy),
                 inc=("S", 1))
            evac_t[par] = P.tok("S")
            a0, a1 = acc[par]
            cw = [evac_t[par]]
            if silu_t[par]:
                cw.append(silu_t[par])
            P.op("vector", lambda e, o=a0, i=xb[:, 0:T],
                 w=c_convw[:, fc * 4:fc * 4 + 1]:
                 e.tensor_scalar(o, i, w, None, ALU.mult), waits=cw,
                 inc=("V", 1))
            P.op("vector", lambda e, o=a1, i=xb[:, 1:T + 1],
                 w=c_convw[:, fc * 4 + 1:fc * 4 + 2], a=a0:
                 e.scalar_tensor_tensor(o, i, w, a, ALU.mult, ALU.add),
                 inc=("V", 1))
            P.op("vector", lambda e, o=a0, i=xb[:, 2:T + 2],
                 w=c_convw[:, fc * 4 + 2:fc * 4 + 3], a=a1:
                 e.scalar_tensor_tensor(o, i, w, a, ALU.mult, ALU.add),
                 inc=("V", 1))
            P.op("vector", lambda e, o=a1, i=xb[:, 3:T + 3],
                 w=c_convw[:, fc * 4 + 3:fc * 4 + 4], a=a0:
                 e.scalar_tensor_tensor(o, i, w, a, ALU.mult, ALU.add),
                 inc=("V", 1))
            conv_t[par] = P.tok("V")
            sw = [conv_t[par]]
            if spill_t[par]:
                sw.append(spill_t[par])
            P.op("scalar", act(xpo[par], a1, AF.Silu,
                               bias=c_convb[:, fc:fc + 1]),
                 waits=sw, inc=("S", 1))
            silu_t[par] = P.tok("S")
            spill_t[par] = dma("sync", xbcp_h[fc * 128:(fc + 1) * 128, :],
                               xpo[par], f"XS{par}", waits=[silu_t[par]])

        zsp = [None, None]
        prev_z = None
        for zg in range(8):
            par = zg % 2
            wl = wload(wipT, zg * 512, 512, wz[par], f"WZ{par}", D,
                       waits=[prev_z] if prev_z else [])
            w0 = [wl, x1_tok, dt_evac]
            if prev_z:
                w0.append(prev_z)
            if evac_t[0]:
                w0 += [evac_t[0], evac_t[1]]
            for k in range(KD):
                for c in range(8):
                    P.op("tensor", lambda e, p=pb[c][:, :], l=lhs_tc(k, c),
                         r=wz[par][:, k * 512:(k + 1) * 512],
                         st=(k == 0), sp=(k == KD - 1):
                         e.matmul(p, l, r, start=st, stop=sp),
                         waits=w0 if (k == 0 and c == 0) else [], inc=("T", 1))
            mm = P.tok("T")
            for c in range(8):
                sw = [mm]
                if zsp[par]:
                    sw.append(zsp[par])
                P.op("scalar", act(szt[par][c], pb[c][:, :], AF.Silu),
                     waits=sw, inc=("S", 1))
                dma("sync", sz_h[c * 128:(c + 1) * 128,
                                 zg * 512:(zg + 1) * 512],
                    szt[par][c], f"ZS{par}", waits=[P.tok("S")])
            zsp[par] = P.tok(f"ZS{par}")
            prev_z = P.tok("S")

        P.op("scalar", lambda e: e.activation(tmp64, v_hm, AF.Abs),
             waits=[dt_evac], inc=("S", 1))
        P.op("scalar", lambda e: e.activation(tmp64, tmp64, AF.Exp,
                                              scale=-1.0), inc=("S", 1))
        P.op("scalar", lambda e: e.activation(tmp64, tmp64, AF.Ln, bias=1.0),
             inc=("S", 1))
        P.op("vector", lambda e: e.tensor_scalar(dtp_hm, v_hm, 0.0, None,
                                                 ALU.max),
             waits=[dt_evac], inc=("V", 1))
        P.op("vector", lambda e: e.tensor_tensor(dtp_hm, dtp_hm, tmp64,
                                                 ALU.add),
             waits=[P.tok("S")], inc=("V", 1))
        P.op("scalar", lambda e: e.activation(c_nega, c_alog, AF.Exp),
             inc=("S", 1))
        P.op("vector", lambda e: e.tensor_scalar(a_hm, dtp_hm, c_nega, None,
                                                 ALU.mult),
             waits=[P.tok("S")], inc=("V", 1))
        P.op("vector", lambda e: e.tensor_scalar(a_hm, a_hm, -1.0, None,
                                                 ALU.mult), inc=("V", 1))
        P.op("vector", lambda e: e.memset(offac, 0.0), inc=("V", 1))
        if dtp_h is not None:
            dma("sync", dtp_h[0:64, :], dtp_hm, "SD", waits=[P.tok("V")])

    def stage_scan_a():
        cv.reset("S")
        cv.reset("H")
        cv.reset("B")
        H = cv.alloc("B", 128, DI, F32)
        C_all = cv.alloc("B", 128, NCK * 128, BF16)
        Eg_all = cv.alloc("B", 128, NCK * 64, F32)
        _refs["C_all"] = C_all
        _refs["Eg_all"] = Eg_all
        x_tok = [cv.alloc("S", 128, DI, BF16) for _ in range(2)]
        Bfm = [cv.alloc("S", 128, 128, BF16) for _ in range(2)]
        Btok = [cv.alloc("S", 128, 128, BF16) for _ in range(2)]
        Cfm = [cv.alloc("S", 128, 128, BF16) for _ in range(2)]
        Xdt = [cv.alloc("S", 128, DI, BF16) for _ in range(2)]
        Xdd = [cv.alloc("S", 128, DI, BF16) for _ in range(2)]
        S_all = cv.alloc("S", 128, NH * 128, BF16)
        rhs_hi = [cv.alloc("S", 128, 512, BF16) for _ in range(2)]
        rhs_lo = [cv.alloc("S", 128, 512, BF16) for _ in range(2)]
        egrp = [cv.alloc("S", 128, 512, BF16) for _ in range(2)]
        gpm = cv.alloc("S", 128, 128, BF16)
        L_a = cv.alloc("S", 64, 128, F32)
        L_b = cv.alloc("S", 64, 128, F32)
        Lg = cv.alloc("S", 64, 128, F32)
        dec_hm = cv.alloc("S", 64, 128, F32)
        dec_e = cv.alloc("S", 64, 128, F32)
        Lt = cv.alloc("S", 128, 64, F32)
        Lgt = cv.alloc("S", 128, 64, F32)
        atok = cv.alloc("S", 128, 64, F32)
        dtok = cv.alloc("S", 128, 64, F32)
        dectok = cv.alloc("S", 128, 64, F32)
        ddec = cv.alloc("S", 128, 64, F32)
        ahi = cv.alloc("S", 128, 64, F32)
        alo = cv.alloc("S", 128, 64, F32)
        ahi16 = cv.alloc("S", 128, 64, BF16)
        eLt = cv.alloc("S", 128, 64, F32)
        edt_c = cv.alloc("S", 64, 1, F32)
        edt_r = cv.alloc("S", 1, 64, F32)
        dtot_bc = cv.alloc("S", 128, 64, F32)
        y = cv.alloc("H", 128, DI, F32)
        tmb = cv.alloc("H", 128, DI, F32)
        dtex = tmb
        H16 = cv.alloc("H", 128, DI, BF16)
        dxbc = cv.alloc("H", 128, DI, F32)

        def p7(o, n, p0=0, p1=128):
            return pb[7][p0:p1, o:o + n]

        def bview(ap, nh, rep):
            return bass.AP(ap.tensor, ap.offset,
                           [[ap.tensor.shape[1], 128], [1, nh], [0, rep]])

        def tview(ap, rep, w):
            return bass.AP(ap.tensor, ap.offset,
                           [[ap.tensor.shape[1], 128], [0, rep], [1, w]])

        P.op("vector", lambda e: e.memset(H, 0.0), inc=("V", 1))
        dxt = dma("sync", dxbc, dexp_bc[:, :], "DX")
        hz = P.tok("V")

        buf_free = [None, None]
        tdone = None
        vdone = None
        prev_prep = None
        prev_tr1 = None
        prev_gtok = None
        xdt_free = [None, None]
        lam_rd = [None, None]
        lam_tok = [None, None]
        s_done = [None, None]
        yA_rd = [None, None]
        yB_rd = [None, None]
        egx_rd = [None, None]
        dC_rd = None
        for c in range(nch):
            par = c % 2
            lw = []
            if buf_free[par] is not None:
                lw = buf_free[par]
            for b in range(32):
                P.op("sync",
                     lambda e, d=x_tok[par][:, b * 128:(b + 1) * 128],
                     s=xbcp_h[b * 128:(b + 1) * 128, c * 128:(c + 1) * 128],
                     t=not noxl:
                     e.dma_start(d, s, transpose=t),
                     waits=lw if b < 2 else [], inc=(f"XT{par}", 16))
            P.op("sync", lambda e, d=Btok[par],
                 s=xbcp_h[DI:DI + 128, c * 128:(c + 1) * 128], t=not noxl:
                 e.dma_start(d, s, transpose=t), inc=(f"XT{par}", 16))
            P.op("sync", lambda e, d=Bfm[par],
                 s=xbcp_h[DI:DI + 128, c * 128:(c + 1) * 128]:
                 e.dma_start(d, s), inc=(f"XT{par}", 16))
            P.op("sync", lambda e, d=Cfm[par],
                 s=xbcp_h[DI + 128:DI + 256, c * 128:(c + 1) * 128]:
                 e.dma_start(d, s), inc=(f"XT{par}", 16))
            ld = P.tok(f"XT{par}")

            if sp < 1:
                vdone = P.tok("V")
                buf_free[par] = [vdone, P.tok("T")]
                tdone = P.tok("T")
                continue
            asl = a_hm[:, c * 128:(c + 1) * 128]
            wv = [t for t in (prev_tr1, prev_gtok) if t]
            P.op("vector", lambda e, a=asl: e.tensor_copy(L_a, a),
                 waits=wv, inc=("V", 1))
            src, dst = L_a, L_b
            for s in [1, 2, 4, 8, 16, 32, 64]:
                P.op("vector", lambda e, sh=s, i=src, o=dst: e.tensor_tensor(
                    o[:, sh:128], i[:, sh:128], i[:, 0:128 - sh], ALU.add),
                    inc=("V", 1))
                P.op("vector", lambda e, sh=s, i=src, o=dst: e.tensor_copy(
                    o[:, 0:sh], i[:, 0:sh]), inc=("V", 1))
                src, dst = dst, src
            L_hm = src
            P.op("vector", lambda e, L=L_hm: e.tensor_scalar(
                Lg, L, offac, None, ALU.add), inc=("V", 1))
            P.op("vector", lambda e, L=L_hm: e.tensor_tensor(
                offac, offac, L[:, 127:128], ALU.add), inc=("V", 1))
            P.op("vector", lambda e, L=L_hm: e.tensor_scalar(
                dec_hm, L, L[:, 127:128], None, ALU.subtract), inc=("V", 1))
            vpre = P.tok("V")
            P.op("scalar", lambda e: e.activation(dec_e, dec_hm, AF.Exp,
                                                  scale=-1.0),
                 waits=[vpre] + ([prev_tr1] if prev_tr1 else []),
                 inc=("S", 1))
            P.op("scalar", lambda e, L=L_hm: e.activation(
                edt_c, L[:, 127:128], AF.Exp), inc=("S", 1))
            lch = P.tok("S")

            trw = [lch]
            if tdone:
                trw.append(tdone)
            if prev_prep:
                trw.append(prev_prep)
            dsl = dtp_hm[:, c * 128:(c + 1) * 128]
            P.op("tensor", lambda e, i=asl: e.transpose(
                p7(0, 64), i, c_iden[0:64, 0:64]), waits=trw, inc=("T", 1))
            P.op("tensor", lambda e, i=dsl: e.transpose(
                p7(64, 64), i, c_iden[0:64, 0:64]), inc=("T", 1))
            P.op("tensor", lambda e, L=L_hm: e.transpose(
                p7(128, 64), L, c_iden[0:64, 0:64]), inc=("T", 1))
            P.op("tensor", lambda e: e.transpose(
                p7(192, 64), Lg, c_iden[0:64, 0:64]), inc=("T", 1))
            P.op("tensor", lambda e: e.transpose(
                p7(256, 64), dec_e, c_iden[0:64, 0:64]), inc=("T", 1))
            P.op("tensor", lambda e: e.transpose(
                p7(448, 64, 0, 1), edt_c, c_iden[0:64, 0:64]), inc=("T", 1))
            P.op("tensor", lambda e, l=Bfm[par], r=Cfm[par]: e.matmul(
                p7(320, 128), l, r, start=True, stop=True),
                waits=[ld], inc=("T", 1))
            tr1 = P.tok("T")
            prev_tr1 = tr1
            P.op("vector", lambda e: e.tensor_copy(atok, p7(0, 64)),
                 waits=[tr1], inc=("V", 1))
            P.op("vector", lambda e: e.tensor_copy(dtok, p7(64, 64)),
                 inc=("V", 1))
            P.op("vector", lambda e: e.tensor_copy(Lt, p7(128, 64)),
                 inc=("V", 1))
            P.op("vector", lambda e: e.tensor_copy(Lgt, p7(192, 64)),
                 inc=("V", 1))
            P.op("vector", lambda e: e.tensor_copy(dectok, p7(256, 64)),
                 inc=("V", 1))
            P.op("vector", lambda e: e.tensor_tensor(gpm, p7(320, 128),
                                                     c_cle, ALU.mult),
                 inc=("V", 1))
            P.op("vector", lambda e: e.tensor_copy(edt_r, p7(448, 64, 0, 1)),
                 inc=("V", 1))
            ev1 = P.tok("V")
            evs = P.tok("S")
            P.op("scalar", lambda e: e.activation(eLt, Lt, AF.Exp),
                 waits=[ev1] + ([prev_gtok] if prev_gtok else []),
                 inc=("S", 1))
            P.op("scalar", lambda e, o=Eg_all[:, c * 64:(c + 1) * 64]:
                 e.activation(o, Lgt, AF.Exp), inc=("S", 1))
            eL_tok = P.tok("S")
            P.op("tensor", lambda e: e.matmul(p7(0, 64), c_ones1, edt_r,
                                              start=True, stop=True),
                 waits=[ev1, evs], inc=("T", 1))
            P.op("vector", lambda e: e.tensor_copy(dtot_bc, p7(0, 64)),
                 waits=[P.tok("T")], inc=("V", 1))
            P.op("vector", lambda e: e.tensor_tensor(ddec, dtok, dectok,
                                                     ALU.mult), inc=("V", 1))
            prep = P.tok("V")
            prev_prep = prep

            if sp < 2:
                vdone = P.tok("V")
                buf_free[par] = [vdone, P.tok("T")]
                tdone = P.tok("T")
                continue
            sgrp_tok = []
            for g in range(16):
                lp = g % 2
                bank = pb[lp]
                bw = [prep]
                if s_done[lp]:
                    bw.append(s_done[lp])
                if lam_tok[lp]:
                    bw.append(lam_tok[lp])
                if g < 2 and tdone:
                    bw.append(tdone)
                P.op("vector", lambda e,
                     o=rhs_hi[lp][:, :].rearrange("p (h i) -> p h i", h=4),
                     i0=tview(c_cle16, 4, 128),
                     i1=bview(atok[:, g * 4:g * 4 + 4], 4, 128):
                     e.tensor_tensor(o, i0, i1, ALU.mult),
                     waits=bw, inc=("V", 1))
                rb = P.tok("V")
                mw = [rb]
                if lam_rd[lp]:
                    mw.append(lam_rd[lp])
                P.op("tensor", lambda e, p=bank[:, :], r=rhs_hi[lp]: e.matmul(
                    p, c_up, r, start=True, stop=True), waits=mw,
                    inc=("T", 1))
                lam = P.tok("T")
                lam_tok[lp] = lam
                ew = [lam]
                if s_done[lp]:
                    ew.append(s_done[lp])
                P.op("scalar", lambda e, o=egrp[lp], p=bank[:, :]:
                     e.activation(o, p, AF.Exp), waits=ew, inc=("S", 1))
                lam_rd[lp] = P.tok("S")
                sw = [lam_rd[lp], ev1]
                P.op("vector", lambda e,
                     o=S_all[:, g * 512:(g + 1) * 512].rearrange(
                         "p (h i) -> p h i", h=4),
                     i0=egrp[lp][:, :].rearrange("p (h i) -> p h i", h=4),
                     i1=tview(gpm, 4, 128):
                     e.tensor_tensor(o, i0, i1, ALU.mult),
                     waits=sw, inc=("V", 1))
                s_done[lp] = P.tok("V")
                sgrp_tok.append(s_done[lp])

            if sp < 3:
                vdone = P.tok("V")
                buf_free[par] = [vdone, P.tok("T")]
                tdone = P.tok("T")
                continue
            xw0 = [prep, ld] + ([xdt_free[par]] if xdt_free[par] else [])
            P.op("vector", lambda e,
                 o=Xdt[par][:, :].rearrange("p (h i) -> p h i", h=64),
                 i0=x_tok[par][:, :].rearrange("p (h i) -> p h i", h=64),
                 i1=bview(dtok, 64, 64):
                 e.tensor_tensor(o, i0, i1, ALU.mult),
                 waits=xw0, inc=("V", 1))
            P.op("vector", lambda e,
                 o=Xdd[par][:, :].rearrange("p (h i) -> p h i", h=64),
                 i0=x_tok[par][:, :].rearrange("p (h i) -> p h i", h=64),
                 i1=bview(ddec, 64, 64):
                 e.tensor_tensor(o, i0, i1, ALU.mult), inc=("V", 1))
            xdt_tok = P.tok("V")
            xdd_tok = xdt_tok
            hw = [hz]
            if tdone:
                hw.append(tdone)
            P.op("vector", lambda e: e.tensor_copy(H16, H), waits=hw,
                 inc=("V", 1))
            h16_tok = P.tok("V")

            if sp < 4:
                vdone = P.tok("V")
                buf_free[par] = [vdone, P.tok("T")]
                tdone = P.tok("T")
                continue
            spill_gate = P.tok("YS") if c > 0 else None
            for g in range(8):
                gp = g % 2
                bA = pb[2 + gp]
                bB = pb[4 + gp]
                bC = pb[6]
                ecols = eL_tok
                wA = [h16_tok, ld]
                if yA_rd[gp]:
                    wA.append(yA_rd[gp])
                P.op("tensor", lambda e, p=bA[:, :], l=Cfm[par],
                     r=H16[:, g * 512:(g + 1) * 512]:
                     e.matmul(p, l, r, start=True, stop=True), waits=wA,
                     inc=("T", 1))
                wB = [sgrp_tok[(g * 8) // 4], sgrp_tok[(g * 8 + 7) // 4],
                      xdt_tok]
                if yB_rd[gp]:
                    wB.append(yB_rd[gp])
                for hh in range(8):
                    h = g * 8 + hh
                    P.op("tensor", lambda e, p=bB[:, hh * 64:(hh + 1) * 64],
                         l=S_all[:, h * 128:(h + 1) * 128],
                         r=Xdt[par][:, h * 64:(h + 1) * 64]:
                         e.matmul(p, l, r, start=True, stop=True),
                         waits=wB if hh == 0 else [], inc=("T", 1))
                yt = P.tok("T")
                dw = [xdd_tok, ld]
                if dC_rd:
                    dw.append(dC_rd)
                for hh in range(8):
                    h = g * 8 + hh
                    P.op("tensor", lambda e, p=bC[:, hh * 64:(hh + 1) * 64],
                         l=Btok[par], r=Xdd[par][:, h * 64:(h + 1) * 64]:
                         e.matmul(p, l, r, start=True, stop=True),
                         waits=dw if hh == 0 else [], inc=("T", 1))
                dtk = P.tok("T")
                yw = [yt, ecols]
                if spill_gate:
                    yw.append(spill_gate)
                P.op("vector", lambda e,
                     o=y[:, g * 512:(g + 1) * 512].rearrange(
                         "p (h i) -> p h i", h=8),
                     p=bA[:, :].rearrange("p (h i) -> p h i", h=8),
                     ee=bview(eLt[:, g * 8:(g + 1) * 8], 8, 64):
                     e.tensor_tensor(o, p, ee, ALU.mult),
                     waits=yw, inc=("V", 1))
                yA_rd[gp] = P.tok("V")
                P.op("vector", lambda e, o=y[:, g * 512:(g + 1) * 512],
                     p=bB[:, :]: e.tensor_tensor(o, o, p, ALU.add),
                     inc=("V", 1))
                yB_rd[gp] = P.tok("V")
                P.op("vector", lambda e,
                     o=H[:, g * 512:(g + 1) * 512].rearrange(
                         "p (h i) -> p h i", h=8),
                     dd=bview(dtot_bc[:, g * 8:(g + 1) * 8], 8, 64):
                     e.tensor_tensor(o, o, dd, ALU.mult),
                     waits=[dtk, h16_tok], inc=("V", 1))
                P.op("vector", lambda e, o=H[:, g * 512:(g + 1) * 512],
                     p=bC[:, :]: e.tensor_tensor(o, o, p, ALU.add),
                     inc=("V", 1))
                dC_rd = P.tok("V")
            P.op("vector", lambda e, o=C_all[:, c * 128:(c + 1) * 128],
                 i=Cfm[par]: e.tensor_copy(o, i), waits=[ld], inc=("V", 1))
            P.op("vector", lambda e, xp=x_tok[par]: e.tensor_tensor(
                tmb, xp, dxbc, ALU.mult), waits=[dxt, xdt_tok], inc=("V", 1))
            P.op("vector", lambda e: e.tensor_tensor(y, y, tmb, ALU.add),
                 inc=("V", 1))
            vdone = P.tok("V")
            buf_free[par] = [vdone, P.tok("T"), P.tok("G")]
            tdone = P.tok("T")
            xdt_free[par] = tdone
            prev_gtok = P.tok("G")
            dma("sync", yp_h[c * 128:(c + 1) * 128, :], y, "YS",
                waits=[vdone])

        if noseg:
            P.op("gpsimd", lambda e: e.dma_start(
                agout[0:128, :], agin[:, :]), waits=[P.tok("GD")] if P.cnt.get("GD") else [],
                inc=("CC", 16))
            return
        P.op("scalar", lambda e: e.activation(edt_c, offac, AF.Exp),
             waits=[P.tok("V")], inc=("S", 1))
        P.op("tensor", lambda e: e.transpose(
            p7(448, 64, 0, 1), edt_c, c_iden[0:64, 0:64]),
            waits=[P.tok("S"), P.tok("V")], inc=("T", 1))
        P.op("vector", lambda e: e.tensor_copy(edt_r, p7(448, 64, 0, 1)),
             waits=[P.tok("T")], inc=("V", 1))
        P.op("tensor", lambda e: e.matmul(p7(0, 64), c_ones1, edt_r,
                                          start=True, stop=True),
             waits=[P.tok("V")], inc=("T", 1))
        P.op("vector", lambda e: e.tensor_copy(dtot_bc, p7(0, 64)),
             waits=[P.tok("T")], inc=("V", 1))
        for h in range(NH):
            P.op("vector", lambda e, o=dtex[:, h * 64:(h + 1) * 64],
                 s=dtot_bc[:, h:h + 1]: e.tensor_scalar(
                     o, c_ones64, s, None, ALU.mult), inc=("V", 1))
        vv = P.tok("V")
        dma("gpsimd", agin[:, 0:DI], H, "GD", waits=[vv])
        dma("gpsimd", agin[:, DI:2 * DI], dtex, "GD")
        if nocc:
            P.op("gpsimd", lambda e: e.dma_start(
                agout[0:128, :], agin[:, :]), waits=[P.tok("GD")],
                inc=("CC", 16))
        else:
            P.op("gpsimd", lambda e: e.collective_compute(
                "AllGather", ALU.bypass, replica_groups=[list(range(8))],
                ins=[agin[:, :]], outs=[agout[:, :]]),
                waits=[P.tok("GD")], inc=("CC", 1))

    def stage_g():
        cv.reset("S")
        cv.reset("H")
        wq = cv.alloc("H", 128, 4 * KD * 512, BF16)
        wfb = [cv.alloc("S", 128, KD * 512, F32) for _ in range(2)]
        accc = cv.alloc("S", 128, 1, F32)
        redc = cv.alloc("S", 128, 1, F32)
        tot = cv.alloc("S", 1, 1, F32)
        srow = cv.alloc("S", 1, 2, F32)
        sbc = cv.alloc("S", 128, 2, F32)
        xnfm = [cv.alloc("S", 128, KD * 128, BF16) for _ in range(2)]
        bb = cv.alloc("B", 128, D, F32)
        g16 = [cv.alloc("S", 128, 512, BF16) for _ in range(4)]
        tball = [cv.alloc("S", 128, 512, F32) for _ in range(2)]
        dma("sync", bb, bias_bc[:, :], "LB")
        bb_tok = P.tok("LB")

        def wf_load(gg, waits):
            src = bass.AP(wfT, gg * 512, [[D, 128], [128 * D, KD], [1, 512]])
            d3 = wfb[gg % 2].rearrange("p (c j) -> p c j", j=512)
            return P.op("sync", lambda e, d=d3, s=src: e.dma_start(d, s),
                        waits=waits, inc=(f"WF{gg % 2}", 16))

        rd_done = [None, None]
        pw = None
        for gg in range(4):
            lt = wf_load(gg, [rd_done[gg % 2]] if rd_done[gg % 2] else [])
            P.op("vector", lambda e, w=wfb[gg % 2]: e.tensor_reduce(
                redc, w, AX.X, ALU.add, apply_absolute_value=True),
                waits=[lt] + ([pw] if pw else []), inc=("V", 1))
            rd_done[gg % 2] = P.tok("V")
            if gg == 0:
                P.op("scalar", lambda e: e.activation(accc, redc, AF.Copy),
                     waits=[P.tok("V")], inc=("S", 1))
            else:
                P.op("scalar", lambda e: e.add(accc, redc, accc),
                     waits=[P.tok("V")], inc=("S", 1))
            pw = P.tok("S")
        P.op("tensor", lambda e: e.matmul(pb[7][0:1, 0:1], c_ones64[:, 0:1],
                                          accc, start=True, stop=True),
             waits=[pw], inc=("T", 1))
        P.op("vector", lambda e: e.tensor_copy(tot, pb[7][0:1, 0:1]),
             waits=[P.tok("T")], inc=("V", 1))
        P.op("vector", lambda e: e.tensor_scalar(
            srow[0:1, 0:1], tot, 1.0 / (D * D), None, ALU.mult), inc=("V", 1))
        P.op("vector", lambda e: e.tensor_scalar(
            srow[0:1, 1:2], srow[0:1, 0:1], FEPS, None, ALU.add),
            inc=("V", 1))
        P.op("vector", lambda e: e.reciprocal(srow[0:1, 1:2], srow[0:1, 1:2]),
             inc=("V", 1))
        P.op("tensor", lambda e: e.matmul(pb[7][:, 4:6], c_ones1, srow,
                                          start=True, stop=True),
             waits=[P.tok("V")], inc=("T", 1))
        P.op("vector", lambda e: e.tensor_copy(sbc, pb[7][:, 4:6]),
             waits=[P.tok("T")], inc=("V", 1))
        sc_tok = P.tok("V")

        gts = cv.alloc("S", 128, 512, F32)
        lts = cv.alloc("S", 128, 512, F32)
        wq_gtok = [None] * 4
        pw = [None, None]
        for gg in range(4):
            lt = wf_load(gg, [sc_tok] + ([pw[gg % 2]] if pw[gg % 2] else []))
            for sl in range(KD):
                wsl = wfb[gg % 2][:, sl * 512:(sl + 1) * 512]
                P.op("vector", lambda e, w=wsl: e.tensor_scalar(
                    gts, w, sbc[:, 1:2], 0.5, ALU.mult, ALU.is_gt),
                    waits=[lt] if sl == 0 else [], inc=("V", 1))
                P.op("vector", lambda e, w=wsl: e.tensor_scalar(
                    lts, w, sbc[:, 1:2], -0.5, ALU.mult, ALU.is_lt),
                    inc=("V", 1))
                P.op("vector", lambda e: e.tensor_tensor(
                    gts, gts, lts, ALU.subtract), inc=("V", 1))
                P.op("vector", lambda e,
                     o=wq[:, gg * KD * 512 + sl * 512:
                          gg * KD * 512 + (sl + 1) * 512]:
                     e.tensor_scalar(o, gts, sbc[:, 0:1], None, ALU.mult),
                     inc=("V", 1))
            pw[gg % 2] = P.tok("V")
            wq_gtok[gg] = P.tok("V")

        def wq_rhs(k, gg):
            b = gg * KD * 512 + k * 512
            return wq[:, b:b + 512]

        xn_free = [None, None]
        bank_free = [None, None, None, None]
        tb_free = [None, None]
        g_free = [[None, None] for _ in range(4)]
        for c in range(8):
            par = c % 2
            w0 = [] if xn_free[par] is None else [xn_free[par]]
            for k in range(KD):
                P.op("sync", lambda e, d=xnfm[par][:, k * 128:(k + 1) * 128],
                     s=xn16_h[c * 128:(c + 1) * 128, k * 128:(k + 1) * 128]:
                     e.dma_start(d, s, transpose=True),
                     waits=w0 if k == 0 else [], inc=(f"XN{par}", 16))
            xl = P.tok(f"XN{par}")
            for gg in range(4):
                mw = [xl, wq_gtok[gg]]
                if bank_free[gg]:
                    mw.append(bank_free[gg])
                for k in range(KD):
                    P.op("tensor", lambda e, p=pb[gg][:, :],
                         l=xnfm[par][:, k * 128:(k + 1) * 128],
                         r=wq_rhs(k, gg), st=(k == 0), sp=(k == KD - 1):
                         e.matmul(p, l, r, start=st, stop=sp),
                         waits=mw if k == 0 else [], inc=("T", 1))
            mm = P.tok("T")
            xn_free[par] = mm
            for gg in range(4):
                tp = gg % 2
                vw = [mm, bb_tok]
                if tb_free[tp]:
                    vw.append(tb_free[tp])
                P.op("vector", lambda e, o=tball[tp], p=pb[gg][:, :],
                     b=bb[:, gg * 512:(gg + 1) * 512]:
                     e.tensor_tensor(o, p, b, ALU.add), waits=vw,
                     inc=("V", 1))
                bank_free[gg] = P.tok("V")
                sw = [P.tok("V")]
                if g_free[gg][0]:
                    sw.append(g_free[gg][0])
                P.op("scalar", lambda e, o=g16[gg], i=tball[tp]:
                     e.activation(o, i, AF.Silu), waits=sw, inc=("S", 1))
                tb_free[tp] = P.tok("S")
                g_free[gg][0] = dma(
                    "sync", g_h[c * 128:(c + 1) * 128,
                                gg * 512:(gg + 1) * 512],
                    g16[gg], f"GS{gg}", waits=[P.tok("S")])

    def stage_scan_b():
        C_all = _refs["C_all"]
        Eg_all = _refs["Eg_all"]
        cv.reset("S")
        cv.reset("H")
        Hi = cv.alloc("H", 128, DI, F32)
        Hi16 = cv.alloc("H", 128, DI, BF16)
        y = cv.alloc("H", 128, DI, F32)
        dump = cv.alloc("H", 128, DI, F32)
        hjb = [cv.alloc("S", 128, 2 * DI, F32) for _ in range(2)]
        dtl = cv.alloc("S", 128, DI, F32)

        P.op("vector", lambda e: e.memset(Hi, 0.0), inc=("V", 1))
        hj_free = [None, None]
        for j in range(7):
            par = j % 2
            lw = [P.tok("CC")]
            if hj_free[par]:
                lw.append(hj_free[par])
            lt = dma("sync", hjb[par], agout[j * 128:(j + 1) * 128, :],
                     f"HJ{par}", waits=lw)
            P.op("vector", lambda e, i=hjb[par][:, DI:2 * DI],
                 m=c_mcol[:, j:j + 1], mi=c_minv[:, j:j + 1]:
                 e.tensor_scalar(dtl, i, m, mi, ALU.mult, ALU.add),
                 waits=[lt], inc=("V", 1))
            if j > 0:
                P.op("vector", lambda e: e.tensor_tensor(Hi, Hi, dtl,
                                                         ALU.mult),
                     inc=("V", 1))
            P.op("vector", lambda e, i=hjb[par][:, 0:DI],
                 m=c_mcol[:, j:j + 1]: e.scalar_tensor_tensor(
                     Hi, i, m, Hi, ALU.mult, ALU.add), inc=("V", 1))
            hj_free[par] = P.tok("V")
        P.op("vector", lambda e: e.tensor_copy(Hi16, Hi), inc=("V", 1))
        hi_tok = P.tok("V")

        cv.reset("S")
        sz = [cv.alloc("S", 128, DI, BF16) for _ in range(2)]
        ypl = [cv.alloc("S", 128, DI, F32) for _ in range(2)]
        yn16 = cv.alloc("S", 128, DI, BF16)
        nwb = cv.alloc("S", 128, DI, F32)
        ssc = cv.alloc("S", 128, 1, F32)
        msb = cv.alloc("S", 128, 1, F32)
        sse = cv.alloc("S", 128, 1, F32)
        r2 = cv.alloc("S", 128, 1, F32)
        nw_tok = dma("sync", nwb, nw_bc[:, :], "LB", waits=[hi_tok])

        buf_free = [None, None]
        yn_sp = None
        bank_rd = [None, None]
        for c in range(NCK):
            par = c % 2
            lw = [hi_tok]
            if buf_free[par]:
                lw.append(buf_free[par])
            dma("sync", ypl[par], yp_h[c * 128:(c + 1) * 128, :],
                f"YP{par}", waits=lw)
            dma("sync", sz[par], sz_h[c * 128:(c + 1) * 128, :], f"YP{par}")
            ld = P.tok(f"YP{par}")
            for g in range(8):
                gp = g % 2
                bA = pb[2 + gp]
                ecols = P.tok("V")
                wA = [hi_tok]
                if bank_rd[gp]:
                    wA.append(bank_rd[gp])
                P.op("tensor", lambda e, p=bA[:, :],
                     l=C_all[:, c * 128:(c + 1) * 128],
                     r=Hi16[:, g * 512:(g + 1) * 512]:
                     e.matmul(p, l, r, start=True, stop=True), waits=wA,
                     inc=("T", 1))
                mt = P.tok("T")
                yw = [mt, ecols, ld]
                if yn_sp and g == 0:
                    yw.append(yn_sp)
                P.op("vector", lambda e,
                     o=y[:, g * 512:(g + 1) * 512].rearrange(
                         "p (h i) -> p h i", h=8),
                     p=bA[:, :].rearrange("p (h i) -> p h i", h=8),
                     ee=bass.AP(Eg_all.tensor, Eg_all.offset + c * 64 + g * 8,
                                [[Eg_all.tensor.shape[1], 128], [1, 8],
                                 [0, 64]]):
                     e.tensor_tensor(o, p, ee, ALU.mult),
                     waits=yw, inc=("V", 1))
                bank_rd[gp] = P.tok("V")
                P.op("vector", lambda e, o=y[:, g * 512:(g + 1) * 512],
                     i=ypl[par][:, g * 512:(g + 1) * 512]:
                     e.tensor_tensor(o, o, i, ALU.add), inc=("V", 1))
            P.op("vector", lambda e, s=sz[par]: e.tensor_tensor(
                y, y, s, ALU.mult), inc=("V", 1))
            buf_free[par] = P.tok("V")
            rms_scale(y, DI, dump, ssc, msb, sse, r2, [P.tok("V")])
            P.op("vector", lambda e: e.scalar_tensor_tensor(
                yn16, y, r2, nwb, ALU.mult, ALU.mult),
                waits=[P.tok("S"), nw_tok] + ([yn_sp] if yn_sp else []),
                inc=("V", 1))
            yn_sp = dma("sync", yn16_h[c * 128:(c + 1) * 128, :], yn16, "YN",
                        waits=[P.tok("V")])

    def stage_outproj():
        cv.reset("S")
        cv.reset("H")
        cv.reset("B")
        hs = cv.alloc("H", 128, 8 * D, F32)
        x2fm = cv.alloc("B", 128, KD * T, BF16)
        _refs["hs"] = hs
        _refs["x2fm"] = x2fm
        wop = cv.alloc("S", 128, 32 * 512, BF16)
        ynb = cv.alloc("S", 128, 32 * 512, BF16)
        xnl = cv.alloc("S", 128, 512, F32)
        dump = cv.alloc("S", 128, D, F32)
        ssc = cv.alloc("S", 128, 1, F32)
        msb = cv.alloc("S", 128, 1, F32)
        sse = cv.alloc("S", 128, 1, F32)
        r2c = cv.alloc("S", 128, 1, F32)
        lnbc2 = cv.alloc("S", 128, D, F32)
        x2t = cv.alloc("S", 128, D, BF16)
        dma("sync", lnbc2, ln2_bc[:, :], "LB")
        ln2_tok = P.tok("LB")

        mm_done = None
        yn_used = None
        for tb in range(2):
            w0 = [] if yn_used is None else [yn_used]
            for b in range(32):
                P.op("sync",
                     lambda e, d=ynb[:, b * 512:(b + 1) * 512],
                     s=yn16_h[tb * 512:(tb + 1) * 512,
                              b * 128:(b + 1) * 128]:
                     e.dma_start(d, s, transpose=True),
                     waits=w0 if b < 2 else [], inc=("YB", 16))
            yl = P.tok("YB")
            for ft in range(4):
                wl = wload(wopT, ft * 512, 512, wop, "W0", DI,
                           waits=[mm_done] if mm_done else [])
                for cc in range(4):
                    c = tb * 4 + cc
                    bank = pb[cc % 2]
                    lx = dma("sync", xnl,
                             xn_h[c * 128:(c + 1) * 128,
                                  ft * 512:(ft + 1) * 512],
                             "XL", waits=[P.tok("V")])
                    for k in range(32):
                        P.op("tensor", lambda e, p=bank[:, :],
                             l=ynb[:, k * 512 + cc * 128:
                                    k * 512 + (cc + 1) * 128],
                             r=wop[:, k * 512:(k + 1) * 512],
                             st=(k == 0), sp=(k == 31):
                             e.matmul(p, l, r, start=st, stop=sp),
                             waits=[yl, wl, P.tok("V")] if k == 0 else [],
                             inc=("T", 1))
                    mm = P.tok("T")
                    P.op("vector", lambda e,
                         o=hs[:, c * D + ft * 512:c * D + (ft + 1) * 512],
                         p=bank[:, :], x=xnl:
                         e.tensor_tensor(o, p, x, ALU.add),
                         waits=[mm, lx], inc=("V", 1))
                mm_done = P.tok("V")
            yn_used = P.tok("T")
        hs_tok = P.tok("V")
        if hs_h is not None:
            for c in range(8):
                dma("sync", hs_h[c * 128:(c + 1) * 128, :],
                    hs[:, c * D:(c + 1) * D], "SD", waits=[hs_tok])

        x2_sp = None
        for c in range(8):
            hsl = hs[:, c * D:(c + 1) * D]
            rms_scale(hsl, D, dump, ssc, msb, sse, r2c,
                      [hs_tok, P.tok("V")])
            vw = [P.tok("S"), ln2_tok]
            if x2_sp:
                vw.append(x2_sp)
            P.op("vector", lambda e, o=x2t, i=hsl:
                 e.scalar_tensor_tensor(o, i, r2c, lnbc2, ALU.mult, ALU.mult),
                 waits=vw, inc=("V", 1))
            x2_sp = dma("sync", x2_h[c * 128:(c + 1) * 128, :], x2t,
                        "X2S", waits=[P.tok("V")])

    def stage_ffn():
        hs = _refs["hs"]
        x2fm = _refs["x2fm"]
        cv.reset("S")
        hgu = cv.alloc("S", 128, KD * T, BF16)
        wb2 = [cv.alloc("S", 128, KD * 512, BF16) for _ in range(2)]
        wdb2 = [cv.alloc("S", 128, 8 * 512, BF16) for _ in range(2)]
        gt = [cv.alloc("S", 128, 512, F32) for _ in range(2)]

        for k in range(KD):
            for tb in range(2):
                P.op("sync", lambda e,
                     d=x2fm[:, k * T + tb * 512:k * T + (tb + 1) * 512],
                     s=x2_h[tb * 512:(tb + 1) * 512, k * 128:(k + 1) * 128]:
                     e.dma_start(d, s, transpose=True), inc=("X2", 16))
        x2tok = P.tok("X2")

        bufA_free = None
        bufB_free = None
        pg_rd = [None, None]
        pu_rd = [None, None]
        gt_rd = [None, None]
        dbank_rd = [None, None]
        wd_free = [None, None]
        for sl in range(4):
            hgu_tok = None
            for gg in range(4):
                f0 = sl * 2048 + gg * 512
                la = wload(wgT, f0, 512, wb2[0], "WA", D,
                           waits=[bufA_free] if bufA_free else [])
                lb = wload(wuT, f0, 512, wb2[1], "WB", D,
                           waits=[bufB_free] if bufB_free else [])
                for m in range(8):
                    ic, tb = m // 2, m % 2
                    i = gg * 4 + ic
                    mp = m % 2
                    bg = pb[2 * mp]
                    bu = pb[2 * mp + 1]
                    mw = [la, x2tok]
                    if pg_rd[mp]:
                        mw.append(pg_rd[mp])
                    for k in range(KD):
                        P.op("tensor", lambda e, p=bg[:, :],
                             l=wb2[0][:, k * 512 + ic * 128:
                                      k * 512 + (ic + 1) * 128],
                             r=x2fm[:, k * T + tb * 512:
                                    k * T + (tb + 1) * 512],
                             st=(k == 0), sp=(k == KD - 1):
                             e.matmul(p, l, r, start=st, stop=sp),
                             waits=mw if k == 0 else [], inc=("T", 1))
                    uw = [lb]
                    if pu_rd[mp]:
                        uw.append(pu_rd[mp])
                    for k in range(KD):
                        P.op("tensor", lambda e, p=bu[:, :],
                             l=wb2[1][:, k * 512 + ic * 128:
                                      k * 512 + (ic + 1) * 128],
                             r=x2fm[:, k * T + tb * 512:
                                    k * T + (tb + 1) * 512],
                             st=(k == 0), sp=(k == KD - 1):
                             e.matmul(p, l, r, start=st, stop=sp),
                             waits=uw if k == 0 else [], inc=("T", 1))
                    mm = P.tok("T")
                    sw = [mm]
                    if gt_rd[mp]:
                        sw.append(gt_rd[mp])
                    P.op("scalar", act(gt[mp], bg[:, :], AF.Silu),
                         waits=sw, inc=("S", 1))
                    pg_rd[mp] = P.tok("S")
                    P.op("vector", lambda e,
                         o=hgu[:, i * T + tb * 512:i * T + (tb + 1) * 512],
                         g=gt[mp], p=bu[:, :]:
                         e.tensor_tensor(o, g, p, ALU.mult),
                         waits=[P.tok("S")], inc=("V", 1))
                    pu_rd[mp] = P.tok("V")
                    gt_rd[mp] = pu_rd[mp]
                bufA_free = P.tok("T")
                bufB_free = P.tok("T")
            hgu_tok = P.tok("V")
            for ft in range(4):
                par = ft % 2
                src = bass.AP(wdT, (sl * 2048) * D + ft * 512,
                              [[D, 128], [128 * D, KD], [1, 512]])
                wdst = wb2[par]
                lw = [bufA_free if par == 0 else bufB_free]
                d3 = wdst.rearrange("p (c j) -> p c j", j=512)
                lt = P.op("gpsimd", lambda e, d=d3, s=src: e.dma_start(d, s),
                          waits=lw + ([P.tok("T")] if sl or ft else []),
                          inc=(f"WD{par}", 16))
                for c in range(8):
                    bank = pb[4 + par]
                    bw = [lt, hgu_tok]
                    if dbank_rd[par]:
                        bw.append(dbank_rd[par])
                    for k in range(KD):
                        P.op("tensor", lambda e, p=bank[:, :],
                             l=hgu[:, k * T + c * 128:k * T + (c + 1) * 128],
                             r=wdst[:, k * 512:(k + 1) * 512],
                             st=(k == 0), sp=(k == KD - 1):
                             e.matmul(p, l, r, start=st, stop=sp),
                             waits=bw if k == 0 else [], inc=("T", 1))
                    mm = P.tok("T")
                    P.op("vector", lambda e,
                         o=hs[:, c * D + ft * 512:c * D + (ft + 1) * 512],
                         p=bank[:, :]: e.tensor_tensor(o, o, p, ALU.add),
                         waits=[mm], inc=("V", 1))
                    dbank_rd[par] = P.tok("V")
                if par == 0:
                    bufA_free = P.tok("T")
                else:
                    bufB_free = P.tok("T")

    def stage_final():
        hs = _refs["hs"]
        cv.reset("S")
        xl = [cv.alloc("S", 128, D, F32) for _ in range(2)]
        gl = [cv.alloc("S", 128, D, BF16) for _ in range(2)]
        ot = [cv.alloc("S", 128, D, F32) for _ in range(2)]
        of = [None, None]
        for c in range(8):
            par = c % 2
            lw = [] if of[par] is None else [of[par]]
            dma("sync", xl[par], xin[3 + c * 128:3 + (c + 1) * 128, :],
                f"F{par}", waits=lw)
            dma("sync", gl[par], g_h[c * 128:(c + 1) * 128, :], f"F{par}")
            ld = P.tok(f"F{par}")
            P.op("vector", lambda e, o=ot[par], s=hs[:, c * D:(c + 1) * D],
                 g=gl[par]: e.tensor_tensor(o, s, g, ALU.mult),
                 waits=[ld] + ([of[par]] if of[par] else []), inc=("V", 1))
            P.op("vector", lambda e, o=ot[par], x=xl[par]:
                 e.tensor_tensor(o, o, x, ALU.add), inc=("V", 1))
            of[par] = dma("sync", out_h[c * 128:(c + 1) * 128, :], ot[par],
                          f"OD{par}", waits=[P.tok("V")])
        P.op("sync", None, waits=[P.tok("OD0"), P.tok("OD1")])

    stages = [stage_ln, stage_inproj, stage_scan_a, stage_g, stage_scan_b,
              stage_outproj, stage_ffn, stage_final]
    P.barrier()
    for i, st in enumerate(stages):
        if i >= upto:
            break
        st()
        P.barrier()

    P.emit(nc)
    return nc


def _prep(inputs):
    x = np.asarray(inputs["L0_attn_hidden_states"], np.float32)
    w = {k: np.asarray(v, np.float32) for k, v in inputs.items()}

    sh = {}
    sh["wipT"] = np.ascontiguousarray(w["L0_attn_in_proj_w"].T)
    sh["wopT"] = np.ascontiguousarray(w["L0_attn_out_proj_w"].T)
    sh["wgT"] = np.ascontiguousarray(w["L0_attn_gate_w"].T)
    sh["wuT"] = np.ascontiguousarray(w["L0_attn_up_w"].T)
    sh["wdT"] = np.ascontiguousarray(w["L0_attn_down_w"].T)
    sh["wfT"] = np.ascontiguousarray(w["L0_ffn_weight"].T)
    sh["convw"] = np.ascontiguousarray(
        w["L0_attn_conv_w"].reshape(34, 128, 4).transpose(1, 0, 2)
        .reshape(128, 34 * 4))
    sh["convb"] = np.ascontiguousarray(w["L0_attn_conv_b"].reshape(34, 128).T)
    sh["dtb"] = np.ascontiguousarray(w["L0_attn_dt_bias"].reshape(64, 1))
    sh["alog"] = np.ascontiguousarray(w["L0_attn_A_log"].reshape(64, 1))
    sh["dexp_bc"] = np.ascontiguousarray(np.broadcast_to(
        np.repeat(w["L0_attn_D_param"], HD)[None, :], (128, DI)))
    sh["nw_bc"] = np.ascontiguousarray(np.broadcast_to(
        w["L0_attn_norm_w"][None, :], (128, DI)))
    sh["ln1_bc"] = np.ascontiguousarray(np.broadcast_to(
        w["L0_attn_ln1_w"][None, :], (128, D)))
    sh["ln2_bc"] = np.ascontiguousarray(np.broadcast_to(
        w["L0_attn_ln2_w"][None, :], (128, D)))
    sh["bias_bc"] = np.ascontiguousarray(np.broadcast_to(
        w["L0_ffn_bias"][None, :], (128, D)))
    r = np.arange(128)
    sh["cle"] = (r[:, None] <= r[None, :]).astype(np.float32)
    sh["cle16"] = (r[:, None] <= r[None, :]).astype(BF16NP)
    sh["up16"] = (r[:, None] > r[None, :]).astype(BF16NP)
    sh["iden"] = np.eye(128, dtype=np.float32)
    sh["ones64"] = np.ones((128, 64), np.float32)
    sh["ones1"] = np.ones((1, 128), np.float32)

    in_maps = []
    for q in range(8):
        b, rq = q // 4, q % 4
        sl = np.zeros((TH, D), np.float32)
        t0 = rq * 1024
        if rq == 0:
            sl[3:] = x[b, 0:1024]
        else:
            sl[:] = x[b, t0 - 3:t0 + 1024]
        m = np.zeros(7, np.float32)
        for j in range(7):
            if j // 4 == b and (j % 4) < rq:
                m[j] = 1.0
        d = dict(sh)
        d["xin"] = np.ascontiguousarray(sl)
        d["mcol"] = np.ascontiguousarray(np.broadcast_to(m[None, :],
                                                         (128, 7)))
        d["minv"] = np.ascontiguousarray(np.broadcast_to((1.0 - m)[None, :],
                                                         (128, 7)))
        in_maps.append(d)
    return in_maps


def kernel(**inputs):
    in_maps = _prep(inputs)
    nc = build()
    res = run_bass_kernel_spmd(nc, in_maps, core_ids=list(range(8)))
    out = np.empty((2, 4096, D), np.float32)
    for q in range(8):
        b, rq = q // 4, q % 4
        out[b, rq * 1024:(rq + 1) * 1024] = np.asarray(
            res.results[q]["out_h"], np.float32)
    return out


# version 09
# speedup vs baseline: 1.0136x; 1.0136x over previous
import numpy as np
import ml_dtypes
import concourse.bass as bass
import concourse.mybir as mybir
from concourse.bass_utils import run_bass_kernel_spmd

F32 = mybir.dt.float32
BF16 = mybir.dt.bfloat16
AF = mybir.ActivationFunctionType
ALU = mybir.AluOpType
AX = mybir.AxisListType
BF16NP = ml_dtypes.bfloat16

D = 2048
DI = 4096
DS = 128
HD = 64
NH = 64
INTER = 8192
CONV = DI + 2 * DS
DIP = 2 * DI + 2 * DS + NH
EPS = 1e-5
FEPS = 1e-10
T = 1024
TH = T + 3
NCK = 8
KD = D // 128

ENGINES = ("sync", "gpsimd", "vector", "scalar", "tensor")


class Plan:
    def __init__(self):
        self.ops = []
        self.cnt = {}

    SELF_SEM = {"vector": "V", "scalar": "S", "gpsimd": "G"}

    def op(self, eng, fn, waits=(), inc=None):
        ws = [(s, v) for (s, v) in waits if v > 0]
        sem = self.SELF_SEM.get(eng)
        if sem is not None and fn is not None:
            if eng != "gpsimd" or (inc is not None and inc[0] == "G"):
                v = self.cnt.get(sem, 0)
                if v > 0:
                    ws.append((sem, v))
        self.ops.append((eng, fn, ws, inc))
        if inc is not None:
            s, amt = inc
            self.cnt[s] = self.cnt.get(s, 0) + amt
            return (s, self.cnt[s])
        return None

    def tok(self, sem):
        return (sem, self.cnt.get(sem, 0))

    def barrier(self):
        toks = [(s, v) for s, v in self.cnt.items() if s != "CC"]
        for eng in ENGINES:
            self.op(eng, None, waits=toks)

    def emit(self, nc):
        from contextlib import ExitStack
        sems = {}
        with ExitStack() as ctx:
            names = set(self.cnt.keys())
            for (_, _, ws, _) in self.ops:
                names.update(s for s, _ in ws)
            for s in sorted(names):
                sems[s] = ctx.enter_context(nc.semaphore(s))
            block = ctx.enter_context(nc.Block())

            def mk(eng_name):
                def emit_eng(eng):
                    for (e, fn, ws, inc) in self.ops:
                        if e != eng_name:
                            continue
                        for s, v in ws:
                            eng.wait_ge(sems[s], v)
                        if fn is not None:
                            inst = fn(eng)
                            if inc is not None:
                                inst.then_inc(sems[inc[0]], inc[1])
                return emit_eng

            block.sync(mk("sync"))
            block.gpsimd(mk("gpsimd"))
            block.vector(mk("vector"))
            block.scalar(mk("scalar"))
            block.tensor(mk("tensor"))


class Carver:

    def __init__(self, nc, total_bytes):
        self.h32 = nc.alloc_sbuf_tensor("arena", [128, total_bytes // 4], F32)
        self.h16 = self.h32.bitcast(BF16)
        self.segs = {}

    def seg(self, name, start, end):
        self.segs[name] = [start, start, end]

    def reset(self, name):
        self.segs[name][1] = self.segs[name][0]

    def alloc(self, seg, p, n, dt, align=64):
        s = self.segs[seg]
        off = (s[1] + align - 1) // align * align
        size = n * (4 if dt == F32 else 2)
        assert off + size <= s[2], f"seg {seg} overflow {off}+{size}>{s[2]}"
        s[1] = off + size
        h = self.h32 if dt == F32 else self.h16
        es = 4 if dt == F32 else 2
        return h[0:p, off // es: off // es + n]


def build(upto=8, dbg=(), nocc=False, nch=NCK, noxl=False, sp=4, noseg=False):
    nc = bass.Bass("TRN2", target_bir_lowering=False)

    def din(name, shape, dt=F32):
        return nc.dram_tensor(name, shape, dt, kind="ExternalInput")

    def dscr(name, shape, dt):
        kind = "ExternalOutput" if name in dbg else "Internal"
        return nc.dram_tensor(name, shape, dt, kind=kind)

    xin = din("xin", [TH, D])
    wipT = din("wipT", [D, DIP])
    wopT = din("wopT", [DI, D])
    wgT = din("wgT", [D, INTER])
    wuT = din("wuT", [D, INTER])
    wdT = din("wdT", [INTER, D])
    wfT = din("wfT", [D, D])
    convw = din("convw", [128, 34 * 4])
    convb = din("convb", [128, 34])
    dtb = din("dtb", [64, 1])
    alog = din("alog", [64, 1])
    dexp_bc = din("dexp_bc", [128, DI])
    nw_bc = din("nw_bc", [128, DI])
    ln1_bc = din("ln1_bc", [128, D])
    ln2_bc = din("ln2_bc", [128, D])
    bias_bc = din("bias_bc", [128, D])
    cle = din("cle", [128, 128])
    cle16 = din("cle16", [128, 128], BF16)
    up16 = din("up16", [128, 128], BF16)
    iden = din("iden", [128, 128])
    ones64 = din("ones64", [128, 64])
    ones1 = din("ones1", [1, 128])
    mcol = din("mcol", [128, 7])
    minv = din("minv", [128, 7])

    xn_h = dscr("xn_h", [T, D], F32)
    xh1_h = dscr("xh1_h", [TH, D], BF16)
    xn16_h = dscr("xn16_h", [T, D], BF16)
    xbcp_h = dscr("xbcp_h", [CONV, T], BF16)
    sz_h = dscr("sz_h", [T, DI], BF16)
    yp_h = dscr("yp_h", [T, DI], F32)
    yn16_h = dscr("yn16_h", [T, DI], BF16)
    x2_h = dscr("x2_h", [T, D], BF16)
    g_h = dscr("g_h", [T, D], BF16)
    hs_h = dscr("hs_h", [T, D], F32) if "hs_h" in dbg else None
    dtp_h = dscr("dtp_h", [64, T], F32) if "dtp_h" in dbg else None
    agin = nc.dram_tensor("agin", [128, 2 * DI], F32)
    agout = nc.dram_tensor("agout", [8 * 128, 2 * DI], F32)
    out_h = nc.dram_tensor("out_h", [T, D], F32, kind="ExternalOutput")

    pb = [nc.alloc_psum_tensor(f"pb{i}", [128, 512], F32) for i in range(8)]

    KB = 1024
    cv = Carver(nc, 212000)
    cv.seg("C", 0, 4 * KB)
    cv.seg("A", 4 * KB, 17 * KB)
    cv.seg("B", 17 * KB, 51 * KB)
    cv.seg("H", 51 * KB, 115 * KB)
    cv.seg("S", 115 * KB, 212000)

    P = Plan()
    _refs = {}

    c_cle = cv.alloc("C", 128, 128, F32)
    c_cle16 = cv.alloc("C", 128, 128, BF16)
    c_up = cv.alloc("C", 128, 128, BF16)
    c_iden = cv.alloc("C", 128, 128, F32)
    c_ones64 = cv.alloc("C", 128, 64, F32)
    c_ones1 = cv.alloc("C", 1, 128, F32)
    c_convw = cv.alloc("C", 128, 34 * 4, F32)
    c_convb = cv.alloc("C", 128, 34, F32)
    c_dtb = cv.alloc("C", 64, 1, F32)
    c_alog = cv.alloc("C", 64, 1, F32)
    c_nega = cv.alloc("C", 64, 1, F32)
    c_mcol = cv.alloc("C", 128, 7, F32)
    c_minv = cv.alloc("C", 128, 7, F32)

    a_hm = cv.alloc("A", 64, T, F32)
    dtp_hm = cv.alloc("A", 64, T, F32)
    v_hm = cv.alloc("A", 64, T, F32)
    offac = cv.alloc("A", 64, 1, F32)
    fcell = cv.alloc("C", 1, 1, F32)

    def sfence(waits=()):
        return P.op("scalar", lambda e: e.mul(fcell, fcell, 1.0),
                    waits=list(waits), inc=("S", 1))

    def dma(eng, dst, src, sem, waits=(), transpose=False):
        if transpose:
            fn = lambda e, d=dst, s=src: e.dma_start(d, s, transpose=True)
        else:
            fn = lambda e, d=dst, s=src: e.dma_start(d, s)
        return P.op(eng, fn, waits=waits, inc=(sem, 16))

    def wload(src_h, f0, width, dst_ap, sem, DD, waits=(), cast=True):
        C = DD // 128
        Fw = src_h.shape[1]
        src = bass.AP(src_h, f0, [[Fw, 128], [128 * Fw, C], [1, width]])
        d3 = dst_ap.rearrange("p (c j) -> p c j", j=width)
        eng = "gpsimd" if cast else "sync"
        return P.op(eng, lambda e, d=d3, s=src: e.dma_start(d, s),
                    waits=waits, inc=(sem, 16))

    def act(out, in_, f, bias=0.0, scale=1.0, accum=None):
        return lambda e, o=out, i=in_, ff=f, b=bias, sc=scale, a=accum: \
            e.activation(o, i, ff, bias=b, scale=sc, accum_out=a)

    def rms_scale(src, n, dump_ap, c_ss, c_ms, c_e, out_r, waits):
        P.op("scalar", act(dump_ap, src, AF.Square), waits=list(waits),
             inc=("S", 1))
        P.op("vector", lambda e, o=c_ss, i=dump_ap: e.reduce_sum(
            o, i, AX.X), waits=[P.tok("S")], inc=("V", 1))
        P.op("scalar", lambda e, o=c_ms, i=c_ss: e.activation(
            o, i, AF.Copy, scale=1.0 / n), waits=[P.tok("V")], inc=("S", 1))
        P.op("vector", lambda e, o=c_e, a=c_ms: e.tensor_scalar(
            o, a, 1.0, EPS, ALU.mult, ALU.add), waits=[P.tok("S")],
            inc=("V", 1))
        P.op("scalar", lambda e, o=c_ms, a=c_e: e.activation(o, a, AF.Ln),
             waits=[P.tok("V")], inc=("S", 1))
        return P.op("scalar", lambda e, o=out_r, a=c_ms: e.activation(
            o, a, AF.Exp, scale=-0.5), inc=("S", 1))

    for dst, src in [(c_cle, cle[:, :]), (c_cle16, cle16[:, :]),
                     (c_up, up16[:, :]),
                     (c_iden, iden[:, :]), (c_ones64, ones64[:, :]),
                     (c_ones1, ones1[:, :]), (c_convw, convw[:, :]),
                     (c_convb, convb[:, :]), (c_dtb, dtb[:, :]),
                     (c_alog, alog[:, :]), (c_mcol, mcol[:, :]),
                     (c_minv, minv[:, :])]:
        dma("sync", dst, src, "SD")

    def stage_ln():
        cv.reset("S")
        cv.reset("B")
        x1fm = cv.alloc("B", 128, KD * 1040, BF16)
        _refs["x1fm"] = x1fm
        xt = [cv.alloc("S", 128, D, F32) for _ in range(2)]
        xnt = [cv.alloc("S", 128, D, F32) for _ in range(2)]
        xh1t = [cv.alloc("S", 128, D, BF16) for _ in range(2)]
        xn16t = [cv.alloc("S", 128, D, BF16) for _ in range(2)]
        dumps = [cv.alloc("S", 128, D, F32) for _ in range(2)]
        cols = [{n: cv.alloc("S", 128, 1, F32)
                 for n in ("sumc", "ssc", "mean", "msq", "m2", "t1", "vpe",
                           "lnv", "rstd", "negmr", "ms1", "msm", "mse", "r1")}
                for _ in range(2)]
        lnbc = cv.alloc("S", 128, D, F32)
        dma("sync", lnbc, ln1_bc[:, :], "LB")
        lnbc_tok = P.tok("LB")
        P.op("vector", lambda e: e.memset(xh1t[0][:, :], 0.0), inc=("V", 1))
        P.op("vector", lambda e: e.memset(xh1t[1][:, :], 0.0), inc=("V", 1))

        tiles = [(-1, 0, 3)] + [(c, 3 + c * 128, 128) for c in range(8)]
        tile_done = [None, None]
        spillt = [None, None]
        xpt = [None, None]
        for idx, (c, r0, rows) in enumerate(tiles):
            par = idx % 2
            cl = {n: a[0:rows, :] for n, a in cols[par].items()}
            xb = xt[par][0:rows, :]
            dump = dumps[par]
            gw = [] if tile_done[par] is None else [tile_done[par]]
            lt = dma("sync", xb, xin[r0:r0 + rows, :], f"L{par}", waits=gw)
            P.op("scalar", act(dump[0:rows, :], xb, AF.Square),
                 waits=[lt] + gw, inc=("S", 1))
            P.op("vector", lambda e, a=cl["ssc"], i=dump[0:rows, :]:
                 e.reduce_sum(a, i, AX.X), waits=[P.tok("S")], inc=("V", 1))
            P.op("vector", lambda e, a=cl["sumc"], b=xb: e.reduce_sum(
                a, b, AX.X), waits=[lt] + gw, inc=("V", 1))
            P.op("scalar", lambda e, o=cl["mean"], i=cl["sumc"]:
                 e.activation(o, i, AF.Copy, scale=1.0 / D),
                 waits=[P.tok("V")], inc=("S", 1))
            P.op("scalar", lambda e, o=cl["msq"], i=cl["ssc"]:
                 e.activation(o, i, AF.Copy, scale=1.0 / D), inc=("S", 1))
            sq_tok = P.tok("S")
            P.op("vector", lambda e, o=cl["m2"], m=cl["mean"]:
                 e.tensor_tensor(o, m, m, ALU.mult), waits=[sq_tok],
                 inc=("V", 1))
            P.op("vector", lambda e, o=cl["t1"], a=cl["m2"]: e.tensor_scalar(
                o, a, -1.0, EPS, ALU.mult, ALU.add), inc=("V", 1))
            P.op("vector", lambda e, o=cl["vpe"], a=cl["msq"], b=cl["t1"]:
                 e.tensor_tensor(o, a, b, ALU.add), inc=("V", 1))
            P.op("scalar", lambda e, o=cl["lnv"], a=cl["vpe"]: e.activation(
                o, a, AF.Ln), waits=[P.tok("V")], inc=("S", 1))
            P.op("scalar", lambda e, o=cl["rstd"], a=cl["lnv"]:
                 e.activation(o, a, AF.Exp, scale=-0.5), inc=("S", 1))
            P.op("vector", lambda e, o=cl["negmr"], m=cl["mean"],
                 r=cl["rstd"]:
                 e.scalar_tensor_tensor(o, m, -1.0, r, ALU.mult, ALU.mult),
                 waits=[P.tok("S")], inc=("V", 1))
            xw = [P.tok("V")]
            if spillt[par] is not None:
                xw.append(spillt[par])
            xo = xnt[par][0:rows, :]
            P.op("vector", lambda e, o=xo, i=xb, r=cl["rstd"],
                 b=cl["negmr"]:
                 e.tensor_scalar(o, i, r, b, ALU.mult, ALU.add),
                 waits=xw, inc=("V", 1))
            xn_tok = P.tok("V")
            if c >= 0:
                dma("sync", xn_h[c * 128:(c + 1) * 128, :], xo, f"SP{par}",
                    waits=[xn_tok])
            rms_scale(xo, D, dump[0:rows, :], cl["ms1"], cl["msm"],
                      cl["mse"], cl["r1"], [xn_tok])
            xw2 = [P.tok("S"), lnbc_tok]
            if xpt[par] is not None:
                xw2.append(xpt[par])
            P.op("vector", lambda e, o=xh1t[par][0:rows, :], i=xo,
                 r=cl["r1"], l=lnbc[0:rows, :]: e.scalar_tensor_tensor(
                     o, i, r, l, ALU.mult, ALU.mult),
                 waits=xw2, inc=("V", 1))
            dma("sync", xh1_h[r0:r0 + rows, :], xh1t[par][0:rows, :],
                f"XP{par}", waits=[P.tok("V")])
            if c >= 0:
                P.op("vector", lambda e, o=xn16t[par][0:rows, :], i=xo:
                     e.tensor_copy(o, i), inc=("V", 1))
                dma("sync", xn16_h[c * 128:(c + 1) * 128, :],
                    xn16t[par][0:rows, :], f"SP{par}", waits=[P.tok("V")])
                spillt[par] = P.tok(f"SP{par}")
            xpt[par] = (f"XP{par}", P.cnt[f"XP{par}"])
            tile_done[par] = P.tok("V")

    def stage_inproj():
        cv.reset("S")
        x1fm = _refs["x1fm"]
        for k in range(KD):
            b = k * 1040
            eng = "sync"
            dma(eng, x1fm[:, b:b + 16], xh1_h[0:16, k * 128:(k + 1) * 128],
                "X0", transpose=True)
            dma(eng, x1fm[:, b + 16:b + 528],
                xh1_h[3:515, k * 128:(k + 1) * 128], "X0", transpose=True)
            dma(eng, x1fm[:, b + 528:b + 1040],
                xh1_h[515:1027, k * 128:(k + 1) * 128], "X0", transpose=True)
        x1_tok = P.tok("X0")
        wz = [cv.alloc("S", 128, KD * 512, BF16) for _ in range(2)]
        wxbc = [cv.alloc("S", 128, KD * 128, BF16) for _ in range(2)]
        wdt = cv.alloc("S", 128, KD * 64, BF16)
        xbcbuf = [cv.alloc("S", 128, 1027, BF16) for _ in range(2)]
        acc = [[cv.alloc("S", 128, T, F32) for _ in range(2)] for _ in range(2)]
        xpo = [cv.alloc("S", 128, T, BF16) for _ in range(2)]
        szt = [[cv.alloc("S", 128, 512, BF16) for _ in range(8)]
               for _ in range(2)]
        tmp64 = cv.alloc("S", 64, T, F32)


        def lhs_tc(k, c):
            b = k * 1040 + 16 + c * 128
            return x1fm[:, b:b + 128]

        def rhs_main(k, half):
            b = k * 1040 + 16 + half * 512
            return x1fm[:, b:b + 512]

        def rhs_halo(k):
            return x1fm[:, k * 1040:k * 1040 + 3]

        wl = wload(wipT, 2 * DI + 2 * DS, 64, wdt, "W0", D)
        for half in range(2):
            ps = pb[6 + half][0:64, :]
            for k in range(KD):
                P.op("tensor", lambda e, p=ps, l=wdt[:, k * 64:(k + 1) * 64],
                     r=rhs_main(k, half), st=(k == 0), sp=(k == KD - 1):
                     e.matmul(p, l, r, start=st, stop=sp),
                     waits=[wl, x1_tok] if k == 0 else [], inc=("T", 1))
            P.op("vector", lambda e, o=v_hm[:, half * 512:(half + 1) * 512],
                 p=ps: e.tensor_scalar(o, p, c_dtb, None, ALU.add),
                 waits=[P.tok("T")], inc=("V", 1))
        dt_evac = P.tok("V")

        evac_t = [None, None]
        conv_t = [None, None]
        silu_t = [None, None]
        spill_t = [None, None]
        for fc in range(34):
            par = fc % 2
            wl = wload(wipT, DI + fc * 128, 128, wxbc[par], f"WX{par}", D,
                       waits=[evac_t[par]] if evac_t[par] else [])
            bs = [pb[3 * par], pb[3 * par + 1], pb[3 * par + 2]]
            w0 = [wl, x1_tok]
            if evac_t[par]:
                w0.append(evac_t[par])
            for k in range(KD):
                lt = wxbc[par][:, k * 128:(k + 1) * 128]
                P.op("tensor", lambda e, p=bs[0][:, :], l=lt, r=rhs_main(k, 0),
                     st=(k == 0), sp=(k == KD - 1):
                     e.matmul(p, l, r, start=st, stop=sp),
                     waits=w0 if k == 0 else [], inc=("T", 1))
                P.op("tensor", lambda e, p=bs[1][:, :], l=lt, r=rhs_main(k, 1),
                     st=(k == 0), sp=(k == KD - 1):
                     e.matmul(p, l, r, start=st, stop=sp), inc=("T", 1))
                P.op("tensor", lambda e, p=bs[2][:, 0:3], l=lt, r=rhs_halo(k),
                     st=(k == 0), sp=(k == KD - 1):
                     e.matmul(p, l, r, start=st, stop=sp), inc=("T", 1))
            mm = P.tok("T")
            ew = [mm]
            if conv_t[par]:
                ew.append(conv_t[par])
            xb = xbcbuf[par]
            P.op("scalar", act(xb[:, 0:3], bs[2][:, 0:3], AF.Copy),
                 waits=ew, inc=("S", 1))
            P.op("scalar", act(xb[:, 3:515], bs[0][:, :], AF.Copy),
                 inc=("S", 1))
            P.op("scalar", act(xb[:, 515:1027], bs[1][:, :], AF.Copy),
                 inc=("S", 1))
            evac_t[par] = P.tok("S")
            a0, a1 = acc[par]
            cw = [evac_t[par]]
            if silu_t[par]:
                cw.append(silu_t[par])
            P.op("vector", lambda e, o=a0, i=xb[:, 0:T],
                 w=c_convw[:, fc * 4:fc * 4 + 1]:
                 e.tensor_scalar(o, i, w, None, ALU.mult), waits=cw,
                 inc=("V", 1))
            P.op("vector", lambda e, o=a1, i=xb[:, 1:T + 1],
                 w=c_convw[:, fc * 4 + 1:fc * 4 + 2], a=a0:
                 e.scalar_tensor_tensor(o, i, w, a, ALU.mult, ALU.add),
                 inc=("V", 1))
            P.op("vector", lambda e, o=a0, i=xb[:, 2:T + 2],
                 w=c_convw[:, fc * 4 + 2:fc * 4 + 3], a=a1:
                 e.scalar_tensor_tensor(o, i, w, a, ALU.mult, ALU.add),
                 inc=("V", 1))
            P.op("vector", lambda e, o=a1, i=xb[:, 3:T + 3],
                 w=c_convw[:, fc * 4 + 3:fc * 4 + 4], a=a0:
                 e.scalar_tensor_tensor(o, i, w, a, ALU.mult, ALU.add),
                 inc=("V", 1))
            conv_t[par] = P.tok("V")
            sw = [conv_t[par]]
            if spill_t[par]:
                sw.append(spill_t[par])
            P.op("scalar", act(xpo[par], a1, AF.Silu,
                               bias=c_convb[:, fc:fc + 1]),
                 waits=sw, inc=("S", 1))
            silu_t[par] = P.tok("S")
            spill_t[par] = dma("sync", xbcp_h[fc * 128:(fc + 1) * 128, :],
                               xpo[par], f"XS{par}", waits=[silu_t[par]])

        zsp = [None, None]
        prev_z = None
        for zg in range(8):
            par = zg % 2
            wl = wload(wipT, zg * 512, 512, wz[par], f"WZ{par}", D,
                       waits=[prev_z] if prev_z else [])
            w0 = [wl, x1_tok, dt_evac]
            if prev_z:
                w0.append(prev_z)
            if evac_t[0]:
                w0 += [evac_t[0], evac_t[1]]
            for k in range(KD):
                for c in range(8):
                    P.op("tensor", lambda e, p=pb[c][:, :], l=lhs_tc(k, c),
                         r=wz[par][:, k * 512:(k + 1) * 512],
                         st=(k == 0), sp=(k == KD - 1):
                         e.matmul(p, l, r, start=st, stop=sp),
                         waits=w0 if (k == 0 and c == 0) else [], inc=("T", 1))
            mm = P.tok("T")
            for c in range(8):
                sw = [mm]
                if zsp[par]:
                    sw.append(zsp[par])
                P.op("scalar", act(szt[par][c], pb[c][:, :], AF.Silu),
                     waits=sw, inc=("S", 1))
                dma("sync", sz_h[c * 128:(c + 1) * 128,
                                 zg * 512:(zg + 1) * 512],
                    szt[par][c], f"ZS{par}", waits=[P.tok("S")])
            zsp[par] = P.tok(f"ZS{par}")
            prev_z = P.tok("S")

        P.op("scalar", lambda e: e.activation(tmp64, v_hm, AF.Abs),
             waits=[dt_evac], inc=("S", 1))
        P.op("scalar", lambda e: e.activation(tmp64, tmp64, AF.Exp,
                                              scale=-1.0), inc=("S", 1))
        P.op("scalar", lambda e: e.activation(tmp64, tmp64, AF.Ln, bias=1.0),
             inc=("S", 1))
        P.op("vector", lambda e: e.tensor_scalar(dtp_hm, v_hm, 0.0, None,
                                                 ALU.max),
             waits=[dt_evac], inc=("V", 1))
        P.op("vector", lambda e: e.tensor_tensor(dtp_hm, dtp_hm, tmp64,
                                                 ALU.add),
             waits=[P.tok("S")], inc=("V", 1))
        P.op("scalar", lambda e: e.activation(c_nega, c_alog, AF.Exp),
             inc=("S", 1))
        P.op("vector", lambda e: e.tensor_scalar(a_hm, dtp_hm, c_nega, None,
                                                 ALU.mult),
             waits=[P.tok("S")], inc=("V", 1))
        P.op("vector", lambda e: e.tensor_scalar(a_hm, a_hm, -1.0, None,
                                                 ALU.mult), inc=("V", 1))
        P.op("vector", lambda e: e.memset(offac, 0.0), inc=("V", 1))
        if dtp_h is not None:
            dma("sync", dtp_h[0:64, :], dtp_hm, "SD", waits=[P.tok("V")])

    def stage_scan_a():
        cv.reset("S")
        cv.reset("H")
        cv.reset("B")
        H = cv.alloc("B", 128, DI, F32)
        C_all = cv.alloc("B", 128, NCK * 128, BF16)
        Eg_all = cv.alloc("B", 128, NCK * 64, F32)
        _refs["C_all"] = C_all
        _refs["Eg_all"] = Eg_all
        x_tok = [cv.alloc("S", 128, DI, BF16) for _ in range(2)]
        Bfm = [cv.alloc("S", 128, 128, BF16) for _ in range(2)]
        Btok = [cv.alloc("S", 128, 128, BF16) for _ in range(2)]
        Cfm = [cv.alloc("S", 128, 128, BF16) for _ in range(2)]
        Xdt = [cv.alloc("S", 128, DI, BF16) for _ in range(2)]
        Xdd = [cv.alloc("S", 128, DI, BF16) for _ in range(2)]
        S_all = cv.alloc("S", 128, NH * 128, BF16)
        rhs_hi = [cv.alloc("S", 128, 512, BF16) for _ in range(2)]
        rhs_lo = [cv.alloc("S", 128, 512, BF16) for _ in range(2)]
        egrp = [cv.alloc("S", 128, 512, BF16) for _ in range(2)]
        gpm = cv.alloc("S", 128, 128, BF16)
        L_a = cv.alloc("S", 64, 128, F32)
        L_b = cv.alloc("S", 64, 128, F32)
        Lg = cv.alloc("S", 64, 128, F32)
        dec_hm = cv.alloc("S", 64, 128, F32)
        dec_e = cv.alloc("S", 64, 128, F32)
        Lt = cv.alloc("S", 128, 64, F32)
        Lgt = cv.alloc("S", 128, 64, F32)
        atok = cv.alloc("S", 128, 64, F32)
        dtok = cv.alloc("S", 128, 64, F32)
        dectok = cv.alloc("S", 128, 64, F32)
        ddec = cv.alloc("S", 128, 64, F32)
        ahi = cv.alloc("S", 128, 64, F32)
        alo = cv.alloc("S", 128, 64, F32)
        ahi16 = cv.alloc("S", 128, 64, BF16)
        eLt = cv.alloc("S", 128, 64, F32)
        edt_c = cv.alloc("S", 64, 1, F32)
        edt_r = cv.alloc("S", 1, 64, F32)
        dtot_bc = cv.alloc("S", 128, 64, F32)
        y = cv.alloc("H", 128, DI, F32)
        tmb = cv.alloc("H", 128, DI, F32)
        dtex = tmb
        H16 = cv.alloc("H", 128, DI, BF16)
        dxbc = cv.alloc("H", 128, DI, F32)

        def p7(o, n, p0=0, p1=128):
            return pb[7][p0:p1, o:o + n]

        def bview(ap, nh, rep):
            return bass.AP(ap.tensor, ap.offset,
                           [[ap.tensor.shape[1], 128], [1, nh], [0, rep]])

        def tview(ap, rep, w):
            return bass.AP(ap.tensor, ap.offset,
                           [[ap.tensor.shape[1], 128], [0, rep], [1, w]])

        P.op("vector", lambda e: e.memset(H, 0.0), inc=("V", 1))
        dxt = dma("sync", dxbc, dexp_bc[:, :], "DX")
        hz = P.tok("V")

        buf_free = [None, None]
        tdone = None
        vdone = None
        prev_prep = None
        prev_tr1 = None
        prev_gtok = None
        xdt_free = [None, None]
        lam_rd = [None, None]
        lam_tok = [None, None]
        s_done = [None, None]
        yA_rd = [None, None]
        yB_rd = [None, None]
        egx_rd = [None, None]
        dC_rd = None
        for c in range(nch):
            par = c % 2
            lw = []
            if buf_free[par] is not None:
                lw = buf_free[par]
            for b in range(32):
                P.op("sync",
                     lambda e, d=x_tok[par][:, b * 128:(b + 1) * 128],
                     s=xbcp_h[b * 128:(b + 1) * 128, c * 128:(c + 1) * 128],
                     t=not noxl:
                     e.dma_start(d, s, transpose=t),
                     waits=lw if b < 2 else [], inc=(f"XT{par}", 16))
            P.op("sync", lambda e, d=Btok[par],
                 s=xbcp_h[DI:DI + 128, c * 128:(c + 1) * 128], t=not noxl:
                 e.dma_start(d, s, transpose=t), inc=(f"XT{par}", 16))
            P.op("sync", lambda e, d=Bfm[par],
                 s=xbcp_h[DI:DI + 128, c * 128:(c + 1) * 128]:
                 e.dma_start(d, s), inc=(f"XT{par}", 16))
            P.op("sync", lambda e, d=Cfm[par],
                 s=xbcp_h[DI + 128:DI + 256, c * 128:(c + 1) * 128]:
                 e.dma_start(d, s), inc=(f"XT{par}", 16))
            ld = P.tok(f"XT{par}")

            if sp < 1:
                vdone = P.tok("V")
                buf_free[par] = [vdone, P.tok("T")]
                tdone = P.tok("T")
                continue
            asl = a_hm[:, c * 128:(c + 1) * 128]
            wv = [t for t in (prev_tr1, prev_gtok) if t]
            P.op("vector", lambda e, a=asl: e.tensor_copy(L_a, a),
                 waits=wv, inc=("V", 1))
            src, dst = L_a, L_b
            for s in [1, 2, 4, 8, 16, 32, 64]:
                P.op("vector", lambda e, sh=s, i=src, o=dst: e.tensor_tensor(
                    o[:, sh:128], i[:, sh:128], i[:, 0:128 - sh], ALU.add),
                    inc=("V", 1))
                P.op("vector", lambda e, sh=s, i=src, o=dst: e.tensor_copy(
                    o[:, 0:sh], i[:, 0:sh]), inc=("V", 1))
                src, dst = dst, src
            L_hm = src
            P.op("vector", lambda e, L=L_hm: e.tensor_scalar(
                Lg, L, offac, None, ALU.add), inc=("V", 1))
            P.op("vector", lambda e, L=L_hm: e.tensor_tensor(
                offac, offac, L[:, 127:128], ALU.add), inc=("V", 1))
            P.op("vector", lambda e, L=L_hm: e.tensor_scalar(
                dec_hm, L, L[:, 127:128], None, ALU.subtract), inc=("V", 1))
            vpre = P.tok("V")
            P.op("scalar", lambda e: e.activation(dec_e, dec_hm, AF.Exp,
                                                  scale=-1.0),
                 waits=[vpre] + ([prev_tr1] if prev_tr1 else []),
                 inc=("S", 1))
            P.op("scalar", lambda e, L=L_hm: e.activation(
                edt_c, L[:, 127:128], AF.Exp), inc=("S", 1))
            lch = P.tok("S")

            trw = [lch]
            if tdone:
                trw.append(tdone)
            if prev_prep:
                trw.append(prev_prep)
            dsl = dtp_hm[:, c * 128:(c + 1) * 128]
            P.op("tensor", lambda e, i=asl: e.transpose(
                p7(0, 64), i, c_iden[0:64, 0:64]), waits=trw, inc=("T", 1))
            P.op("tensor", lambda e, i=dsl: e.transpose(
                p7(64, 64), i, c_iden[0:64, 0:64]), inc=("T", 1))
            P.op("tensor", lambda e, L=L_hm: e.transpose(
                p7(128, 64), L, c_iden[0:64, 0:64]), inc=("T", 1))
            P.op("tensor", lambda e: e.transpose(
                p7(192, 64), Lg, c_iden[0:64, 0:64]), inc=("T", 1))
            P.op("tensor", lambda e: e.transpose(
                p7(256, 64), dec_e, c_iden[0:64, 0:64]), inc=("T", 1))
            P.op("tensor", lambda e: e.transpose(
                p7(448, 64, 0, 1), edt_c, c_iden[0:64, 0:64]), inc=("T", 1))
            P.op("tensor", lambda e, l=Bfm[par], r=Cfm[par]: e.matmul(
                p7(320, 128), l, r, start=True, stop=True),
                waits=[ld], inc=("T", 1))
            tr1 = P.tok("T")
            prev_tr1 = tr1
            P.op("vector", lambda e: e.tensor_copy(atok, p7(0, 64)),
                 waits=[tr1], inc=("V", 1))
            P.op("vector", lambda e: e.tensor_copy(dtok, p7(64, 64)),
                 inc=("V", 1))
            P.op("vector", lambda e: e.tensor_copy(Lt, p7(128, 64)),
                 inc=("V", 1))
            P.op("vector", lambda e: e.tensor_copy(Lgt, p7(192, 64)),
                 inc=("V", 1))
            P.op("vector", lambda e: e.tensor_copy(dectok, p7(256, 64)),
                 inc=("V", 1))
            P.op("vector", lambda e: e.tensor_tensor(gpm, p7(320, 128),
                                                     c_cle, ALU.mult),
                 inc=("V", 1))
            P.op("vector", lambda e: e.tensor_copy(edt_r, p7(448, 64, 0, 1)),
                 inc=("V", 1))
            ev1 = P.tok("V")
            evs = P.tok("S")
            P.op("scalar", lambda e: e.activation(eLt, Lt, AF.Exp),
                 waits=[ev1] + ([prev_gtok] if prev_gtok else []),
                 inc=("S", 1))
            P.op("scalar", lambda e, o=Eg_all[:, c * 64:(c + 1) * 64]:
                 e.activation(o, Lgt, AF.Exp), inc=("S", 1))
            eL_tok = P.tok("S")
            P.op("tensor", lambda e: e.matmul(p7(0, 64), c_ones1, edt_r,
                                              start=True, stop=True),
                 waits=[ev1, evs], inc=("T", 1))
            P.op("vector", lambda e: e.tensor_copy(dtot_bc, p7(0, 64)),
                 waits=[P.tok("T")], inc=("V", 1))
            P.op("vector", lambda e: e.tensor_tensor(ddec, dtok, dectok,
                                                     ALU.mult), inc=("V", 1))
            prep = P.tok("V")
            prev_prep = prep

            if sp < 2:
                vdone = P.tok("V")
                buf_free[par] = [vdone, P.tok("T")]
                tdone = P.tok("T")
                continue
            sgrp_tok = []
            for g in range(16):
                lp = g % 2
                bank = pb[lp]
                bw = [prep]
                if s_done[lp]:
                    bw.append(s_done[lp])
                if lam_tok[lp]:
                    bw.append(lam_tok[lp])
                if g < 2 and tdone:
                    bw.append(tdone)
                P.op("vector", lambda e,
                     o=rhs_hi[lp][:, :].rearrange("p (h i) -> p h i", h=4),
                     i0=tview(c_cle16, 4, 128),
                     i1=bview(atok[:, g * 4:g * 4 + 4], 4, 128):
                     e.tensor_tensor(o, i0, i1, ALU.mult),
                     waits=bw, inc=("V", 1))
                rb = P.tok("V")
                mw = [rb]
                if lam_rd[lp]:
                    mw.append(lam_rd[lp])
                P.op("tensor", lambda e, p=bank[:, :], r=rhs_hi[lp]: e.matmul(
                    p, c_up, r, start=True, stop=True), waits=mw,
                    inc=("T", 1))
                lam = P.tok("T")
                lam_tok[lp] = lam
                ew = [lam]
                if s_done[lp]:
                    ew.append(s_done[lp])
                P.op("scalar", lambda e, o=egrp[lp], p=bank[:, :]:
                     e.activation(o, p, AF.Exp), waits=ew, inc=("S", 1))
                lam_rd[lp] = P.tok("S")
                sw = [lam_rd[lp], ev1]
                P.op("vector", lambda e,
                     o=S_all[:, g * 512:(g + 1) * 512].rearrange(
                         "p (h i) -> p h i", h=4),
                     i0=egrp[lp][:, :].rearrange("p (h i) -> p h i", h=4),
                     i1=tview(gpm, 4, 128):
                     e.tensor_tensor(o, i0, i1, ALU.mult),
                     waits=sw, inc=("V", 1))
                s_done[lp] = P.tok("V")
                sgrp_tok.append(s_done[lp])

            if sp < 3:
                vdone = P.tok("V")
                buf_free[par] = [vdone, P.tok("T")]
                tdone = P.tok("T")
                continue
            xw0 = [prep, ld] + ([xdt_free[par]] if xdt_free[par] else [])
            P.op("vector", lambda e,
                 o=Xdt[par][:, :].rearrange("p (h i) -> p h i", h=64),
                 i0=x_tok[par][:, :].rearrange("p (h i) -> p h i", h=64),
                 i1=bview(dtok, 64, 64):
                 e.tensor_tensor(o, i0, i1, ALU.mult),
                 waits=xw0, inc=("V", 1))
            P.op("vector", lambda e,
                 o=Xdd[par][:, :].rearrange("p (h i) -> p h i", h=64),
                 i0=x_tok[par][:, :].rearrange("p (h i) -> p h i", h=64),
                 i1=bview(ddec, 64, 64):
                 e.tensor_tensor(o, i0, i1, ALU.mult), inc=("V", 1))
            xdt_tok = P.tok("V")
            xdd_tok = xdt_tok
            hw = [hz]
            if tdone:
                hw.append(tdone)
            P.op("vector", lambda e: e.tensor_copy(H16, H), waits=hw,
                 inc=("V", 1))
            h16_tok = P.tok("V")

            if sp < 4:
                vdone = P.tok("V")
                buf_free[par] = [vdone, P.tok("T")]
                tdone = P.tok("T")
                continue
            spill_gate = P.tok("YS") if c > 0 else None
            for g in range(8):
                gp = g % 2
                bA = pb[2 + gp]
                bB = pb[4 + gp]
                bC = pb[6]
                ecols = eL_tok
                wA = [h16_tok, ld]
                if yA_rd[gp]:
                    wA.append(yA_rd[gp])
                P.op("tensor", lambda e, p=bA[:, :], l=Cfm[par],
                     r=H16[:, g * 512:(g + 1) * 512]:
                     e.matmul(p, l, r, start=True, stop=True), waits=wA,
                     inc=("T", 1))
                wB = [sgrp_tok[(g * 8) // 4], sgrp_tok[(g * 8 + 7) // 4],
                      xdt_tok]
                if yB_rd[gp]:
                    wB.append(yB_rd[gp])
                for hh in range(8):
                    h = g * 8 + hh
                    P.op("tensor", lambda e, p=bB[:, hh * 64:(hh + 1) * 64],
                         l=S_all[:, h * 128:(h + 1) * 128],
                         r=Xdt[par][:, h * 64:(h + 1) * 64]:
                         e.matmul(p, l, r, start=True, stop=True),
                         waits=wB if hh == 0 else [], inc=("T", 1))
                yt = P.tok("T")
                dw = [xdd_tok, ld]
                if dC_rd:
                    dw.append(dC_rd)
                for hh in range(8):
                    h = g * 8 + hh
                    P.op("tensor", lambda e, p=bC[:, hh * 64:(hh + 1) * 64],
                         l=Btok[par], r=Xdd[par][:, h * 64:(h + 1) * 64]:
                         e.matmul(p, l, r, start=True, stop=True),
                         waits=dw if hh == 0 else [], inc=("T", 1))
                dtk = P.tok("T")
                yw = [yt, ecols]
                if spill_gate:
                    yw.append(spill_gate)
                P.op("vector", lambda e,
                     o=y[:, g * 512:(g + 1) * 512].rearrange(
                         "p (h i) -> p h i", h=8),
                     p=bA[:, :].rearrange("p (h i) -> p h i", h=8),
                     ee=bview(eLt[:, g * 8:(g + 1) * 8], 8, 64):
                     e.tensor_tensor(o, p, ee, ALU.mult),
                     waits=yw, inc=("V", 1))
                yA_rd[gp] = P.tok("V")
                P.op("vector", lambda e, o=y[:, g * 512:(g + 1) * 512],
                     p=bB[:, :]: e.tensor_tensor(o, o, p, ALU.add),
                     inc=("V", 1))
                yB_rd[gp] = P.tok("V")
                P.op("vector", lambda e,
                     o=H[:, g * 512:(g + 1) * 512].rearrange(
                         "p (h i) -> p h i", h=8),
                     dd=bview(dtot_bc[:, g * 8:(g + 1) * 8], 8, 64):
                     e.tensor_tensor(o, o, dd, ALU.mult),
                     waits=[dtk, h16_tok], inc=("V", 1))
                P.op("vector", lambda e, o=H[:, g * 512:(g + 1) * 512],
                     p=bC[:, :]: e.tensor_tensor(o, o, p, ALU.add),
                     inc=("V", 1))
                dC_rd = P.tok("V")
            P.op("vector", lambda e, o=C_all[:, c * 128:(c + 1) * 128],
                 i=Cfm[par]: e.tensor_copy(o, i), waits=[ld], inc=("V", 1))
            P.op("vector", lambda e, xp=x_tok[par]: e.tensor_tensor(
                tmb, xp, dxbc, ALU.mult), waits=[dxt, xdt_tok], inc=("V", 1))
            P.op("vector", lambda e: e.tensor_tensor(y, y, tmb, ALU.add),
                 inc=("V", 1))
            vdone = P.tok("V")
            buf_free[par] = [vdone, P.tok("T"), P.tok("G")]
            tdone = P.tok("T")
            xdt_free[par] = tdone
            prev_gtok = P.tok("G")
            dma("sync", yp_h[c * 128:(c + 1) * 128, :], y, "YS",
                waits=[vdone])

        if noseg:
            P.op("gpsimd", lambda e: e.dma_start(
                agout[0:128, :], agin[:, :]), waits=[P.tok("GD")] if P.cnt.get("GD") else [],
                inc=("CC", 16))
            return
        P.op("scalar", lambda e: e.activation(edt_c, offac, AF.Exp),
             waits=[P.tok("V")], inc=("S", 1))
        P.op("tensor", lambda e: e.transpose(
            p7(448, 64, 0, 1), edt_c, c_iden[0:64, 0:64]),
            waits=[P.tok("S"), P.tok("V")], inc=("T", 1))
        P.op("vector", lambda e: e.tensor_copy(edt_r, p7(448, 64, 0, 1)),
             waits=[P.tok("T")], inc=("V", 1))
        P.op("tensor", lambda e: e.matmul(p7(0, 64), c_ones1, edt_r,
                                          start=True, stop=True),
             waits=[P.tok("V")], inc=("T", 1))
        P.op("vector", lambda e: e.tensor_copy(dtot_bc, p7(0, 64)),
             waits=[P.tok("T")], inc=("V", 1))
        for h in range(NH):
            P.op("vector", lambda e, o=dtex[:, h * 64:(h + 1) * 64],
                 s=dtot_bc[:, h:h + 1]: e.tensor_scalar(
                     o, c_ones64, s, None, ALU.mult), inc=("V", 1))
        vv = P.tok("V")
        dma("gpsimd", agin[:, 0:DI], H, "GD", waits=[vv])
        dma("gpsimd", agin[:, DI:2 * DI], dtex, "GD")
        if nocc:
            P.op("gpsimd", lambda e: e.dma_start(
                agout[0:128, :], agin[:, :]), waits=[P.tok("GD")],
                inc=("CC", 16))
        else:
            P.op("gpsimd", lambda e: e.collective_compute(
                "AllGather", ALU.bypass, replica_groups=[list(range(8))],
                ins=[agin[:, :]], outs=[agout[:, :]]),
                waits=[P.tok("GD")], inc=("CC", 1))

    def stage_g():
        cv.reset("S")
        cv.reset("H")
        wq = cv.alloc("H", 128, 4 * KD * 512, BF16)
        wfb = [cv.alloc("S", 128, KD * 512, F32) for _ in range(2)]
        accc = cv.alloc("S", 128, 1, F32)
        redc = cv.alloc("S", 128, 1, F32)
        tot = cv.alloc("S", 1, 1, F32)
        srow = cv.alloc("S", 1, 2, F32)
        sbc = cv.alloc("S", 128, 2, F32)
        xnfm = [cv.alloc("S", 128, KD * 128, BF16) for _ in range(2)]
        bb = cv.alloc("B", 128, D, F32)
        g16 = [cv.alloc("S", 128, 512, BF16) for _ in range(4)]
        tball = [cv.alloc("S", 128, 512, F32) for _ in range(2)]
        dma("sync", bb, bias_bc[:, :], "LB")
        bb_tok = P.tok("LB")

        def wf_load(gg, waits):
            src = bass.AP(wfT, gg * 512, [[D, 128], [128 * D, KD], [1, 512]])
            d3 = wfb[gg % 2].rearrange("p (c j) -> p c j", j=512)
            return P.op("gpsimd", lambda e, d=d3, s=src: e.dma_start(d, s),
                        waits=waits, inc=(f"WF{gg % 2}", 16))

        rd_done = [None, None]
        pw = None
        for gg in range(4):
            lt = wf_load(gg, [rd_done[gg % 2]] if rd_done[gg % 2] else [])
            P.op("vector", lambda e, w=wfb[gg % 2]: e.tensor_reduce(
                redc, w, AX.X, ALU.add, apply_absolute_value=True),
                waits=[lt] + ([pw] if pw else []), inc=("V", 1))
            rd_done[gg % 2] = P.tok("V")
            if gg == 0:
                P.op("scalar", lambda e: e.activation(accc, redc, AF.Copy),
                     waits=[P.tok("V")], inc=("S", 1))
            else:
                P.op("scalar", lambda e: e.add(accc, redc, accc),
                     waits=[P.tok("V")], inc=("S", 1))
            pw = P.tok("S")
        P.op("tensor", lambda e: e.matmul(pb[7][0:1, 0:1], c_ones64[:, 0:1],
                                          accc, start=True, stop=True),
             waits=[pw], inc=("T", 1))
        P.op("vector", lambda e: e.tensor_copy(tot, pb[7][0:1, 0:1]),
             waits=[P.tok("T")], inc=("V", 1))
        P.op("vector", lambda e: e.tensor_scalar(
            srow[0:1, 0:1], tot, 1.0 / (D * D), None, ALU.mult), inc=("V", 1))
        P.op("vector", lambda e: e.tensor_scalar(
            srow[0:1, 1:2], srow[0:1, 0:1], FEPS, None, ALU.add),
            inc=("V", 1))
        P.op("vector", lambda e: e.reciprocal(srow[0:1, 1:2], srow[0:1, 1:2]),
             inc=("V", 1))
        P.op("tensor", lambda e: e.matmul(pb[7][:, 4:6], c_ones1, srow,
                                          start=True, stop=True),
             waits=[P.tok("V")], inc=("T", 1))
        P.op("vector", lambda e: e.tensor_copy(sbc, pb[7][:, 4:6]),
             waits=[P.tok("T")], inc=("V", 1))
        sc_tok = P.tok("V")

        gts = cv.alloc("S", 128, 512, F32)
        lts = cv.alloc("S", 128, 512, F32)
        wq_gtok = [None] * 4
        pw = [None, None]
        for gg in range(4):
            lt = wf_load(gg, [sc_tok] + ([pw[gg % 2]] if pw[gg % 2] else []))
            for sl in range(KD):
                wsl = wfb[gg % 2][:, sl * 512:(sl + 1) * 512]
                P.op("vector", lambda e, w=wsl: e.tensor_scalar(
                    gts, w, sbc[:, 1:2], 0.5, ALU.mult, ALU.is_gt),
                    waits=[lt] if sl == 0 else [], inc=("V", 1))
                P.op("vector", lambda e, w=wsl: e.tensor_scalar(
                    lts, w, sbc[:, 1:2], -0.5, ALU.mult, ALU.is_lt),
                    inc=("V", 1))
                P.op("vector", lambda e: e.tensor_tensor(
                    gts, gts, lts, ALU.subtract), inc=("V", 1))
                P.op("vector", lambda e,
                     o=wq[:, gg * KD * 512 + sl * 512:
                          gg * KD * 512 + (sl + 1) * 512]:
                     e.tensor_scalar(o, gts, sbc[:, 0:1], None, ALU.mult),
                     inc=("V", 1))
            pw[gg % 2] = P.tok("V")
            wq_gtok[gg] = P.tok("V")

        def wq_rhs(k, gg):
            b = gg * KD * 512 + k * 512
            return wq[:, b:b + 512]

        xn_free = [None, None]
        bank_free = [None, None, None, None]
        tb_free = [None, None]
        g_free = [[None, None] for _ in range(4)]
        for c in range(8):
            par = c % 2
            w0 = [] if xn_free[par] is None else [xn_free[par]]
            for k in range(KD):
                P.op("sync", lambda e, d=xnfm[par][:, k * 128:(k + 1) * 128],
                     s=xn16_h[c * 128:(c + 1) * 128, k * 128:(k + 1) * 128]:
                     e.dma_start(d, s, transpose=True),
                     waits=w0 if k == 0 else [], inc=(f"XN{par}", 16))
            xl = P.tok(f"XN{par}")
            for gg in range(4):
                mw = [xl, wq_gtok[gg]]
                if bank_free[gg]:
                    mw.append(bank_free[gg])
                for k in range(KD):
                    P.op("tensor", lambda e, p=pb[gg][:, :],
                         l=xnfm[par][:, k * 128:(k + 1) * 128],
                         r=wq_rhs(k, gg), st=(k == 0), sp=(k == KD - 1):
                         e.matmul(p, l, r, start=st, stop=sp),
                         waits=mw if k == 0 else [], inc=("T", 1))
            mm = P.tok("T")
            xn_free[par] = mm
            for gg in range(4):
                tp = gg % 2
                vw = [mm, bb_tok]
                if tb_free[tp]:
                    vw.append(tb_free[tp])
                P.op("vector", lambda e, o=tball[tp], p=pb[gg][:, :],
                     b=bb[:, gg * 512:(gg + 1) * 512]:
                     e.tensor_tensor(o, p, b, ALU.add), waits=vw,
                     inc=("V", 1))
                bank_free[gg] = P.tok("V")
                sw = [P.tok("V")]
                if g_free[gg][0]:
                    sw.append(g_free[gg][0])
                P.op("scalar", lambda e, o=g16[gg], i=tball[tp]:
                     e.activation(o, i, AF.Silu), waits=sw, inc=("S", 1))
                tb_free[tp] = P.tok("S")
                g_free[gg][0] = dma(
                    "sync", g_h[c * 128:(c + 1) * 128,
                                gg * 512:(gg + 1) * 512],
                    g16[gg], f"GS{gg}", waits=[P.tok("S")])

    def stage_scan_b():
        C_all = _refs["C_all"]
        Eg_all = _refs["Eg_all"]
        cv.reset("S")
        cv.reset("H")
        Hi = cv.alloc("H", 128, DI, F32)
        Hi16 = cv.alloc("H", 128, DI, BF16)
        y = cv.alloc("H", 128, DI, F32)
        dump = cv.alloc("H", 128, DI, F32)
        hjb = [cv.alloc("S", 128, 2 * DI, F32) for _ in range(2)]
        dtl = cv.alloc("S", 128, DI, F32)

        P.op("vector", lambda e: e.memset(Hi, 0.0), inc=("V", 1))
        hj_free = [None, None]
        for j in range(7):
            par = j % 2
            lw = [P.tok("CC")]
            if hj_free[par]:
                lw.append(hj_free[par])
            lt = dma("sync", hjb[par], agout[j * 128:(j + 1) * 128, :],
                     f"HJ{par}", waits=lw)
            P.op("vector", lambda e, i=hjb[par][:, DI:2 * DI],
                 m=c_mcol[:, j:j + 1], mi=c_minv[:, j:j + 1]:
                 e.tensor_scalar(dtl, i, m, mi, ALU.mult, ALU.add),
                 waits=[lt], inc=("V", 1))
            if j > 0:
                P.op("vector", lambda e: e.tensor_tensor(Hi, Hi, dtl,
                                                         ALU.mult),
                     inc=("V", 1))
            P.op("vector", lambda e, i=hjb[par][:, 0:DI],
                 m=c_mcol[:, j:j + 1]: e.scalar_tensor_tensor(
                     Hi, i, m, Hi, ALU.mult, ALU.add), inc=("V", 1))
            hj_free[par] = P.tok("V")
        P.op("vector", lambda e: e.tensor_copy(Hi16, Hi), inc=("V", 1))
        hi_tok = P.tok("V")

        cv.reset("S")
        sz = [cv.alloc("S", 128, DI, BF16) for _ in range(2)]
        ypl = [cv.alloc("S", 128, DI, F32) for _ in range(2)]
        yn16 = cv.alloc("S", 128, DI, BF16)
        nwb = cv.alloc("S", 128, DI, F32)
        ssc = cv.alloc("S", 128, 1, F32)
        msb = cv.alloc("S", 128, 1, F32)
        sse = cv.alloc("S", 128, 1, F32)
        r2 = cv.alloc("S", 128, 1, F32)
        nw_tok = dma("sync", nwb, nw_bc[:, :], "LB", waits=[hi_tok])

        buf_free = [None, None]
        yn_sp = None
        bank_rd = [None, None]
        for c in range(NCK):
            par = c % 2
            lw = [hi_tok]
            if buf_free[par]:
                lw.append(buf_free[par])
            dma("sync", ypl[par], yp_h[c * 128:(c + 1) * 128, :],
                f"YP{par}", waits=lw)
            dma("sync", sz[par], sz_h[c * 128:(c + 1) * 128, :], f"YP{par}")
            ld = P.tok(f"YP{par}")
            for g in range(8):
                gp = g % 2
                bA = pb[2 + gp]
                ecols = P.tok("V")
                wA = [hi_tok]
                if bank_rd[gp]:
                    wA.append(bank_rd[gp])
                P.op("tensor", lambda e, p=bA[:, :],
                     l=C_all[:, c * 128:(c + 1) * 128],
                     r=Hi16[:, g * 512:(g + 1) * 512]:
                     e.matmul(p, l, r, start=True, stop=True), waits=wA,
                     inc=("T", 1))
                mt = P.tok("T")
                yw = [mt, ecols, ld]
                if yn_sp and g == 0:
                    yw.append(yn_sp)
                P.op("vector", lambda e,
                     o=y[:, g * 512:(g + 1) * 512].rearrange(
                         "p (h i) -> p h i", h=8),
                     p=bA[:, :].rearrange("p (h i) -> p h i", h=8),
                     ee=bass.AP(Eg_all.tensor, Eg_all.offset + c * 64 + g * 8,
                                [[Eg_all.tensor.shape[1], 128], [1, 8],
                                 [0, 64]]):
                     e.tensor_tensor(o, p, ee, ALU.mult),
                     waits=yw, inc=("V", 1))
                bank_rd[gp] = P.tok("V")
                P.op("vector", lambda e, o=y[:, g * 512:(g + 1) * 512],
                     i=ypl[par][:, g * 512:(g + 1) * 512]:
                     e.tensor_tensor(o, o, i, ALU.add), inc=("V", 1))
            P.op("vector", lambda e, s=sz[par]: e.tensor_tensor(
                y, y, s, ALU.mult), inc=("V", 1))
            buf_free[par] = P.tok("V")
            rms_scale(y, DI, dump, ssc, msb, sse, r2, [P.tok("V")])
            P.op("vector", lambda e: e.scalar_tensor_tensor(
                yn16, y, r2, nwb, ALU.mult, ALU.mult),
                waits=[P.tok("S"), nw_tok] + ([yn_sp] if yn_sp else []),
                inc=("V", 1))
            yn_sp = dma("sync", yn16_h[c * 128:(c + 1) * 128, :], yn16, "YN",
                        waits=[P.tok("V")])

    def stage_outproj():
        cv.reset("S")
        cv.reset("H")
        cv.reset("B")
        hs = cv.alloc("H", 128, 8 * D, F32)
        x2fm = cv.alloc("B", 128, KD * T, BF16)
        _refs["hs"] = hs
        _refs["x2fm"] = x2fm
        wop = cv.alloc("S", 128, 32 * 512, BF16)
        ynb = cv.alloc("S", 128, 32 * 512, BF16)
        xnl = cv.alloc("S", 128, 512, F32)
        dump = cv.alloc("S", 128, D, F32)
        ssc = cv.alloc("S", 128, 1, F32)
        msb = cv.alloc("S", 128, 1, F32)
        sse = cv.alloc("S", 128, 1, F32)
        r2c = cv.alloc("S", 128, 1, F32)
        lnbc2 = cv.alloc("S", 128, D, F32)
        x2t = cv.alloc("S", 128, D, BF16)
        dma("sync", lnbc2, ln2_bc[:, :], "LB")
        ln2_tok = P.tok("LB")

        mm_done = None
        yn_used = None
        for tb in range(2):
            w0 = [] if yn_used is None else [yn_used]
            for b in range(32):
                P.op("sync",
                     lambda e, d=ynb[:, b * 512:(b + 1) * 512],
                     s=yn16_h[tb * 512:(tb + 1) * 512,
                              b * 128:(b + 1) * 128]:
                     e.dma_start(d, s, transpose=True),
                     waits=w0 if b < 2 else [], inc=("YB", 16))
            yl = P.tok("YB")
            for ft in range(4):
                wl = wload(wopT, ft * 512, 512, wop, "W0", DI,
                           waits=[mm_done] if mm_done else [])
                for cc in range(4):
                    c = tb * 4 + cc
                    bank = pb[cc % 2]
                    lx = dma("sync", xnl,
                             xn_h[c * 128:(c + 1) * 128,
                                  ft * 512:(ft + 1) * 512],
                             "XL", waits=[P.tok("V")])
                    for k in range(32):
                        P.op("tensor", lambda e, p=bank[:, :],
                             l=ynb[:, k * 512 + cc * 128:
                                    k * 512 + (cc + 1) * 128],
                             r=wop[:, k * 512:(k + 1) * 512],
                             st=(k == 0), sp=(k == 31):
                             e.matmul(p, l, r, start=st, stop=sp),
                             waits=[yl, wl, P.tok("V")] if k == 0 else [],
                             inc=("T", 1))
                    mm = P.tok("T")
                    P.op("vector", lambda e,
                         o=hs[:, c * D + ft * 512:c * D + (ft + 1) * 512],
                         p=bank[:, :], x=xnl:
                         e.tensor_tensor(o, p, x, ALU.add),
                         waits=[mm, lx], inc=("V", 1))
                mm_done = P.tok("V")
            yn_used = P.tok("T")
        hs_tok = P.tok("V")
        if hs_h is not None:
            for c in range(8):
                dma("sync", hs_h[c * 128:(c + 1) * 128, :],
                    hs[:, c * D:(c + 1) * D], "SD", waits=[hs_tok])

        x2_sp = None
        for c in range(8):
            hsl = hs[:, c * D:(c + 1) * D]
            rms_scale(hsl, D, dump, ssc, msb, sse, r2c,
                      [hs_tok, P.tok("V")])
            vw = [P.tok("S"), ln2_tok]
            if x2_sp:
                vw.append(x2_sp)
            P.op("vector", lambda e, o=x2t, i=hsl:
                 e.scalar_tensor_tensor(o, i, r2c, lnbc2, ALU.mult, ALU.mult),
                 waits=vw, inc=("V", 1))
            x2_sp = dma("sync", x2_h[c * 128:(c + 1) * 128, :], x2t,
                        "X2S", waits=[P.tok("V")])

    def stage_ffn():
        hs = _refs["hs"]
        x2fm = _refs["x2fm"]
        cv.reset("S")
        hgu = cv.alloc("S", 128, KD * T, BF16)
        wb2 = [cv.alloc("S", 128, KD * 512, BF16) for _ in range(2)]
        wdb2 = [cv.alloc("S", 128, 8 * 512, BF16) for _ in range(2)]
        gt = [cv.alloc("S", 128, 512, F32) for _ in range(2)]

        for k in range(KD):
            for tb in range(2):
                P.op("sync", lambda e,
                     d=x2fm[:, k * T + tb * 512:k * T + (tb + 1) * 512],
                     s=x2_h[tb * 512:(tb + 1) * 512, k * 128:(k + 1) * 128]:
                     e.dma_start(d, s, transpose=True), inc=("X2", 16))
        x2tok = P.tok("X2")

        bufA_free = None
        bufB_free = None
        pg_rd = [None, None]
        pu_rd = [None, None]
        gt_rd = [None, None]
        dbank_rd = [None, None]
        wd_free = [None, None]
        for sl in range(4):
            hgu_tok = None
            for gg in range(4):
                f0 = sl * 2048 + gg * 512
                la = wload(wgT, f0, 512, wb2[0], "WA", D,
                           waits=[bufA_free] if bufA_free else [])
                lb = wload(wuT, f0, 512, wb2[1], "WB", D,
                           waits=[bufB_free] if bufB_free else [])
                for m in range(8):
                    ic, tb = m // 2, m % 2
                    i = gg * 4 + ic
                    mp = m % 2
                    bg = pb[2 * mp]
                    bu = pb[2 * mp + 1]
                    mw = [la, x2tok]
                    if pg_rd[mp]:
                        mw.append(pg_rd[mp])
                    for k in range(KD):
                        P.op("tensor", lambda e, p=bg[:, :],
                             l=wb2[0][:, k * 512 + ic * 128:
                                      k * 512 + (ic + 1) * 128],
                             r=x2fm[:, k * T + tb * 512:
                                    k * T + (tb + 1) * 512],
                             st=(k == 0), sp=(k == KD - 1):
                             e.matmul(p, l, r, start=st, stop=sp),
                             waits=mw if k == 0 else [], inc=("T", 1))
                    uw = [lb]
                    if pu_rd[mp]:
                        uw.append(pu_rd[mp])
                    for k in range(KD):
                        P.op("tensor", lambda e, p=bu[:, :],
                             l=wb2[1][:, k * 512 + ic * 128:
                                      k * 512 + (ic + 1) * 128],
                             r=x2fm[:, k * T + tb * 512:
                                    k * T + (tb + 1) * 512],
                             st=(k == 0), sp=(k == KD - 1):
                             e.matmul(p, l, r, start=st, stop=sp),
                             waits=uw if k == 0 else [], inc=("T", 1))
                    mm = P.tok("T")
                    sw = [mm]
                    if gt_rd[mp]:
                        sw.append(gt_rd[mp])
                    P.op("scalar", act(gt[mp], bg[:, :], AF.Silu),
                         waits=sw, inc=("S", 1))
                    pg_rd[mp] = P.tok("S")
                    P.op("vector", lambda e,
                         o=hgu[:, i * T + tb * 512:i * T + (tb + 1) * 512],
                         g=gt[mp], p=bu[:, :]:
                         e.tensor_tensor(o, g, p, ALU.mult),
                         waits=[P.tok("S")], inc=("V", 1))
                    pu_rd[mp] = P.tok("V")
                    gt_rd[mp] = pu_rd[mp]
                bufA_free = P.tok("T")
                bufB_free = P.tok("T")
            hgu_tok = P.tok("V")
            for ft in range(4):
                par = ft % 2
                src = bass.AP(wdT, (sl * 2048) * D + ft * 512,
                              [[D, 128], [128 * D, KD], [1, 512]])
                wdst = wb2[par]
                lw = [bufA_free if par == 0 else bufB_free]
                d3 = wdst.rearrange("p (c j) -> p c j", j=512)
                lt = P.op("gpsimd", lambda e, d=d3, s=src: e.dma_start(d, s),
                          waits=lw + ([P.tok("T")] if sl or ft else []),
                          inc=(f"WD{par}", 16))
                for c in range(8):
                    bank = pb[4 + par]
                    bw = [lt, hgu_tok]
                    if dbank_rd[par]:
                        bw.append(dbank_rd[par])
                    for k in range(KD):
                        P.op("tensor", lambda e, p=bank[:, :],
                             l=hgu[:, k * T + c * 128:k * T + (c + 1) * 128],
                             r=wdst[:, k * 512:(k + 1) * 512],
                             st=(k == 0), sp=(k == KD - 1):
                             e.matmul(p, l, r, start=st, stop=sp),
                             waits=bw if k == 0 else [], inc=("T", 1))
                    mm = P.tok("T")
                    P.op("vector", lambda e,
                         o=hs[:, c * D + ft * 512:c * D + (ft + 1) * 512],
                         p=bank[:, :]: e.tensor_tensor(o, o, p, ALU.add),
                         waits=[mm], inc=("V", 1))
                    dbank_rd[par] = P.tok("V")
                if par == 0:
                    bufA_free = P.tok("T")
                else:
                    bufB_free = P.tok("T")

    def stage_final():
        hs = _refs["hs"]
        cv.reset("S")
        xl = [cv.alloc("S", 128, D, F32) for _ in range(2)]
        gl = [cv.alloc("S", 128, D, BF16) for _ in range(2)]
        ot = [cv.alloc("S", 128, D, F32) for _ in range(2)]
        of = [None, None]
        for c in range(8):
            par = c % 2
            lw = [] if of[par] is None else [of[par]]
            dma("sync", xl[par], xin[3 + c * 128:3 + (c + 1) * 128, :],
                f"F{par}", waits=lw)
            dma("sync", gl[par], g_h[c * 128:(c + 1) * 128, :], f"F{par}")
            ld = P.tok(f"F{par}")
            P.op("vector", lambda e, o=ot[par], s=hs[:, c * D:(c + 1) * D],
                 g=gl[par]: e.tensor_tensor(o, s, g, ALU.mult),
                 waits=[ld] + ([of[par]] if of[par] else []), inc=("V", 1))
            P.op("vector", lambda e, o=ot[par], x=xl[par]:
                 e.tensor_tensor(o, o, x, ALU.add), inc=("V", 1))
            of[par] = dma("sync", out_h[c * 128:(c + 1) * 128, :], ot[par],
                          f"OD{par}", waits=[P.tok("V")])
        P.op("sync", None, waits=[P.tok("OD0"), P.tok("OD1")])

    stages = [stage_ln, stage_inproj, stage_scan_a, stage_g, stage_scan_b,
              stage_outproj, stage_ffn, stage_final]
    P.barrier()
    for i, st in enumerate(stages):
        if i >= upto:
            break
        st()
        P.barrier()

    P.emit(nc)
    return nc


def _prep(inputs):
    x = np.asarray(inputs["L0_attn_hidden_states"], np.float32)
    w = {k: np.asarray(v, np.float32) for k, v in inputs.items()}

    sh = {}
    sh["wipT"] = np.ascontiguousarray(w["L0_attn_in_proj_w"].T)
    sh["wopT"] = np.ascontiguousarray(w["L0_attn_out_proj_w"].T)
    sh["wgT"] = np.ascontiguousarray(w["L0_attn_gate_w"].T)
    sh["wuT"] = np.ascontiguousarray(w["L0_attn_up_w"].T)
    sh["wdT"] = np.ascontiguousarray(w["L0_attn_down_w"].T)
    sh["wfT"] = np.ascontiguousarray(w["L0_ffn_weight"].T)
    sh["convw"] = np.ascontiguousarray(
        w["L0_attn_conv_w"].reshape(34, 128, 4).transpose(1, 0, 2)
        .reshape(128, 34 * 4))
    sh["convb"] = np.ascontiguousarray(w["L0_attn_conv_b"].reshape(34, 128).T)
    sh["dtb"] = np.ascontiguousarray(w["L0_attn_dt_bias"].reshape(64, 1))
    sh["alog"] = np.ascontiguousarray(w["L0_attn_A_log"].reshape(64, 1))
    sh["dexp_bc"] = np.ascontiguousarray(np.broadcast_to(
        np.repeat(w["L0_attn_D_param"], HD)[None, :], (128, DI)))
    sh["nw_bc"] = np.ascontiguousarray(np.broadcast_to(
        w["L0_attn_norm_w"][None, :], (128, DI)))
    sh["ln1_bc"] = np.ascontiguousarray(np.broadcast_to(
        w["L0_attn_ln1_w"][None, :], (128, D)))
    sh["ln2_bc"] = np.ascontiguousarray(np.broadcast_to(
        w["L0_attn_ln2_w"][None, :], (128, D)))
    sh["bias_bc"] = np.ascontiguousarray(np.broadcast_to(
        w["L0_ffn_bias"][None, :], (128, D)))
    r = np.arange(128)
    sh["cle"] = (r[:, None] <= r[None, :]).astype(np.float32)
    sh["cle16"] = (r[:, None] <= r[None, :]).astype(BF16NP)
    sh["up16"] = (r[:, None] > r[None, :]).astype(BF16NP)
    sh["iden"] = np.eye(128, dtype=np.float32)
    sh["ones64"] = np.ones((128, 64), np.float32)
    sh["ones1"] = np.ones((1, 128), np.float32)

    in_maps = []
    for q in range(8):
        b, rq = q // 4, q % 4
        sl = np.zeros((TH, D), np.float32)
        t0 = rq * 1024
        if rq == 0:
            sl[3:] = x[b, 0:1024]
        else:
            sl[:] = x[b, t0 - 3:t0 + 1024]
        m = np.zeros(7, np.float32)
        for j in range(7):
            if j // 4 == b and (j % 4) < rq:
                m[j] = 1.0
        d = dict(sh)
        d["xin"] = np.ascontiguousarray(sl)
        d["mcol"] = np.ascontiguousarray(np.broadcast_to(m[None, :],
                                                         (128, 7)))
        d["minv"] = np.ascontiguousarray(np.broadcast_to((1.0 - m)[None, :],
                                                         (128, 7)))
        in_maps.append(d)
    return in_maps


def kernel(**inputs):
    in_maps = _prep(inputs)
    nc = build()
    res = run_bass_kernel_spmd(nc, in_maps, core_ids=list(range(8)))
    out = np.empty((2, 4096, D), np.float32)
    for q in range(8):
        b, rq = q // 4, q % 4
        out[b, rq * 1024:(rq + 1) * 1024] = np.asarray(
            res.results[q]["out_h"], np.float32)
    return out


# version 10 (final)
# speedup vs baseline: 1.0390x; 1.0250x over previous
import numpy as np
import ml_dtypes
import concourse.bass as bass
import concourse.mybir as mybir
from concourse.bass_utils import run_bass_kernel_spmd

F32 = mybir.dt.float32
BF16 = mybir.dt.bfloat16
AF = mybir.ActivationFunctionType
ALU = mybir.AluOpType
AX = mybir.AxisListType
BF16NP = ml_dtypes.bfloat16

D = 2048
DI = 4096
DS = 128
HD = 64
NH = 64
INTER = 8192
CONV = DI + 2 * DS
DIP = 2 * DI + 2 * DS + NH
EPS = 1e-5
FEPS = 1e-10
T = 1024
TH = T + 3
NCK = 8
KD = D // 128

ENGINES = ("sync", "gpsimd", "vector", "scalar", "tensor")


class Plan:
    def __init__(self):
        self.ops = []
        self.cnt = {}

    SELF_SEM = {"vector": "V", "scalar": "S", "gpsimd": "G"}

    def op(self, eng, fn, waits=(), inc=None):
        ws = [(s, v) for (s, v) in waits if v > 0]
        sem = self.SELF_SEM.get(eng)
        if sem is not None and fn is not None:
            if eng != "gpsimd" or (inc is not None and inc[0] == "G"):
                v = self.cnt.get(sem, 0)
                if v > 0:
                    ws.append((sem, v))
        self.ops.append((eng, fn, ws, inc))
        if inc is not None:
            s, amt = inc
            self.cnt[s] = self.cnt.get(s, 0) + amt
            return (s, self.cnt[s])
        return None

    def tok(self, sem):
        return (sem, self.cnt.get(sem, 0))

    def barrier(self):
        toks = [(s, v) for s, v in self.cnt.items() if s != "CC"]
        for eng in ENGINES:
            self.op(eng, None, waits=toks)

    def emit(self, nc):
        from contextlib import ExitStack
        sems = {}
        with ExitStack() as ctx:
            names = set(self.cnt.keys())
            for (_, _, ws, _) in self.ops:
                names.update(s for s, _ in ws)
            for s in sorted(names):
                sems[s] = ctx.enter_context(nc.semaphore(s))
            block = ctx.enter_context(nc.Block())

            def mk(eng_name):
                def emit_eng(eng):
                    for (e, fn, ws, inc) in self.ops:
                        if e != eng_name:
                            continue
                        for s, v in ws:
                            eng.wait_ge(sems[s], v)
                        if fn is not None:
                            inst = fn(eng)
                            if inc is not None:
                                inst.then_inc(sems[inc[0]], inc[1])
                return emit_eng

            block.sync(mk("sync"))
            block.gpsimd(mk("gpsimd"))
            block.vector(mk("vector"))
            block.scalar(mk("scalar"))
            block.tensor(mk("tensor"))


class Carver:

    def __init__(self, nc, total_bytes):
        self.h32 = nc.alloc_sbuf_tensor("arena", [128, total_bytes // 4], F32)
        self.h16 = self.h32.bitcast(BF16)
        self.segs = {}

    def seg(self, name, start, end):
        self.segs[name] = [start, start, end]

    def reset(self, name):
        self.segs[name][1] = self.segs[name][0]

    def alloc(self, seg, p, n, dt, align=64):
        s = self.segs[seg]
        off = (s[1] + align - 1) // align * align
        size = n * (4 if dt == F32 else 2)
        assert off + size <= s[2], f"seg {seg} overflow {off}+{size}>{s[2]}"
        s[1] = off + size
        h = self.h32 if dt == F32 else self.h16
        es = 4 if dt == F32 else 2
        return h[0:p, off // es: off // es + n]


def build(upto=8, dbg=(), nocc=False, nch=NCK, noxl=False, sp=4, noseg=False):
    nc = bass.Bass("TRN2", target_bir_lowering=False)

    def din(name, shape, dt=F32):
        return nc.dram_tensor(name, shape, dt, kind="ExternalInput")

    def dscr(name, shape, dt):
        kind = "ExternalOutput" if name in dbg else "Internal"
        return nc.dram_tensor(name, shape, dt, kind=kind)

    xin = din("xin", [TH, D])
    wipT = din("wipT", [D, DIP])
    wopT = din("wopT", [DI, D])
    wgT = din("wgT", [D, INTER])
    wuT = din("wuT", [D, INTER])
    wdT = din("wdT", [INTER, D])
    wfT = din("wfT", [D, D])
    convw = din("convw", [128, 34 * 4])
    convb = din("convb", [128, 34])
    dtb = din("dtb", [64, 1])
    alog = din("alog", [64, 1])
    dexp_bc = din("dexp_bc", [128, DI])
    nw_bc = din("nw_bc", [128, DI])
    ln1_bc = din("ln1_bc", [128, D])
    ln2_bc = din("ln2_bc", [128, D])
    bias_bc = din("bias_bc", [128, D])
    cle = din("cle", [128, 128])
    cle16 = din("cle16", [128, 128], BF16)
    up16 = din("up16", [128, 128], BF16)
    iden = din("iden", [128, 128])
    ones64 = din("ones64", [128, 64])
    ones1 = din("ones1", [1, 128])
    mcol = din("mcol", [128, 7])
    minv = din("minv", [128, 7])

    xn_h = dscr("xn_h", [T, D], F32)
    xh1_h = dscr("xh1_h", [TH, D], BF16)
    xn16_h = dscr("xn16_h", [T, D], BF16)
    xbcp_h = dscr("xbcp_h", [CONV, T], BF16)
    sz_h = dscr("sz_h", [T, DI], BF16)
    yp_h = dscr("yp_h", [T, DI], F32)
    yn16_h = dscr("yn16_h", [T, DI], BF16)
    x2_h = dscr("x2_h", [T, D], BF16)
    g_h = dscr("g_h", [T, D], BF16)
    hs_h = dscr("hs_h", [T, D], F32) if "hs_h" in dbg else None
    dtp_h = dscr("dtp_h", [64, T], F32) if "dtp_h" in dbg else None
    agin = nc.dram_tensor("agin", [128, 2 * DI], F32)
    agout = nc.dram_tensor("agout", [8 * 128, 2 * DI], F32)
    out_h = nc.dram_tensor("out_h", [T, D], F32, kind="ExternalOutput")

    pb = [nc.alloc_psum_tensor(f"pb{i}", [128, 512], F32) for i in range(8)]

    KB = 1024
    cv = Carver(nc, 212000)
    cv.seg("C", 0, 4 * KB)
    cv.seg("A", 4 * KB, 17 * KB)
    cv.seg("B", 17 * KB, 51 * KB)
    cv.seg("H", 51 * KB, 115 * KB)
    cv.seg("S", 115 * KB, 212000)

    P = Plan()
    _refs = {}

    c_cle = cv.alloc("C", 128, 128, F32)
    c_cle16 = cv.alloc("C", 128, 128, BF16)
    c_up = cv.alloc("C", 128, 128, BF16)
    c_iden = cv.alloc("C", 128, 128, F32)
    c_ones64 = cv.alloc("C", 128, 64, F32)
    c_ones1 = cv.alloc("C", 1, 128, F32)
    c_convw = cv.alloc("C", 128, 34 * 4, F32)
    c_convb = cv.alloc("C", 128, 34, F32)
    c_dtb = cv.alloc("C", 64, 1, F32)
    c_alog = cv.alloc("C", 64, 1, F32)
    c_nega = cv.alloc("C", 64, 1, F32)
    c_mcol = cv.alloc("C", 128, 7, F32)
    c_minv = cv.alloc("C", 128, 7, F32)

    a_hm = cv.alloc("A", 64, T, F32)
    dtp_hm = cv.alloc("A", 64, T, F32)
    v_hm = cv.alloc("A", 64, T, F32)
    offac = cv.alloc("A", 64, 1, F32)
    fcell = cv.alloc("C", 1, 1, F32)

    def sfence(waits=()):
        return P.op("scalar", lambda e: e.mul(fcell, fcell, 1.0),
                    waits=list(waits), inc=("S", 1))

    def dma(eng, dst, src, sem, waits=(), transpose=False):
        if transpose:
            fn = lambda e, d=dst, s=src: e.dma_start(d, s, transpose=True)
        else:
            fn = lambda e, d=dst, s=src: e.dma_start(d, s)
        return P.op(eng, fn, waits=waits, inc=(sem, 16))

    def wload(src_h, f0, width, dst_ap, sem, DD, waits=(), cast=True):
        C = DD // 128
        Fw = src_h.shape[1]
        src = bass.AP(src_h, f0, [[Fw, 128], [128 * Fw, C], [1, width]])
        d3 = dst_ap.rearrange("p (c j) -> p c j", j=width)
        eng = "gpsimd" if cast else "sync"
        return P.op(eng, lambda e, d=d3, s=src: e.dma_start(d, s),
                    waits=waits, inc=(sem, 16))

    def act(out, in_, f, bias=0.0, scale=1.0, accum=None):
        return lambda e, o=out, i=in_, ff=f, b=bias, sc=scale, a=accum: \
            e.activation(o, i, ff, bias=b, scale=sc, accum_out=a)

    def rms_scale(src, n, dump_ap, c_ss, c_ms, c_e, out_r, waits):
        P.op("scalar", act(dump_ap, src, AF.Square), waits=list(waits),
             inc=("S", 1))
        P.op("vector", lambda e, o=c_ss, i=dump_ap: e.reduce_sum(
            o, i, AX.X), waits=[P.tok("S")], inc=("V", 1))
        P.op("scalar", lambda e, o=c_ms, i=c_ss: e.activation(
            o, i, AF.Copy, scale=1.0 / n), waits=[P.tok("V")], inc=("S", 1))
        P.op("vector", lambda e, o=c_e, a=c_ms: e.tensor_scalar(
            o, a, 1.0, EPS, ALU.mult, ALU.add), waits=[P.tok("S")],
            inc=("V", 1))
        P.op("scalar", lambda e, o=c_ms, a=c_e: e.activation(o, a, AF.Ln),
             waits=[P.tok("V")], inc=("S", 1))
        return P.op("scalar", lambda e, o=out_r, a=c_ms: e.activation(
            o, a, AF.Exp, scale=-0.5), inc=("S", 1))

    for dst, src in [(c_cle, cle[:, :]), (c_cle16, cle16[:, :]),
                     (c_up, up16[:, :]),
                     (c_iden, iden[:, :]), (c_ones64, ones64[:, :]),
                     (c_ones1, ones1[:, :]), (c_convw, convw[:, :]),
                     (c_convb, convb[:, :]), (c_dtb, dtb[:, :]),
                     (c_alog, alog[:, :]), (c_mcol, mcol[:, :]),
                     (c_minv, minv[:, :])]:
        dma("sync", dst, src, "SD")

    def stage_ln():
        cv.reset("S")
        cv.reset("B")
        x1fm = cv.alloc("B", 128, KD * 1040, BF16)
        _refs["x1fm"] = x1fm
        xt = [cv.alloc("S", 128, D, F32) for _ in range(2)]
        xnt = [cv.alloc("S", 128, D, F32) for _ in range(2)]
        xh1t = [cv.alloc("S", 128, D, BF16) for _ in range(2)]
        xn16t = [cv.alloc("S", 128, D, BF16) for _ in range(2)]
        dumps = [cv.alloc("S", 128, D, F32) for _ in range(2)]
        cols = [{n: cv.alloc("S", 128, 1, F32)
                 for n in ("sumc", "ssc", "mean", "msq", "m2", "t1", "vpe",
                           "lnv", "rstd", "negmr", "ms1", "msm", "mse", "r1")}
                for _ in range(2)]
        lnbc = cv.alloc("S", 128, D, F32)
        dma("sync", lnbc, ln1_bc[:, :], "LB")
        lnbc_tok = P.tok("LB")
        P.op("vector", lambda e: e.memset(xh1t[0][:, :], 0.0), inc=("V", 1))
        P.op("vector", lambda e: e.memset(xh1t[1][:, :], 0.0), inc=("V", 1))

        tiles = [(-1, 0, 3)] + [(c, 3 + c * 128, 128) for c in range(8)]
        tile_done = [None, None]
        spillt = [None, None]
        xpt = [None, None]
        for idx, (c, r0, rows) in enumerate(tiles):
            par = idx % 2
            cl = {n: a[0:rows, :] for n, a in cols[par].items()}
            xb = xt[par][0:rows, :]
            dump = dumps[par]
            gw = [] if tile_done[par] is None else [tile_done[par]]
            lt = dma("sync", xb, xin[r0:r0 + rows, :], f"L{par}", waits=gw)
            P.op("scalar", act(dump[0:rows, :], xb, AF.Square),
                 waits=[lt] + gw, inc=("S", 1))
            P.op("vector", lambda e, a=cl["ssc"], i=dump[0:rows, :]:
                 e.reduce_sum(a, i, AX.X), waits=[P.tok("S")], inc=("V", 1))
            P.op("vector", lambda e, a=cl["sumc"], b=xb: e.reduce_sum(
                a, b, AX.X), waits=[lt] + gw, inc=("V", 1))
            P.op("scalar", lambda e, o=cl["mean"], i=cl["sumc"]:
                 e.activation(o, i, AF.Copy, scale=1.0 / D),
                 waits=[P.tok("V")], inc=("S", 1))
            P.op("scalar", lambda e, o=cl["msq"], i=cl["ssc"]:
                 e.activation(o, i, AF.Copy, scale=1.0 / D), inc=("S", 1))
            sq_tok = P.tok("S")
            P.op("vector", lambda e, o=cl["m2"], m=cl["mean"]:
                 e.tensor_tensor(o, m, m, ALU.mult), waits=[sq_tok],
                 inc=("V", 1))
            P.op("vector", lambda e, o=cl["t1"], a=cl["m2"]: e.tensor_scalar(
                o, a, -1.0, EPS, ALU.mult, ALU.add), inc=("V", 1))
            P.op("vector", lambda e, o=cl["vpe"], a=cl["msq"], b=cl["t1"]:
                 e.tensor_tensor(o, a, b, ALU.add), inc=("V", 1))
            P.op("scalar", lambda e, o=cl["lnv"], a=cl["vpe"]: e.activation(
                o, a, AF.Ln), waits=[P.tok("V")], inc=("S", 1))
            P.op("scalar", lambda e, o=cl["rstd"], a=cl["lnv"]:
                 e.activation(o, a, AF.Exp, scale=-0.5), inc=("S", 1))
            P.op("vector", lambda e, o=cl["negmr"], m=cl["mean"],
                 r=cl["rstd"]:
                 e.scalar_tensor_tensor(o, m, -1.0, r, ALU.mult, ALU.mult),
                 waits=[P.tok("S")], inc=("V", 1))
            xw = [P.tok("V")]
            if spillt[par] is not None:
                xw.append(spillt[par])
            xo = xnt[par][0:rows, :]
            P.op("vector", lambda e, o=xo, i=xb, r=cl["rstd"],
                 b=cl["negmr"]:
                 e.tensor_scalar(o, i, r, b, ALU.mult, ALU.add),
                 waits=xw, inc=("V", 1))
            xn_tok = P.tok("V")
            if c >= 0:
                dma("sync", xn_h[c * 128:(c + 1) * 128, :], xo, f"SP{par}",
                    waits=[xn_tok])
            rms_scale(xo, D, dump[0:rows, :], cl["ms1"], cl["msm"],
                      cl["mse"], cl["r1"], [xn_tok])
            xw2 = [P.tok("S"), lnbc_tok]
            if xpt[par] is not None:
                xw2.append(xpt[par])
            P.op("vector", lambda e, o=xh1t[par][0:rows, :], i=xo,
                 r=cl["r1"], l=lnbc[0:rows, :]: e.scalar_tensor_tensor(
                     o, i, r, l, ALU.mult, ALU.mult),
                 waits=xw2, inc=("V", 1))
            dma("sync", xh1_h[r0:r0 + rows, :], xh1t[par][0:rows, :],
                f"XP{par}", waits=[P.tok("V")])
            if c >= 0:
                P.op("vector", lambda e, o=xn16t[par][0:rows, :], i=xo:
                     e.tensor_copy(o, i), inc=("V", 1))
                dma("sync", xn16_h[c * 128:(c + 1) * 128, :],
                    xn16t[par][0:rows, :], f"SP{par}", waits=[P.tok("V")])
                spillt[par] = P.tok(f"SP{par}")
            xpt[par] = (f"XP{par}", P.cnt[f"XP{par}"])
            tile_done[par] = P.tok("V")

    def stage_inproj():
        cv.reset("S")
        x1fm = _refs["x1fm"]
        for k in range(KD):
            b = k * 1040
            eng = "sync"
            dma(eng, x1fm[:, b:b + 16], xh1_h[0:16, k * 128:(k + 1) * 128],
                "X0", transpose=True)
            dma(eng, x1fm[:, b + 16:b + 528],
                xh1_h[3:515, k * 128:(k + 1) * 128], "X0", transpose=True)
            dma(eng, x1fm[:, b + 528:b + 1040],
                xh1_h[515:1027, k * 128:(k + 1) * 128], "X0", transpose=True)
        x1_tok = P.tok("X0")
        wz = [cv.alloc("S", 128, KD * 512, BF16) for _ in range(2)]
        wxbc = [cv.alloc("S", 128, KD * 128, BF16) for _ in range(2)]
        wdt = cv.alloc("S", 128, KD * 64, BF16)
        xbcbuf = [cv.alloc("S", 128, 1027, BF16) for _ in range(2)]
        acc = [[cv.alloc("S", 128, T, F32) for _ in range(2)] for _ in range(2)]
        xpo = [cv.alloc("S", 128, T, BF16) for _ in range(2)]
        szt = [[cv.alloc("S", 128, 512, BF16) for _ in range(8)]
               for _ in range(2)]
        tmp64 = cv.alloc("S", 64, T, F32)


        def lhs_tc(k, c):
            b = k * 1040 + 16 + c * 128
            return x1fm[:, b:b + 128]

        def rhs_main(k, half):
            b = k * 1040 + 16 + half * 512
            return x1fm[:, b:b + 512]

        def rhs_halo(k):
            return x1fm[:, k * 1040:k * 1040 + 3]

        wl = wload(wipT, 2 * DI + 2 * DS, 64, wdt, "W0", D)
        for half in range(2):
            ps = pb[6 + half][0:64, :]
            for k in range(KD):
                P.op("tensor", lambda e, p=ps, l=wdt[:, k * 64:(k + 1) * 64],
                     r=rhs_main(k, half), st=(k == 0), sp=(k == KD - 1):
                     e.matmul(p, l, r, start=st, stop=sp),
                     waits=[wl, x1_tok] if k == 0 else [], inc=("T", 1))
            P.op("vector", lambda e, o=v_hm[:, half * 512:(half + 1) * 512],
                 p=ps: e.tensor_scalar(o, p, c_dtb, None, ALU.add),
                 waits=[P.tok("T")], inc=("V", 1))
        dt_evac = P.tok("V")

        evac_t = [None, None]
        conv_t = [None, None]
        silu_t = [None, None]
        spill_t = [None, None]
        for fc in range(34):
            par = fc % 2
            wl = wload(wipT, DI + fc * 128, 128, wxbc[par], f"WX{par}", D,
                       waits=[evac_t[par]] if evac_t[par] else [])
            bs = [pb[3 * par], pb[3 * par + 1], pb[3 * par + 2]]
            w0 = [wl, x1_tok]
            if evac_t[par]:
                w0.append(evac_t[par])
            for k in range(KD):
                lt = wxbc[par][:, k * 128:(k + 1) * 128]
                P.op("tensor", lambda e, p=bs[0][:, :], l=lt, r=rhs_main(k, 0),
                     st=(k == 0), sp=(k == KD - 1):
                     e.matmul(p, l, r, start=st, stop=sp),
                     waits=w0 if k == 0 else [], inc=("T", 1))
                P.op("tensor", lambda e, p=bs[1][:, :], l=lt, r=rhs_main(k, 1),
                     st=(k == 0), sp=(k == KD - 1):
                     e.matmul(p, l, r, start=st, stop=sp), inc=("T", 1))
                P.op("tensor", lambda e, p=bs[2][:, 0:3], l=lt, r=rhs_halo(k),
                     st=(k == 0), sp=(k == KD - 1):
                     e.matmul(p, l, r, start=st, stop=sp), inc=("T", 1))
            mm = P.tok("T")
            ew = [mm]
            if conv_t[par]:
                ew.append(conv_t[par])
            xb = xbcbuf[par]
            P.op("scalar", act(xb[:, 0:3], bs[2][:, 0:3], AF.Copy),
                 waits=ew, inc=("S", 1))
            P.op("scalar", act(xb[:, 3:515], bs[0][:, :], AF.Copy),
                 inc=("S", 1))
            P.op("scalar", act(xb[:, 515:1027], bs[1][:, :], AF.Copy),
                 inc=("S", 1))
            evac_t[par] = P.tok("S")
            a0, a1 = acc[par]
            cw = [evac_t[par]]
            if silu_t[par]:
                cw.append(silu_t[par])
            P.op("vector", lambda e, o=a0, i=xb[:, 0:T],
                 w=c_convw[:, fc * 4:fc * 4 + 1]:
                 e.tensor_scalar(o, i, w, None, ALU.mult), waits=cw,
                 inc=("V", 1))
            P.op("vector", lambda e, o=a1, i=xb[:, 1:T + 1],
                 w=c_convw[:, fc * 4 + 1:fc * 4 + 2], a=a0:
                 e.scalar_tensor_tensor(o, i, w, a, ALU.mult, ALU.add),
                 inc=("V", 1))
            P.op("vector", lambda e, o=a0, i=xb[:, 2:T + 2],
                 w=c_convw[:, fc * 4 + 2:fc * 4 + 3], a=a1:
                 e.scalar_tensor_tensor(o, i, w, a, ALU.mult, ALU.add),
                 inc=("V", 1))
            P.op("vector", lambda e, o=a1, i=xb[:, 3:T + 3],
                 w=c_convw[:, fc * 4 + 3:fc * 4 + 4], a=a0:
                 e.scalar_tensor_tensor(o, i, w, a, ALU.mult, ALU.add),
                 inc=("V", 1))
            conv_t[par] = P.tok("V")
            sw = [conv_t[par]]
            if spill_t[par]:
                sw.append(spill_t[par])
            P.op("scalar", act(xpo[par], a1, AF.Silu,
                               bias=c_convb[:, fc:fc + 1]),
                 waits=sw, inc=("S", 1))
            silu_t[par] = P.tok("S")
            spill_t[par] = dma("sync", xbcp_h[fc * 128:(fc + 1) * 128, :],
                               xpo[par], f"XS{par}", waits=[silu_t[par]])

        zsp = [None, None]
        prev_z = None
        for zg in range(8):
            par = zg % 2
            wl = wload(wipT, zg * 512, 512, wz[par], f"WZ{par}", D,
                       waits=[prev_z] if prev_z else [])
            w0 = [wl, x1_tok, dt_evac]
            if prev_z:
                w0.append(prev_z)
            if evac_t[0]:
                w0 += [evac_t[0], evac_t[1]]
            for k in range(KD):
                for c in range(8):
                    P.op("tensor", lambda e, p=pb[c][:, :], l=lhs_tc(k, c),
                         r=wz[par][:, k * 512:(k + 1) * 512],
                         st=(k == 0), sp=(k == KD - 1):
                         e.matmul(p, l, r, start=st, stop=sp),
                         waits=w0 if (k == 0 and c == 0) else [], inc=("T", 1))
            mm = P.tok("T")
            for c in range(8):
                sw = [mm]
                if zsp[par]:
                    sw.append(zsp[par])
                P.op("scalar", act(szt[par][c], pb[c][:, :], AF.Silu),
                     waits=sw, inc=("S", 1))
                dma("sync", sz_h[c * 128:(c + 1) * 128,
                                 zg * 512:(zg + 1) * 512],
                    szt[par][c], f"ZS{par}", waits=[P.tok("S")])
            zsp[par] = P.tok(f"ZS{par}")
            prev_z = P.tok("S")

        P.op("scalar", lambda e: e.activation(tmp64, v_hm, AF.Abs),
             waits=[dt_evac], inc=("S", 1))
        P.op("scalar", lambda e: e.activation(tmp64, tmp64, AF.Exp,
                                              scale=-1.0), inc=("S", 1))
        P.op("scalar", lambda e: e.activation(tmp64, tmp64, AF.Ln, bias=1.0),
             inc=("S", 1))
        P.op("vector", lambda e: e.tensor_scalar(dtp_hm, v_hm, 0.0, None,
                                                 ALU.max),
             waits=[dt_evac], inc=("V", 1))
        P.op("vector", lambda e: e.tensor_tensor(dtp_hm, dtp_hm, tmp64,
                                                 ALU.add),
             waits=[P.tok("S")], inc=("V", 1))
        P.op("scalar", lambda e: e.activation(c_nega, c_alog, AF.Exp),
             inc=("S", 1))
        P.op("vector", lambda e: e.tensor_scalar(a_hm, dtp_hm, c_nega, None,
                                                 ALU.mult),
             waits=[P.tok("S")], inc=("V", 1))
        P.op("vector", lambda e: e.tensor_scalar(a_hm, a_hm, -1.0, None,
                                                 ALU.mult), inc=("V", 1))
        P.op("vector", lambda e: e.memset(offac, 0.0), inc=("V", 1))
        if dtp_h is not None:
            dma("sync", dtp_h[0:64, :], dtp_hm, "SD", waits=[P.tok("V")])

    def stage_scan_a():
        cv.reset("S")
        cv.reset("H")
        cv.reset("B")
        H = cv.alloc("B", 128, DI, F32)
        C_all = cv.alloc("B", 128, NCK * 128, BF16)
        Eg_all = cv.alloc("B", 128, NCK * 64, F32)
        _refs["C_all"] = C_all
        _refs["Eg_all"] = Eg_all
        x_tok = [cv.alloc("S", 128, DI, BF16) for _ in range(2)]
        Bfm = [cv.alloc("S", 128, 128, BF16) for _ in range(2)]
        Btok = [cv.alloc("S", 128, 128, BF16) for _ in range(2)]
        Cfm = [cv.alloc("S", 128, 128, BF16) for _ in range(2)]
        Xdt = [cv.alloc("S", 128, DI, BF16) for _ in range(2)]
        Xdd = [cv.alloc("S", 128, DI, BF16) for _ in range(2)]
        S_all = cv.alloc("S", 128, NH * 128, BF16)
        rhs_hi = [cv.alloc("S", 128, 512, BF16) for _ in range(2)]
        rhs_lo = [cv.alloc("S", 128, 512, BF16) for _ in range(2)]
        egrp = [cv.alloc("S", 128, 512, BF16) for _ in range(2)]
        gpm = cv.alloc("S", 128, 128, BF16)
        L_a = cv.alloc("S", 64, 128, F32)
        L_b = cv.alloc("S", 64, 128, F32)
        Lg = cv.alloc("S", 64, 128, F32)
        dec_hm = cv.alloc("S", 64, 128, F32)
        dec_e = cv.alloc("S", 64, 128, F32)
        Lt = cv.alloc("S", 128, 64, F32)
        Lgt = cv.alloc("S", 128, 64, F32)
        atok = cv.alloc("S", 128, 64, F32)
        dtok = cv.alloc("S", 128, 64, F32)
        dectok = cv.alloc("S", 128, 64, F32)
        ddec = cv.alloc("S", 128, 64, F32)
        ahi = cv.alloc("S", 128, 64, F32)
        alo = cv.alloc("S", 128, 64, F32)
        ahi16 = cv.alloc("S", 128, 64, BF16)
        eLt = cv.alloc("S", 128, 64, F32)
        edt_c = cv.alloc("S", 64, 1, F32)
        edt_r = cv.alloc("S", 1, 64, F32)
        dtot_bc = cv.alloc("S", 128, 64, F32)
        y = cv.alloc("H", 128, DI, F32)
        tmb = cv.alloc("H", 128, DI, F32)
        dtex = tmb
        H16 = cv.alloc("H", 128, DI, BF16)
        dxbc = cv.alloc("H", 128, DI, F32)

        def p7(o, n, p0=0, p1=128):
            return pb[7][p0:p1, o:o + n]

        def bview(ap, nh, rep):
            return bass.AP(ap.tensor, ap.offset,
                           [[ap.tensor.shape[1], 128], [1, nh], [0, rep]])

        def tview(ap, rep, w):
            return bass.AP(ap.tensor, ap.offset,
                           [[ap.tensor.shape[1], 128], [0, rep], [1, w]])

        P.op("vector", lambda e: e.memset(H, 0.0), inc=("V", 1))
        dxt = dma("sync", dxbc, dexp_bc[:, :], "DX")
        hz = P.tok("V")

        buf_free = [None, None]
        tdone = None
        vdone = None
        prev_prep = None
        prev_tr1 = None
        prev_gtok = None
        xdt_free = [None, None]
        lam_rd = [None, None]
        lam_tok = [None, None]
        s_done = [None, None]
        yA_rd = [None, None]
        yB_rd = [None, None]
        egx_rd = [None, None]
        dC_rd = None
        for c in range(nch):
            par = c % 2
            lw = []
            if buf_free[par] is not None:
                lw = buf_free[par]
            for b in range(32):
                P.op("sync",
                     lambda e, d=x_tok[par][:, b * 128:(b + 1) * 128],
                     s=xbcp_h[b * 128:(b + 1) * 128, c * 128:(c + 1) * 128],
                     t=not noxl:
                     e.dma_start(d, s, transpose=t),
                     waits=lw if b < 2 else [], inc=(f"XT{par}", 16))
            P.op("sync", lambda e, d=Btok[par],
                 s=xbcp_h[DI:DI + 128, c * 128:(c + 1) * 128], t=not noxl:
                 e.dma_start(d, s, transpose=t), inc=(f"XT{par}", 16))
            P.op("sync", lambda e, d=Bfm[par],
                 s=xbcp_h[DI:DI + 128, c * 128:(c + 1) * 128]:
                 e.dma_start(d, s), inc=(f"XT{par}", 16))
            P.op("sync", lambda e, d=Cfm[par],
                 s=xbcp_h[DI + 128:DI + 256, c * 128:(c + 1) * 128]:
                 e.dma_start(d, s), inc=(f"XT{par}", 16))
            ld = P.tok(f"XT{par}")

            if sp < 1:
                vdone = P.tok("V")
                buf_free[par] = [vdone, P.tok("T")]
                tdone = P.tok("T")
                continue
            asl = a_hm[:, c * 128:(c + 1) * 128]
            wv = [t for t in (prev_tr1, prev_gtok) if t]
            P.op("vector", lambda e, a=asl: e.tensor_copy(L_a, a),
                 waits=wv, inc=("V", 1))
            src, dst = L_a, L_b
            for s in [1, 2, 4, 8, 16, 32, 64]:
                P.op("vector", lambda e, sh=s, i=src, o=dst: e.tensor_tensor(
                    o[:, sh:128], i[:, sh:128], i[:, 0:128 - sh], ALU.add),
                    inc=("V", 1))
                P.op("vector", lambda e, sh=s, i=src, o=dst: e.tensor_copy(
                    o[:, 0:sh], i[:, 0:sh]), inc=("V", 1))
                src, dst = dst, src
            L_hm = src
            P.op("vector", lambda e, L=L_hm: e.tensor_scalar(
                Lg, L, offac, None, ALU.add), inc=("V", 1))
            P.op("vector", lambda e, L=L_hm: e.tensor_tensor(
                offac, offac, L[:, 127:128], ALU.add), inc=("V", 1))
            P.op("vector", lambda e, L=L_hm: e.tensor_scalar(
                dec_hm, L, L[:, 127:128], None, ALU.subtract), inc=("V", 1))
            vpre = P.tok("V")
            P.op("scalar", lambda e: e.activation(dec_e, dec_hm, AF.Exp,
                                                  scale=-1.0),
                 waits=[vpre] + ([prev_tr1] if prev_tr1 else []),
                 inc=("S", 1))
            P.op("scalar", lambda e, L=L_hm: e.activation(
                edt_c, L[:, 127:128], AF.Exp), inc=("S", 1))
            lch = P.tok("S")

            trw = [lch]
            if tdone:
                trw.append(tdone)
            if prev_prep:
                trw.append(prev_prep)
            dsl = dtp_hm[:, c * 128:(c + 1) * 128]
            P.op("tensor", lambda e, i=asl: e.transpose(
                p7(0, 64), i, c_iden[0:64, 0:64]), waits=trw, inc=("T", 1))
            P.op("tensor", lambda e, i=dsl: e.transpose(
                p7(64, 64), i, c_iden[0:64, 0:64]), inc=("T", 1))
            P.op("tensor", lambda e, L=L_hm: e.transpose(
                p7(128, 64), L, c_iden[0:64, 0:64]), inc=("T", 1))
            P.op("tensor", lambda e: e.transpose(
                p7(192, 64), Lg, c_iden[0:64, 0:64]), inc=("T", 1))
            P.op("tensor", lambda e: e.transpose(
                p7(256, 64), dec_e, c_iden[0:64, 0:64]), inc=("T", 1))
            P.op("tensor", lambda e: e.transpose(
                p7(448, 64, 0, 1), edt_c, c_iden[0:64, 0:64]), inc=("T", 1))
            P.op("tensor", lambda e, l=Bfm[par], r=Cfm[par]: e.matmul(
                p7(320, 128), l, r, start=True, stop=True),
                waits=[ld], inc=("T", 1))
            tr1 = P.tok("T")
            prev_tr1 = tr1
            P.op("vector", lambda e: e.tensor_copy(atok, p7(0, 64)),
                 waits=[tr1], inc=("V", 1))
            P.op("vector", lambda e: e.tensor_copy(dtok, p7(64, 64)),
                 inc=("V", 1))
            P.op("vector", lambda e: e.tensor_copy(Lt, p7(128, 64)),
                 inc=("V", 1))
            P.op("vector", lambda e: e.tensor_copy(Lgt, p7(192, 64)),
                 inc=("V", 1))
            P.op("vector", lambda e: e.tensor_copy(dectok, p7(256, 64)),
                 inc=("V", 1))
            P.op("vector", lambda e: e.tensor_tensor(gpm, p7(320, 128),
                                                     c_cle, ALU.mult),
                 inc=("V", 1))
            P.op("vector", lambda e: e.tensor_copy(edt_r, p7(448, 64, 0, 1)),
                 inc=("V", 1))
            ev1 = P.tok("V")
            evs = P.tok("S")
            P.op("scalar", lambda e: e.activation(eLt, Lt, AF.Exp),
                 waits=[ev1] + ([prev_gtok] if prev_gtok else []),
                 inc=("S", 1))
            P.op("scalar", lambda e, o=Eg_all[:, c * 64:(c + 1) * 64]:
                 e.activation(o, Lgt, AF.Exp), inc=("S", 1))
            eL_tok = P.tok("S")
            P.op("tensor", lambda e: e.matmul(p7(0, 64), c_ones1, edt_r,
                                              start=True, stop=True),
                 waits=[ev1, evs], inc=("T", 1))
            P.op("vector", lambda e: e.tensor_copy(dtot_bc, p7(0, 64)),
                 waits=[P.tok("T")], inc=("V", 1))
            P.op("vector", lambda e: e.tensor_tensor(ddec, dtok, dectok,
                                                     ALU.mult), inc=("V", 1))
            prep = P.tok("V")
            prev_prep = prep

            if sp < 2:
                vdone = P.tok("V")
                buf_free[par] = [vdone, P.tok("T")]
                tdone = P.tok("T")
                continue
            sgrp_tok = []
            for g in range(16):
                lp = g % 2
                bank = pb[lp]
                bw = [prep]
                if s_done[lp]:
                    bw.append(s_done[lp])
                if lam_tok[lp]:
                    bw.append(lam_tok[lp])
                if g < 2 and tdone:
                    bw.append(tdone)
                P.op("vector", lambda e,
                     o=rhs_hi[lp][:, :].rearrange("p (h i) -> p h i", h=4),
                     i0=tview(c_cle16, 4, 128),
                     i1=bview(atok[:, g * 4:g * 4 + 4], 4, 128):
                     e.tensor_tensor(o, i0, i1, ALU.mult),
                     waits=bw, inc=("V", 1))
                rb = P.tok("V")
                mw = [rb]
                if lam_rd[lp]:
                    mw.append(lam_rd[lp])
                P.op("tensor", lambda e, p=bank[:, :], r=rhs_hi[lp]: e.matmul(
                    p, c_up, r, start=True, stop=True), waits=mw,
                    inc=("T", 1))
                lam = P.tok("T")
                lam_tok[lp] = lam
                ew = [lam]
                if s_done[lp]:
                    ew.append(s_done[lp])
                P.op("scalar", lambda e, o=egrp[lp], p=bank[:, :]:
                     e.activation(o, p, AF.Exp), waits=ew, inc=("S", 1))
                lam_rd[lp] = P.tok("S")
                sw = [lam_rd[lp], ev1]
                P.op("vector", lambda e,
                     o=S_all[:, g * 512:(g + 1) * 512].rearrange(
                         "p (h i) -> p h i", h=4),
                     i0=egrp[lp][:, :].rearrange("p (h i) -> p h i", h=4),
                     i1=tview(gpm, 4, 128):
                     e.tensor_tensor(o, i0, i1, ALU.mult),
                     waits=sw, inc=("V", 1))
                s_done[lp] = P.tok("V")
                sgrp_tok.append(s_done[lp])

            if sp < 3:
                vdone = P.tok("V")
                buf_free[par] = [vdone, P.tok("T")]
                tdone = P.tok("T")
                continue
            xw0 = [prep, ld] + ([xdt_free[par]] if xdt_free[par] else [])
            P.op("vector", lambda e,
                 o=Xdt[par][:, :].rearrange("p (h i) -> p h i", h=64),
                 i0=x_tok[par][:, :].rearrange("p (h i) -> p h i", h=64),
                 i1=bview(dtok, 64, 64):
                 e.tensor_tensor(o, i0, i1, ALU.mult),
                 waits=xw0, inc=("V", 1))
            P.op("vector", lambda e,
                 o=Xdd[par][:, :].rearrange("p (h i) -> p h i", h=64),
                 i0=x_tok[par][:, :].rearrange("p (h i) -> p h i", h=64),
                 i1=bview(ddec, 64, 64):
                 e.tensor_tensor(o, i0, i1, ALU.mult), inc=("V", 1))
            xdt_tok = P.tok("V")
            xdd_tok = xdt_tok
            hw = [hz]
            if tdone:
                hw.append(tdone)
            P.op("vector", lambda e: e.tensor_copy(H16, H), waits=hw,
                 inc=("V", 1))
            h16_tok = P.tok("V")

            if sp < 4:
                vdone = P.tok("V")
                buf_free[par] = [vdone, P.tok("T")]
                tdone = P.tok("T")
                continue
            spill_gate = P.tok("YS") if c > 0 else None
            for g in range(8):
                gp = g % 2
                bA = pb[2 + gp]
                bB = pb[4 + gp]
                bC = pb[6]
                ecols = eL_tok
                wA = [h16_tok, ld]
                if yA_rd[gp]:
                    wA.append(yA_rd[gp])
                P.op("tensor", lambda e, p=bA[:, :], l=Cfm[par],
                     r=H16[:, g * 512:(g + 1) * 512]:
                     e.matmul(p, l, r, start=True, stop=True), waits=wA,
                     inc=("T", 1))
                wB = [sgrp_tok[(g * 8) // 4], sgrp_tok[(g * 8 + 7) // 4],
                      xdt_tok]
                if yB_rd[gp]:
                    wB.append(yB_rd[gp])
                for hh in range(8):
                    h = g * 8 + hh
                    P.op("tensor", lambda e, p=bB[:, hh * 64:(hh + 1) * 64],
                         l=S_all[:, h * 128:(h + 1) * 128],
                         r=Xdt[par][:, h * 64:(h + 1) * 64]:
                         e.matmul(p, l, r, start=True, stop=True),
                         waits=wB if hh == 0 else [], inc=("T", 1))
                yt = P.tok("T")
                dw = [xdd_tok, ld]
                if dC_rd:
                    dw.append(dC_rd)
                for hh in range(8):
                    h = g * 8 + hh
                    P.op("tensor", lambda e, p=bC[:, hh * 64:(hh + 1) * 64],
                         l=Btok[par], r=Xdd[par][:, h * 64:(h + 1) * 64]:
                         e.matmul(p, l, r, start=True, stop=True),
                         waits=dw if hh == 0 else [], inc=("T", 1))
                dtk = P.tok("T")
                yw = [yt, ecols]
                if spill_gate:
                    yw.append(spill_gate)
                P.op("vector", lambda e,
                     o=y[:, g * 512:(g + 1) * 512].rearrange(
                         "p (h i) -> p h i", h=8),
                     p=bA[:, :].rearrange("p (h i) -> p h i", h=8),
                     ee=bview(eLt[:, g * 8:(g + 1) * 8], 8, 64):
                     e.tensor_tensor(o, p, ee, ALU.mult),
                     waits=yw, inc=("V", 1))
                yA_rd[gp] = P.tok("V")
                P.op("vector", lambda e, o=y[:, g * 512:(g + 1) * 512],
                     p=bB[:, :]: e.tensor_tensor(o, o, p, ALU.add),
                     inc=("V", 1))
                yB_rd[gp] = P.tok("V")
                P.op("vector", lambda e,
                     o=H[:, g * 512:(g + 1) * 512].rearrange(
                         "p (h i) -> p h i", h=8),
                     dd=bview(dtot_bc[:, g * 8:(g + 1) * 8], 8, 64):
                     e.tensor_tensor(o, o, dd, ALU.mult),
                     waits=[dtk, h16_tok], inc=("V", 1))
                P.op("vector", lambda e, o=H[:, g * 512:(g + 1) * 512],
                     p=bC[:, :]: e.tensor_tensor(o, o, p, ALU.add),
                     inc=("V", 1))
                dC_rd = P.tok("V")
            P.op("vector", lambda e, o=C_all[:, c * 128:(c + 1) * 128],
                 i=Cfm[par]: e.tensor_copy(o, i), waits=[ld], inc=("V", 1))
            P.op("vector", lambda e, xp=x_tok[par]: e.tensor_tensor(
                tmb, xp, dxbc, ALU.mult), waits=[dxt, xdt_tok], inc=("V", 1))
            P.op("vector", lambda e: e.tensor_tensor(y, y, tmb, ALU.add),
                 inc=("V", 1))
            vdone = P.tok("V")
            buf_free[par] = [vdone, P.tok("T"), P.tok("G")]
            tdone = P.tok("T")
            xdt_free[par] = tdone
            prev_gtok = P.tok("G")
            dma("sync", yp_h[c * 128:(c + 1) * 128, :], y, "YS",
                waits=[vdone])

        if noseg:
            P.op("gpsimd", lambda e: e.dma_start(
                agout[0:128, :], agin[:, :]), waits=[P.tok("GD")] if P.cnt.get("GD") else [],
                inc=("CC", 16))
            return
        P.op("scalar", lambda e: e.activation(edt_c, offac, AF.Exp),
             waits=[P.tok("V")], inc=("S", 1))
        P.op("tensor", lambda e: e.transpose(
            p7(448, 64, 0, 1), edt_c, c_iden[0:64, 0:64]),
            waits=[P.tok("S"), P.tok("V")], inc=("T", 1))
        P.op("vector", lambda e: e.tensor_copy(edt_r, p7(448, 64, 0, 1)),
             waits=[P.tok("T")], inc=("V", 1))
        P.op("tensor", lambda e: e.matmul(p7(0, 64), c_ones1, edt_r,
                                          start=True, stop=True),
             waits=[P.tok("V")], inc=("T", 1))
        P.op("vector", lambda e: e.tensor_copy(dtot_bc, p7(0, 64)),
             waits=[P.tok("T")], inc=("V", 1))
        for h in range(NH):
            P.op("vector", lambda e, o=dtex[:, h * 64:(h + 1) * 64],
                 s=dtot_bc[:, h:h + 1]: e.tensor_scalar(
                     o, c_ones64, s, None, ALU.mult), inc=("V", 1))
        vv = P.tok("V")
        dma("gpsimd", agin[:, 0:DI], H, "GD", waits=[vv])
        dma("gpsimd", agin[:, DI:2 * DI], dtex, "GD")
        if nocc:
            P.op("gpsimd", lambda e: e.dma_start(
                agout[0:128, :], agin[:, :]), waits=[P.tok("GD")],
                inc=("CC", 16))
        else:
            P.op("gpsimd", lambda e: e.collective_compute(
                "AllGather", ALU.bypass, replica_groups=[list(range(8))],
                ins=[agin[:, :]], outs=[agout[:, :]]),
                waits=[P.tok("GD")], inc=("CC", 1))

    def stage_g():
        cv.reset("S")
        cv.reset("H")
        wq = cv.alloc("H", 128, 4 * KD * 512, BF16)
        wfb = [cv.alloc("S", 128, KD * 512, F32) for _ in range(2)]
        accc = cv.alloc("S", 128, 1, F32)
        redc = cv.alloc("S", 128, 1, F32)
        tot = cv.alloc("S", 1, 1, F32)
        srow = cv.alloc("S", 1, 2, F32)
        sbc = cv.alloc("S", 128, 2, F32)
        xnfm = [cv.alloc("S", 128, KD * 128, BF16) for _ in range(2)]
        bb = cv.alloc("B", 128, D, F32)
        g16 = [cv.alloc("S", 128, 512, BF16) for _ in range(4)]
        tball = [cv.alloc("S", 128, 512, F32) for _ in range(2)]
        dma("sync", bb, bias_bc[:, :], "LB")
        bb_tok = P.tok("LB")

        def wf_load(gg, waits):
            src = bass.AP(wfT, gg * 512, [[D, 128], [128 * D, KD], [1, 512]])
            d3 = wfb[gg % 2].rearrange("p (c j) -> p c j", j=512)
            return P.op("gpsimd", lambda e, d=d3, s=src: e.dma_start(d, s),
                        waits=waits, inc=(f"WF{gg % 2}", 16))

        rd_done = [None, None]
        pw = None
        for gg in range(4):
            lt = wf_load(gg, [rd_done[gg % 2]] if rd_done[gg % 2] else [])
            P.op("vector", lambda e, w=wfb[gg % 2]: e.tensor_reduce(
                redc, w, AX.X, ALU.add, apply_absolute_value=True),
                waits=[lt] + ([pw] if pw else []), inc=("V", 1))
            rd_done[gg % 2] = P.tok("V")
            if gg == 0:
                P.op("scalar", lambda e: e.activation(accc, redc, AF.Copy),
                     waits=[P.tok("V")], inc=("S", 1))
            else:
                P.op("scalar", lambda e: e.add(accc, redc, accc),
                     waits=[P.tok("V")], inc=("S", 1))
            pw = P.tok("S")
        P.op("tensor", lambda e: e.matmul(pb[7][0:1, 0:1], c_ones64[:, 0:1],
                                          accc, start=True, stop=True),
             waits=[pw], inc=("T", 1))
        P.op("vector", lambda e: e.tensor_copy(tot, pb[7][0:1, 0:1]),
             waits=[P.tok("T")], inc=("V", 1))
        P.op("vector", lambda e: e.tensor_scalar(
            srow[0:1, 0:1], tot, 1.0 / (D * D), None, ALU.mult), inc=("V", 1))
        P.op("vector", lambda e: e.tensor_scalar(
            srow[0:1, 1:2], srow[0:1, 0:1], FEPS, None, ALU.add),
            inc=("V", 1))
        P.op("vector", lambda e: e.reciprocal(srow[0:1, 1:2], srow[0:1, 1:2]),
             inc=("V", 1))
        P.op("tensor", lambda e: e.matmul(pb[7][:, 4:6], c_ones1, srow,
                                          start=True, stop=True),
             waits=[P.tok("V")], inc=("T", 1))
        P.op("vector", lambda e: e.tensor_copy(sbc, pb[7][:, 4:6]),
             waits=[P.tok("T")], inc=("V", 1))
        sc_tok = P.tok("V")

        gts = cv.alloc("S", 128, 1024, F32)
        lts = cv.alloc("S", 128, 1024, F32)
        wq_gtok = [None] * 4
        pw = [None, None]
        for gg in range(4):
            lt = wf_load(gg, [sc_tok] + ([pw[gg % 2]] if pw[gg % 2] else []))
            for sl in range(KD // 2):
                wsl = wfb[gg % 2][:, sl * 1024:(sl + 1) * 1024]
                P.op("vector", lambda e, w=wsl: e.tensor_scalar(
                    gts, w, sbc[:, 1:2], 0.5, ALU.mult, ALU.is_gt),
                    waits=[lt] if sl == 0 else [], inc=("V", 1))
                P.op("vector", lambda e, w=wsl: e.tensor_scalar(
                    lts, w, sbc[:, 1:2], -0.5, ALU.mult, ALU.is_lt),
                    inc=("V", 1))
                P.op("vector", lambda e: e.tensor_tensor(
                    gts, gts, lts, ALU.subtract), inc=("V", 1))
                P.op("vector", lambda e,
                     o=wq[:, gg * KD * 512 + sl * 1024:
                          gg * KD * 512 + (sl + 1) * 1024]:
                     e.tensor_scalar(o, gts, sbc[:, 0:1], None, ALU.mult),
                     inc=("V", 1))
            pw[gg % 2] = P.tok("V")
            wq_gtok[gg] = P.tok("V")

        def wq_rhs(k, gg):
            b = gg * KD * 512 + k * 512
            return wq[:, b:b + 512]

        xn_free = [None, None]
        bank_free = [None, None, None, None]
        tb_free = [None, None]
        g_free = [[None, None] for _ in range(4)]
        for c in range(8):
            par = c % 2
            w0 = [] if xn_free[par] is None else [xn_free[par]]
            for k in range(KD):
                P.op("sync", lambda e, d=xnfm[par][:, k * 128:(k + 1) * 128],
                     s=xn16_h[c * 128:(c + 1) * 128, k * 128:(k + 1) * 128]:
                     e.dma_start(d, s, transpose=True),
                     waits=w0 if k == 0 else [], inc=(f"XN{par}", 16))
            xl = P.tok(f"XN{par}")
            for gg in range(4):
                mw = [xl, wq_gtok[gg]]
                if bank_free[gg]:
                    mw.append(bank_free[gg])
                for k in range(KD):
                    P.op("tensor", lambda e, p=pb[gg][:, :],
                         l=xnfm[par][:, k * 128:(k + 1) * 128],
                         r=wq_rhs(k, gg), st=(k == 0), sp=(k == KD - 1):
                         e.matmul(p, l, r, start=st, stop=sp),
                         waits=mw if k == 0 else [], inc=("T", 1))
            mm = P.tok("T")
            xn_free[par] = mm
            for gg in range(4):
                tp = gg % 2
                vw = [mm, bb_tok]
                if tb_free[tp]:
                    vw.append(tb_free[tp])
                P.op("vector", lambda e, o=tball[tp], p=pb[gg][:, :],
                     b=bb[:, gg * 512:(gg + 1) * 512]:
                     e.tensor_tensor(o, p, b, ALU.add), waits=vw,
                     inc=("V", 1))
                bank_free[gg] = P.tok("V")
                sw = [P.tok("V")]
                if g_free[gg][0]:
                    sw.append(g_free[gg][0])
                P.op("scalar", lambda e, o=g16[gg], i=tball[tp]:
                     e.activation(o, i, AF.Silu), waits=sw, inc=("S", 1))
                tb_free[tp] = P.tok("S")
                g_free[gg][0] = dma(
                    "gpsimd", g_h[c * 128:(c + 1) * 128,
                                  gg * 512:(gg + 1) * 512],
                    g16[gg], f"GS{gg}", waits=[P.tok("S")])

    def stage_scan_b():
        C_all = _refs["C_all"]
        Eg_all = _refs["Eg_all"]
        cv.reset("S")
        cv.reset("H")
        Hi = cv.alloc("H", 128, DI, F32)
        Hi16 = cv.alloc("H", 128, DI, BF16)
        y = cv.alloc("H", 128, DI, F32)
        dump = cv.alloc("H", 128, DI, F32)
        hjb = [cv.alloc("S", 128, 2 * DI, F32) for _ in range(2)]
        dtl = cv.alloc("S", 128, DI, F32)

        P.op("vector", lambda e: e.memset(Hi, 0.0), inc=("V", 1))
        hj_free = [None, None]
        for j in range(7):
            par = j % 2
            lw = [P.tok("CC")]
            if hj_free[par]:
                lw.append(hj_free[par])
            lt = dma("sync", hjb[par], agout[j * 128:(j + 1) * 128, :],
                     f"HJ{par}", waits=lw)
            P.op("vector", lambda e, i=hjb[par][:, DI:2 * DI],
                 m=c_mcol[:, j:j + 1], mi=c_minv[:, j:j + 1]:
                 e.tensor_scalar(dtl, i, m, mi, ALU.mult, ALU.add),
                 waits=[lt], inc=("V", 1))
            if j > 0:
                P.op("vector", lambda e: e.tensor_tensor(Hi, Hi, dtl,
                                                         ALU.mult),
                     inc=("V", 1))
            P.op("vector", lambda e, i=hjb[par][:, 0:DI],
                 m=c_mcol[:, j:j + 1]: e.scalar_tensor_tensor(
                     Hi, i, m, Hi, ALU.mult, ALU.add), inc=("V", 1))
            hj_free[par] = P.tok("V")
        P.op("vector", lambda e: e.tensor_copy(Hi16, Hi), inc=("V", 1))
        hi_tok = P.tok("V")

        cv.reset("S")
        sz = [cv.alloc("S", 128, DI, BF16) for _ in range(2)]
        ypl = [cv.alloc("S", 128, DI, F32) for _ in range(2)]
        yn16 = cv.alloc("S", 128, DI, BF16)
        nwb = cv.alloc("S", 128, DI, F32)
        ssc = cv.alloc("S", 128, 1, F32)
        msb = cv.alloc("S", 128, 1, F32)
        sse = cv.alloc("S", 128, 1, F32)
        r2 = cv.alloc("S", 128, 1, F32)
        nw_tok = dma("sync", nwb, nw_bc[:, :], "LB", waits=[hi_tok])

        buf_free = [None, None]
        yn_sp = None
        bank_rd = [None, None]
        for c in range(NCK):
            par = c % 2
            lw = [hi_tok]
            if buf_free[par]:
                lw.append(buf_free[par])
            dma("sync", ypl[par], yp_h[c * 128:(c + 1) * 128, :],
                f"YP{par}", waits=lw)
            dma("sync", sz[par], sz_h[c * 128:(c + 1) * 128, :], f"YP{par}")
            ld = P.tok(f"YP{par}")
            for g in range(8):
                gp = g % 2
                bA = pb[2 + gp]
                ecols = P.tok("V")
                wA = [hi_tok]
                if bank_rd[gp]:
                    wA.append(bank_rd[gp])
                P.op("tensor", lambda e, p=bA[:, :],
                     l=C_all[:, c * 128:(c + 1) * 128],
                     r=Hi16[:, g * 512:(g + 1) * 512]:
                     e.matmul(p, l, r, start=True, stop=True), waits=wA,
                     inc=("T", 1))
                mt = P.tok("T")
                yw = [mt, ecols, ld]
                if yn_sp and g == 0:
                    yw.append(yn_sp)
                P.op("vector", lambda e,
                     o=y[:, g * 512:(g + 1) * 512].rearrange(
                         "p (h i) -> p h i", h=8),
                     p=bA[:, :].rearrange("p (h i) -> p h i", h=8),
                     ee=bass.AP(Eg_all.tensor, Eg_all.offset + c * 64 + g * 8,
                                [[Eg_all.tensor.shape[1], 128], [1, 8],
                                 [0, 64]]):
                     e.tensor_tensor(o, p, ee, ALU.mult),
                     waits=yw, inc=("V", 1))
                bank_rd[gp] = P.tok("V")
                P.op("vector", lambda e, o=y[:, g * 512:(g + 1) * 512],
                     i=ypl[par][:, g * 512:(g + 1) * 512]:
                     e.tensor_tensor(o, o, i, ALU.add), inc=("V", 1))
            P.op("vector", lambda e, s=sz[par]: e.tensor_tensor(
                y, y, s, ALU.mult), inc=("V", 1))
            buf_free[par] = P.tok("V")
            rms_scale(y, DI, dump, ssc, msb, sse, r2, [P.tok("V")])
            P.op("vector", lambda e: e.scalar_tensor_tensor(
                yn16, y, r2, nwb, ALU.mult, ALU.mult),
                waits=[P.tok("S"), nw_tok] + ([yn_sp] if yn_sp else []),
                inc=("V", 1))
            yn_sp = dma("sync", yn16_h[c * 128:(c + 1) * 128, :], yn16, "YN",
                        waits=[P.tok("V")])

    def stage_outproj():
        cv.reset("S")
        cv.reset("H")
        cv.reset("B")
        hs = cv.alloc("H", 128, 8 * D, F32)
        x2fm = cv.alloc("B", 128, KD * T, BF16)
        _refs["hs"] = hs
        _refs["x2fm"] = x2fm
        wop = cv.alloc("S", 128, 32 * 512, BF16)
        ynb = cv.alloc("S", 128, 32 * 512, BF16)
        xnl = cv.alloc("S", 128, 512, F32)
        dump = cv.alloc("S", 128, D, F32)
        ssc = cv.alloc("S", 128, 1, F32)
        msb = cv.alloc("S", 128, 1, F32)
        sse = cv.alloc("S", 128, 1, F32)
        r2c = cv.alloc("S", 128, 1, F32)
        lnbc2 = cv.alloc("S", 128, D, F32)
        x2t = cv.alloc("S", 128, D, BF16)
        dma("sync", lnbc2, ln2_bc[:, :], "LB")
        ln2_tok = P.tok("LB")

        mm_done = None
        yn_used = None
        for tb in range(2):
            w0 = [] if yn_used is None else [yn_used]
            for b in range(32):
                P.op("sync",
                     lambda e, d=ynb[:, b * 512:(b + 1) * 512],
                     s=yn16_h[tb * 512:(tb + 1) * 512,
                              b * 128:(b + 1) * 128]:
                     e.dma_start(d, s, transpose=True),
                     waits=w0 if b < 2 else [], inc=("YB", 16))
            yl = P.tok("YB")
            for ft in range(4):
                wl = wload(wopT, ft * 512, 512, wop, "W0", DI,
                           waits=[mm_done] if mm_done else [])
                for cc in range(4):
                    c = tb * 4 + cc
                    bank = pb[cc % 2]
                    lx = dma("sync", xnl,
                             xn_h[c * 128:(c + 1) * 128,
                                  ft * 512:(ft + 1) * 512],
                             "XL", waits=[P.tok("V")])
                    for k in range(32):
                        P.op("tensor", lambda e, p=bank[:, :],
                             l=ynb[:, k * 512 + cc * 128:
                                    k * 512 + (cc + 1) * 128],
                             r=wop[:, k * 512:(k + 1) * 512],
                             st=(k == 0), sp=(k == 31):
                             e.matmul(p, l, r, start=st, stop=sp),
                             waits=[yl, wl, P.tok("V")] if k == 0 else [],
                             inc=("T", 1))
                    mm = P.tok("T")
                    P.op("vector", lambda e,
                         o=hs[:, c * D + ft * 512:c * D + (ft + 1) * 512],
                         p=bank[:, :], x=xnl:
                         e.tensor_tensor(o, p, x, ALU.add),
                         waits=[mm, lx], inc=("V", 1))
                mm_done = P.tok("V")
            yn_used = P.tok("T")
        hs_tok = P.tok("V")
        if hs_h is not None:
            for c in range(8):
                dma("sync", hs_h[c * 128:(c + 1) * 128, :],
                    hs[:, c * D:(c + 1) * D], "SD", waits=[hs_tok])

        x2_sp = None
        for c in range(8):
            hsl = hs[:, c * D:(c + 1) * D]
            rms_scale(hsl, D, dump, ssc, msb, sse, r2c,
                      [hs_tok, P.tok("V")])
            vw = [P.tok("S"), ln2_tok]
            if x2_sp:
                vw.append(x2_sp)
            P.op("vector", lambda e, o=x2t, i=hsl:
                 e.scalar_tensor_tensor(o, i, r2c, lnbc2, ALU.mult, ALU.mult),
                 waits=vw, inc=("V", 1))
            x2_sp = dma("sync", x2_h[c * 128:(c + 1) * 128, :], x2t,
                        "X2S", waits=[P.tok("V")])

    def stage_ffn():
        hs = _refs["hs"]
        x2fm = _refs["x2fm"]
        cv.reset("S")
        hgu = cv.alloc("S", 128, KD * T, BF16)
        wb2 = [cv.alloc("S", 128, KD * 512, BF16) for _ in range(2)]
        wdb2 = [cv.alloc("S", 128, 8 * 512, BF16) for _ in range(2)]
        gt = [cv.alloc("S", 128, 512, F32) for _ in range(2)]

        for k in range(KD):
            for tb in range(2):
                P.op("sync", lambda e,
                     d=x2fm[:, k * T + tb * 512:k * T + (tb + 1) * 512],
                     s=x2_h[tb * 512:(tb + 1) * 512, k * 128:(k + 1) * 128]:
                     e.dma_start(d, s, transpose=True), inc=("X2", 16))
        x2tok = P.tok("X2")

        bufA_free = None
        bufB_free = None
        pg_rd = [None, None]
        pu_rd = [None, None]
        gt_rd = [None, None]
        dbank_rd = [None, None]
        wd_free = [None, None]
        for sl in range(4):
            hgu_tok = None
            for gg in range(4):
                f0 = sl * 2048 + gg * 512
                la = wload(wgT, f0, 512, wb2[0], "WA", D,
                           waits=[bufA_free] if bufA_free else [])
                lb = wload(wuT, f0, 512, wb2[1], "WB", D,
                           waits=[bufB_free] if bufB_free else [])
                for m in range(8):
                    ic, tb = m // 2, m % 2
                    i = gg * 4 + ic
                    mp = m % 2
                    bg = pb[2 * mp]
                    bu = pb[2 * mp + 1]
                    mw = [la, x2tok]
                    if pg_rd[mp]:
                        mw.append(pg_rd[mp])
                    for k in range(KD):
                        P.op("tensor", lambda e, p=bg[:, :],
                             l=wb2[0][:, k * 512 + ic * 128:
                                      k * 512 + (ic + 1) * 128],
                             r=x2fm[:, k * T + tb * 512:
                                    k * T + (tb + 1) * 512],
                             st=(k == 0), sp=(k == KD - 1):
                             e.matmul(p, l, r, start=st, stop=sp),
                             waits=mw if k == 0 else [], inc=("T", 1))
                    uw = [lb]
                    if pu_rd[mp]:
                        uw.append(pu_rd[mp])
                    for k in range(KD):
                        P.op("tensor", lambda e, p=bu[:, :],
                             l=wb2[1][:, k * 512 + ic * 128:
                                      k * 512 + (ic + 1) * 128],
                             r=x2fm[:, k * T + tb * 512:
                                    k * T + (tb + 1) * 512],
                             st=(k == 0), sp=(k == KD - 1):
                             e.matmul(p, l, r, start=st, stop=sp),
                             waits=uw if k == 0 else [], inc=("T", 1))
                    mm = P.tok("T")
                    sw = [mm]
                    if gt_rd[mp]:
                        sw.append(gt_rd[mp])
                    P.op("scalar", act(gt[mp], bg[:, :], AF.Silu),
                         waits=sw, inc=("S", 1))
                    pg_rd[mp] = P.tok("S")
                    P.op("vector", lambda e,
                         o=hgu[:, i * T + tb * 512:i * T + (tb + 1) * 512],
                         g=gt[mp], p=bu[:, :]:
                         e.tensor_tensor(o, g, p, ALU.mult),
                         waits=[P.tok("S")], inc=("V", 1))
                    pu_rd[mp] = P.tok("V")
                    gt_rd[mp] = pu_rd[mp]
                bufA_free = P.tok("T")
                bufB_free = P.tok("T")
            hgu_tok = P.tok("V")
            for ft in range(4):
                par = ft % 2
                src = bass.AP(wdT, (sl * 2048) * D + ft * 512,
                              [[D, 128], [128 * D, KD], [1, 512]])
                wdst = wb2[par]
                lw = [bufA_free if par == 0 else bufB_free]
                d3 = wdst.rearrange("p (c j) -> p c j", j=512)
                lt = P.op("gpsimd", lambda e, d=d3, s=src: e.dma_start(d, s),
                          waits=lw + ([P.tok("T")] if sl or ft else []),
                          inc=(f"WD{par}", 16))
                for c in range(8):
                    bank = pb[4 + par]
                    bw = [lt, hgu_tok]
                    if dbank_rd[par]:
                        bw.append(dbank_rd[par])
                    for k in range(KD):
                        P.op("tensor", lambda e, p=bank[:, :],
                             l=hgu[:, k * T + c * 128:k * T + (c + 1) * 128],
                             r=wdst[:, k * 512:(k + 1) * 512],
                             st=(k == 0), sp=(k == KD - 1):
                             e.matmul(p, l, r, start=st, stop=sp),
                             waits=bw if k == 0 else [], inc=("T", 1))
                    mm = P.tok("T")
                    P.op("vector", lambda e,
                         o=hs[:, c * D + ft * 512:c * D + (ft + 1) * 512],
                         p=bank[:, :]: e.tensor_tensor(o, o, p, ALU.add),
                         waits=[mm], inc=("V", 1))
                    dbank_rd[par] = P.tok("V")
                if par == 0:
                    bufA_free = P.tok("T")
                else:
                    bufB_free = P.tok("T")

    def stage_final():
        hs = _refs["hs"]
        cv.reset("S")
        xl = [cv.alloc("S", 128, D, F32) for _ in range(2)]
        gl = [cv.alloc("S", 128, D, BF16) for _ in range(2)]
        ot = [cv.alloc("S", 128, D, F32) for _ in range(2)]
        of = [None, None]
        for c in range(8):
            par = c % 2
            lw = [] if of[par] is None else [of[par]]
            dma("sync", xl[par], xin[3 + c * 128:3 + (c + 1) * 128, :],
                f"F{par}", waits=lw)
            dma("sync", gl[par], g_h[c * 128:(c + 1) * 128, :], f"F{par}")
            ld = P.tok(f"F{par}")
            P.op("vector", lambda e, o=ot[par], s=hs[:, c * D:(c + 1) * D],
                 g=gl[par]: e.tensor_tensor(o, s, g, ALU.mult),
                 waits=[ld] + ([of[par]] if of[par] else []), inc=("V", 1))
            P.op("vector", lambda e, o=ot[par], x=xl[par]:
                 e.tensor_tensor(o, o, x, ALU.add), inc=("V", 1))
            of[par] = dma("sync", out_h[c * 128:(c + 1) * 128, :], ot[par],
                          f"OD{par}", waits=[P.tok("V")])
        P.op("sync", None, waits=[P.tok("OD0"), P.tok("OD1")])

    stages = [stage_ln, stage_inproj, stage_scan_a, stage_g, stage_scan_b,
              stage_outproj, stage_ffn, stage_final]
    P.barrier()
    for i, st in enumerate(stages):
        if i >= upto:
            break
        st()
        P.barrier()

    P.emit(nc)
    return nc


def _prep(inputs):
    x = np.asarray(inputs["L0_attn_hidden_states"], np.float32)
    w = {k: np.asarray(v, np.float32) for k, v in inputs.items()}

    sh = {}
    sh["wipT"] = np.ascontiguousarray(w["L0_attn_in_proj_w"].T)
    sh["wopT"] = np.ascontiguousarray(w["L0_attn_out_proj_w"].T)
    sh["wgT"] = np.ascontiguousarray(w["L0_attn_gate_w"].T)
    sh["wuT"] = np.ascontiguousarray(w["L0_attn_up_w"].T)
    sh["wdT"] = np.ascontiguousarray(w["L0_attn_down_w"].T)
    sh["wfT"] = np.ascontiguousarray(w["L0_ffn_weight"].T)
    sh["convw"] = np.ascontiguousarray(
        w["L0_attn_conv_w"].reshape(34, 128, 4).transpose(1, 0, 2)
        .reshape(128, 34 * 4))
    sh["convb"] = np.ascontiguousarray(w["L0_attn_conv_b"].reshape(34, 128).T)
    sh["dtb"] = np.ascontiguousarray(w["L0_attn_dt_bias"].reshape(64, 1))
    sh["alog"] = np.ascontiguousarray(w["L0_attn_A_log"].reshape(64, 1))
    sh["dexp_bc"] = np.ascontiguousarray(np.broadcast_to(
        np.repeat(w["L0_attn_D_param"], HD)[None, :], (128, DI)))
    sh["nw_bc"] = np.ascontiguousarray(np.broadcast_to(
        w["L0_attn_norm_w"][None, :], (128, DI)))
    sh["ln1_bc"] = np.ascontiguousarray(np.broadcast_to(
        w["L0_attn_ln1_w"][None, :], (128, D)))
    sh["ln2_bc"] = np.ascontiguousarray(np.broadcast_to(
        w["L0_attn_ln2_w"][None, :], (128, D)))
    sh["bias_bc"] = np.ascontiguousarray(np.broadcast_to(
        w["L0_ffn_bias"][None, :], (128, D)))
    r = np.arange(128)
    sh["cle"] = (r[:, None] <= r[None, :]).astype(np.float32)
    sh["cle16"] = (r[:, None] <= r[None, :]).astype(BF16NP)
    sh["up16"] = (r[:, None] > r[None, :]).astype(BF16NP)
    sh["iden"] = np.eye(128, dtype=np.float32)
    sh["ones64"] = np.ones((128, 64), np.float32)
    sh["ones1"] = np.ones((1, 128), np.float32)

    in_maps = []
    for q in range(8):
        b, rq = q // 4, q % 4
        sl = np.zeros((TH, D), np.float32)
        t0 = rq * 1024
        if rq == 0:
            sl[3:] = x[b, 0:1024]
        else:
            sl[:] = x[b, t0 - 3:t0 + 1024]
        m = np.zeros(7, np.float32)
        for j in range(7):
            if j // 4 == b and (j % 4) < rq:
                m[j] = 1.0
        d = dict(sh)
        d["xin"] = np.ascontiguousarray(sl)
        d["mcol"] = np.ascontiguousarray(np.broadcast_to(m[None, :],
                                                         (128, 7)))
        d["minv"] = np.ascontiguousarray(np.broadcast_to((1.0 - m)[None, :],
                                                         (128, 7)))
        in_maps.append(d)
    return in_maps


def kernel(**inputs):
    in_maps = _prep(inputs)
    nc = build()
    res = run_bass_kernel_spmd(nc, in_maps, core_ids=list(range(8)))
    out = np.empty((2, 4096, D), np.float32)
    for q in range(8):
        b, rq = q // 4, q % 4
        out[b, rq * 1024:(rq + 1) * 1024] = np.asarray(
            res.results[q]["out_h"], np.float32)
    return out
